# version 06 (final)
# baseline (speedup 1.0000x reference)
import contextlib
import numpy as np
import ml_dtypes
import concourse.bass as bass
import concourse.mybir as mybir
from concourse.bass_utils import run_bass_kernel_spmd

F32 = mybir.dt.float32
BF16 = mybir.dt.bfloat16
AF = mybir.ActivationFunctionType
ALU = mybir.AluOpType
AX = mybir.AxisListType
BFNP = ml_dtypes.bfloat16

B, T, E = 2, 8192, 1024
CORES = 8
CH = 1024
TPC = 2 * CH
NCH = T // CH
NDT = E // 128
MAGIC = 1.5 * 2.0 ** 23

ALL_LAYERS = ("mla", "ffn", "griffin", "ffn2", "deltanet", "moe", "winlat", "moe2")
EMITTERS = {}
DBG_TAG = None
STOP_AFTER = None
DBG_DRAM = False


def dbg_hook(P, tag, ap, n):
    if DBG_TAG != tag:
        return
    npart = ap.shape[0]
    def dump(e, ap=ap, n=n, npart=npart):
        e.dma_start(out=P.out["dbg"][0:npart, 0:n], in_=ap).then_inc(P.s_tick, 16)
        return 1
    dphase(P, "sync", dump)


class Prog:
    def __init__(self, nc, es):
        self.nc = nc
        self.es = es
        self.block = None
        self.inp = {}
        self.out = {}
        self.sb = {}
        self.dram = {}
        self.ps = []
        self.tick = 0
        self.s_tick = None
        self.selfsem = {}
        self.selfcnt = {}
        self.state = {}
        self.swdma_n = 0
        self.regn = 0

    def sec(self, eng, fn):
        getattr(self.block, eng)(fn)

    def bank(self, i):
        return self.ps[i]


ENGS = ("sync", "tensor", "vector", "scalar", "gpsimd")


class EngProxy:
    _nolast = {"wait_ge", "nop", "dma_start", "dma_start_transpose", "indirect_dma_start", "reg_load", "reg_add", "reg_mov", "sem_clear"}

    def __init__(self, eng):
        self._e = eng
        self.last = None

    def __getattr__(self, name):
        attr = getattr(self._e, name)
        if callable(attr) and not name.startswith("_"):
            def call(*a, _n=name, **k):
                r = attr(*a, **k)
                if _n not in EngProxy._nolast and r is not None and hasattr(r, "then_inc"):
                    self.last = r
                return r
            return call
        return attr


def phase(P, **fns):
    entry = P.tick
    extra = [0]
    n_engines = len(fns)
    for eng, fn in fns.items():
        def wrapped(e, fn=fn):
            p = EngProxy(e)
            e.wait_ge(P.s_tick, entry)
            r = fn(p)
            extra[0] += r or 0
            last = p.last if p.last is not None else e.nop()
            last.then_inc(P.s_tick, 1)
        P.sec(eng, wrapped)
    P.tick = entry + n_engines + extra[0]


def dphase(P, eng, fn):
    entry = P.tick
    n_out = [0]

    def wrapped(e):
        e.wait_ge(P.s_tick, entry)
        n = fn(e)
        n_out[0] = n
        e.nop().then_inc(P.s_tick, 1)
    P.sec(eng, wrapped)
    P.tick = entry + 1 + 16 * n_out[0]


def dphase_sw(P, eng, fn):
    entry = P.tick

    def wrapped(e):
        e.wait_ge(P.s_tick, entry)
        n = fn(e)
        P.swdma_n += 16 * n
        e.wait_ge(P.s_swdma, P.swdma_n)
        e.nop().then_inc(P.s_tick, 1)
    P.sec(eng, wrapped)
    P.tick = entry + 1


def selfsync(P, e, name):
    s = P.selfsem[name]
    P.selfcnt[name] += 1
    last = getattr(e, "last", None)
    if last is None:
        last = e.nop()
    last.then_inc(s, 1)
    e.wait_ge(s, P.selfcnt[name])
    if hasattr(e, "last"):
        e.last = None



def emit_input_stage(P):
    RES, SCR_F, IDENT = P.sb["RES"], P.sb["SCR_F"], P.sb["IDENT"]
    x_in = P.inp["x"]
    for g0 in range(0, TPC // 128, 8):
        def load(e, g0=g0):
            for i in range(8):
                tt = g0 + i
                e.dma_start(out=SCR_F[:, i * 1024:(i + 1) * 1024],
                            in_=x_in[tt * 128:(tt + 1) * 128, :]).then_inc(P.s_tick, 16)
            return 8
        dphase(P, "sync", load)

        for i in range(8):
            def trans(t, i=i):
                for dt in range(NDT):
                    t.transpose(P.bank(dt)[:, 0:128],
                                SCR_F[:, i * 1024 + dt * 128:i * 1024 + (dt + 1) * 128],
                                IDENT[:, :])
            phase(P, tensor=trans)

            def evac(sc, g0=g0, i=i):
                tt = g0 + i
                for dt in range(NDT):
                    sc.copy(RES[:, dt * TPC + tt * 128:dt * TPC + (tt + 1) * 128],
                            P.bank(dt)[:, 0:128])
            phase(P, scalar=evac)


def emit_output_stage(P):
    RES, SCR_F, IDENT = P.sb["RES"], P.sb["SCR_F"], P.sb["IDENT"]
    y_out = P.out["y"]
    for g0 in range(0, TPC // 128, 8):
        for i in range(8):
            def trans(t, g0=g0, i=i):
                tt = g0 + i
                for dt in range(NDT):
                    t.transpose(P.bank(dt)[:, 0:128],
                                RES[:, dt * TPC + tt * 128:dt * TPC + (tt + 1) * 128],
                                IDENT[:, :])
            phase(P, tensor=trans)

            def evac(sc, i=i):
                for dt in range(NDT):
                    sc.copy(SCR_F[:, i * 1024 + dt * 128:i * 1024 + (dt + 1) * 128],
                            P.bank(dt)[:, 0:128])
            phase(P, scalar=evac)

        def store(e, g0=g0):
            for i in range(8):
                tt = g0 + i
                e.dma_start(out=y_out[tt * 128:(tt + 1) * 128, :],
                            in_=SCR_F[:, i * 1024:(i + 1) * 1024]).then_inc(P.s_tick, 16)
            return 8
        dphase(P, "sync", store)

    def fin(e):
        pass
    phase(P, sync=fin)



def emit_ln(P, want_bf16=True, dst=None):
    RES, SCR_F = P.sb["RES"], P.sb["SCR_F"]
    MROW, IROW, TROW = P.sb["MROW"], P.sb["IROW"], P.sb["TROW"]
    IBROW, MIBROW = P.sb["IBROW"], P.sb["MIBROW"]
    ONES_F = P.sb["ONES_F"]
    if dst is None:
        dst = P.sb["WRK_B"]

    for half in range(2):
        cks = (2 * half, 2 * half + 1)

        def squares(sc, cks=cks):
            for j, ck in enumerate(cks):
                for dt in range(NDT):
                    sc.activation(SCR_F[:, j * 4096 + dt * 512:j * 4096 + (dt + 1) * 512],
                                  RES[:, dt * TPC + ck * 512:dt * TPC + ck * 512 + 512],
                                  AF.Square)
        phase(P, scalar=squares)

        def sums2(t, cks=cks):
            for j, ck in enumerate(cks):
                for dt in range(NDT):
                    t.matmul(P.bank(j)[0:1, 0:512], ONES_F[:, 0:1],
                             RES[:, dt * TPC + ck * 512:dt * TPC + ck * 512 + 512],
                             start=(dt == 0), stop=(dt == NDT - 1))
                for dt in range(NDT):
                    t.matmul(P.bank(2 + j)[0:1, 0:512], ONES_F[:, 0:1],
                             SCR_F[:, j * 4096 + dt * 512:j * 4096 + (dt + 1) * 512],
                             start=(dt == 0), stop=(dt == NDT - 1))
        phase(P, tensor=sums2)

        def rows(v, cks=cks):
            for j, ck in enumerate(cks):
                c0 = ck * 512
                v.tensor_scalar(MROW[0:1, c0:c0 + 512], P.bank(j)[0:1, 0:512],
                                1.0 / E, None, ALU.mult)
            selfsync(P, v, "v")
            for j, ck in enumerate(cks):
                c0 = ck * 512
                v.tensor_tensor(TROW[0:1, c0:c0 + 512], MROW[0:1, c0:c0 + 512],
                                MROW[0:1, c0:c0 + 512], ALU.mult)
            selfsync(P, v, "v")
            for j, ck in enumerate(cks):
                c0 = ck * 512
                v.scalar_tensor_tensor(TROW[0:1, c0:c0 + 512], P.bank(2 + j)[0:1, 0:512],
                                       1.0 / E, TROW[0:1, c0:c0 + 512],
                                       ALU.mult, ALU.subtract)
        phase(P, vector=rows)

    def rows2a(v):
        v.tensor_scalar(TROW[0:1, 0:TPC], TROW[0:1, 0:TPC], 1e-5, None, ALU.add)
        selfsync(P, v, "v")
        v.reciprocal(IROW[0:1, 0:TPC], TROW[0:1, 0:TPC])
    phase(P, vector=rows2a)

    def rows2(sc):
        sc.activation(IROW[0:1, 0:TPC], IROW[0:1, 0:TPC], AF.Sqrt)
    phase(P, scalar=rows2)

    def rows3(v):
        v.tensor_copy(IBROW[0:1, 0:TPC], IROW[0:1, 0:TPC])
        v.tensor_tensor(TROW[0:1, 0:TPC], MROW[0:1, 0:TPC], IROW[0:1, 0:TPC], ALU.mult)
        selfsync(P, v, "v")
        v.tensor_copy(MIBROW[0:1, 0:TPC], TROW[0:1, 0:TPC])
    phase(P, vector=rows3)

    if want_bf16:
        emit_ln_apply(P, dst)


def emit_ln_apply(P, dst):
    RES, SCR_F = P.sb["RES"], P.sb["SCR_F"]
    IBROW, MIBROW, ONES1B = P.sb["IBROW"], P.sb["MIBROW"], P.sb["ONES1B"]

    def bc(t):
        for ck in range(4):
            c0 = ck * 512
            t.matmul(P.bank(ck)[:, 0:512], ONES1B[0:1, :], IBROW[0:1, c0:c0 + 512],
                     start=True, stop=True)
            t.matmul(P.bank(4 + ck)[:, 0:512], ONES1B[0:1, :], MIBROW[0:1, c0:c0 + 512],
                     start=True, stop=True)
    phase(P, tensor=bc)

    def apply(v):
        for g in range(2):
            for ck in range(2 * g, 2 * g + 2):
                c0 = ck * 512
                for dt in range(NDT):
                    v.tensor_tensor(SCR_F[:, (ck % 2) * 4096 + dt * 512:(ck % 2) * 4096 + (dt + 1) * 512],
                                    RES[:, dt * TPC + c0:dt * TPC + c0 + 512],
                                    P.bank(ck)[:, 0:512], ALU.mult)
            selfsync(P, v, "v")
            for ck in range(2 * g, 2 * g + 2):
                c0 = ck * 512
                for dt in range(NDT):
                    v.tensor_tensor(dst[:, dt * TPC + c0:dt * TPC + c0 + 512],
                                    SCR_F[:, (ck % 2) * 4096 + dt * 512:(ck % 2) * 4096 + (dt + 1) * 512],
                                    P.bank(4 + ck)[:, 0:512], ALU.subtract)
            selfsync(P, v, "v")
    phase(P, vector=apply)



def emit_ffn_weight_quant(P):
    SCR_F, SCR_B = P.sb["SCR_F"], P.sb["SCR_B"]
    COLS, WSROW, WSCOL = P.sb["COLS"], P.sb["WSROW"], P.sb["WSCOL"]
    ONES_F, ONESROW_F = P.sb["ONES_F"], P.sb["ONESROW_F"]
    specs = [("ffn_wg_t", "wqg", 0, "swz"), ("ffn_wu_t", "wqu", 1, "swz"),
             ("ffn_wd_t", "wqd", 2, "rows")]
    for wname, qname, si, mode in specs:
        src_t = P.inp[wname]
        nslab = 8
        for sl in range(nslab):
            def load(e, sl=sl, src_t=src_t, mode=mode):
                if mode == "swz":
                    e.dma_start(out=SCR_F[:, 0:4096],
                                in_=src_t[sl * 128:(sl + 1) * 128, :]).then_inc(P.s_tick, 16)
                    return 1
                for j in range(4):
                    r0 = (sl * 4 + j) * 128
                    e.dma_start(out=SCR_F[:, j * 1024:(j + 1) * 1024],
                                in_=src_t[r0:r0 + 128, :]).then_inc(P.s_tick, 16)
                return 4
            dphase(P, "sync", load)

            def acc(v, sl=sl):
                v.tensor_reduce(COLS[:, sl:sl + 1], SCR_F[:, 0:4096], axis=AX.X,
                                op=ALU.add, apply_absolute_value=True)
            phase(P, vector=acc)

        def tot(t):
            t.matmul(P.bank(6)[0:1, 0:8], ONES_F[:, 0:1], COLS[:, 0:8],
                     start=True, stop=True)
        phase(P, tensor=tot)

        def scal(v, si=si):
            v.tensor_reduce(WSROW[0:1, si:si + 1], P.bank(6)[0:1, 0:8],
                            axis=AX.X, op=ALU.add)
            selfsync(P, v, "v")
            v.tensor_scalar(WSROW[0:1, si:si + 1], WSROW[0:1, si:si + 1],
                            1.0 / (E * 4096), 1e-5, ALU.mult, ALU.max)
            selfsync(P, v, "v")
            v.reciprocal(WSROW[0:1, 8 + si:9 + si], WSROW[0:1, si:si + 1])
        phase(P, vector=scal)

        def bcw(t, si=si):
            t.matmul(P.bank(7)[:, 0:1], ONESROW_F[0:1, :], WSROW[0:1, 8 + si:9 + si],
                     start=True, stop=True)
        phase(P, tensor=bcw)

        def cpw(sc, si=si):
            sc.copy(WSCOL[:, si:si + 1], P.bank(7)[:, 0:1])
        phase(P, scalar=cpw)

        for sl in range(nslab):
            def load2(e, sl=sl, src_t=src_t, mode=mode):
                if mode == "swz":
                    e.dma_start(out=SCR_F[:, 0:4096],
                                in_=src_t[sl * 128:(sl + 1) * 128, :]).then_inc(P.s_tick, 16)
                    return 1
                for j in range(4):
                    r0 = (sl * 4 + j) * 128
                    e.dma_start(out=SCR_F[:, j * 1024:(j + 1) * 1024],
                                in_=src_t[r0:r0 + 128, :]).then_inc(P.s_tick, 16)
                return 4
            dphase(P, "sync", load2)

            def quant(v, si=si):
                v.tensor_scalar(SCR_F[:, 4096:8192], SCR_F[:, 0:4096],
                                WSCOL[:, si:si + 1], MAGIC, ALU.mult, ALU.add)
                selfsync(P, v, "v")
                v.tensor_scalar(SCR_F[:, 4096:8192], SCR_F[:, 4096:8192],
                                MAGIC, None, ALU.subtract)
                selfsync(P, v, "v")
                v.tensor_scalar(SCR_B[:, 0:4096], SCR_F[:, 4096:8192],
                                1.0, -1.0, ALU.min, ALU.max)
            phase(P, vector=quant)

            def store(e, sl=sl, qname=qname, mode=mode):
                q = P.dram[qname]
                if mode == "swz":
                    qv = q.rearrange("a (p b) -> a p b", p=128)
                    for it in range(32):
                        e.dma_start(out=qv[it, :, sl * 128:sl * 128 + 128],
                                    in_=SCR_B[:, it * 128:(it + 1) * 128]).then_inc(P.s_swdma, 16)
                    return 32
                for j in range(4):
                    r0 = (sl * 4 + j) * 128
                    e.dma_start(out=q[r0:r0 + 128, :],
                                in_=SCR_B[:, j * 1024:(j + 1) * 1024]).then_inc(P.s_swdma, 16)
                return 4
            dphase_sw(P, "gpsimd", store)


def emit_act_quant_x(P):
    RES, WRK_B, SCR_F = P.sb["RES"], P.sb["WRK_B"], P.sb["SCR_F"]
    MROW = P.sb["MROW"]
    IBROW, MIBROW = P.sb["IBROW"], P.sb["MIBROW"]
    IXROW, TROW, FROW, BROW = P.sb["IXROW"], P.sb["TROW"], P.sb["FROW"], P.sb["BROW"]
    COLS, IDENT, ONES1B = P.sb["COLS"], P.sb["IDENT"], P.sb["ONES1B"]
    for ck in range(4):
        c0 = ck * 512

        def bc(t, c0=c0):
            t.matmul(P.bank(4)[:, 0:512], P.sb["ONESROW_F"][0:1, 0:128],
                     P.sb["IROW"][0:1, c0:c0 + 512], start=True, stop=True)
            t.matmul(P.bank(5)[:, 0:512], P.sb["ONESROW_F"][0:1, 0:128],
                     P.sb["TROW"][0:1, c0:c0 + 512], start=True, stop=True)
        phase(P, tensor=bc)

        def xln(v, c0=c0):
            for dt in range(NDT):
                v.tensor_tensor(SCR_F[:, dt * 512:(dt + 1) * 512],
                                RES[:, dt * TPC + c0:dt * TPC + c0 + 512],
                                P.bank(4)[:, 0:512], ALU.mult)
            selfsync(P, v, "v")
            for dt in range(NDT):
                v.tensor_tensor(SCR_F[:, dt * 512:(dt + 1) * 512],
                                SCR_F[:, dt * 512:(dt + 1) * 512],
                                P.bank(5)[:, 0:512], ALU.subtract)
            selfsync(P, v, "v")
            v.tensor_tensor(SCR_F[:, 4096:4608], SCR_F[:, 0:512],
                            SCR_F[:, 512:1024], ALU.max)
            v.tensor_tensor(SCR_F[:, 7680:8192], SCR_F[:, 0:512],
                            SCR_F[:, 512:1024], ALU.min)
            for dt in range(2, NDT):
                selfsync(P, v, "v")
                v.tensor_tensor(SCR_F[:, 4096:4608], SCR_F[:, 4096:4608],
                                SCR_F[:, dt * 512:(dt + 1) * 512], ALU.max)
                v.tensor_tensor(SCR_F[:, 7680:8192], SCR_F[:, 7680:8192],
                                SCR_F[:, dt * 512:(dt + 1) * 512], ALU.min)
            selfsync(P, v, "v")
            v.tensor_scalar(SCR_F[:, 7680:8192], SCR_F[:, 7680:8192],
                            -1.0, None, ALU.mult)
            selfsync(P, v, "v")
            v.tensor_tensor(SCR_F[:, 4096:4608], SCR_F[:, 4096:4608],
                            SCR_F[:, 7680:8192], ALU.max)
        phase(P, vector=xln)

        def tr(t):
            for i in range(4):
                t.transpose(P.bank(i)[:, 0:128],
                            SCR_F[:, 4096 + i * 128:4096 + (i + 1) * 128], IDENT[:, :])
        phase(P, tensor=tr)

        def red(v):
            for i in range(4):
                v.tensor_reduce(COLS[:, 8 + i:9 + i], P.bank(i)[:, 0:128],
                                axis=AX.X, op=ALU.max)
        phase(P, vector=red)

        def tr2(t):
            for i in range(4):
                t.transpose(P.bank(6)[0:1, i * 128:(i + 1) * 128],
                            COLS[:, 8 + i:9 + i], IDENT[:, :])
        phase(P, tensor=tr2)

        def sx(v, c0=c0):
            v.tensor_scalar(TROW[0:1, c0:c0 + 512], P.bank(6)[0:1, 0:512],
                            1.0, 1e-5, ALU.mult, ALU.max)
            selfsync(P, v, "v")
            v.reciprocal(FROW[0:1, 0:512], TROW[0:1, c0:c0 + 512])
            v.tensor_scalar(IXROW[0:1, c0:c0 + 512], TROW[0:1, c0:c0 + 512],
                            1.0 / 127.0, None, ALU.mult)
            selfsync(P, v, "v")
            v.tensor_scalar(FROW[0:1, 0:512], FROW[0:1, 0:512],
                            127.0, None, ALU.mult)
            selfsync(P, v, "v")
            v.tensor_copy(BROW[0:1, 1024:1536], FROW[0:1, 0:512])
        phase(P, vector=sx)

        def bc2(t, c0=c0):
            t.matmul(P.bank(7)[:, 0:512], P.sb["ONESROW_F"][0:1, 0:128],
                     FROW[0:1, 0:512], start=True, stop=True)
        phase(P, tensor=bc2)

        def mulq(v):
            for dt in range(NDT):
                v.tensor_tensor(SCR_F[:, dt * 512:(dt + 1) * 512],
                                SCR_F[:, dt * 512:(dt + 1) * 512],
                                P.bank(7)[:, 0:512], ALU.mult)
        phase(P, vector=mulq)

        def rnd(sc):
            for dt in range(NDT):
                sc.activation(SCR_F[:, dt * 512:(dt + 1) * 512],
                              SCR_F[:, dt * 512:(dt + 1) * 512], AF.Identity, bias=P.sb["MAGCOL"][:, 0:1])
            selfsync(P, sc, "a")
            for dt in range(NDT):
                sc.activation(SCR_F[:, dt * 512:(dt + 1) * 512],
                              SCR_F[:, dt * 512:(dt + 1) * 512], AF.Identity, bias=P.sb["MAGCOL"][:, 1:2])
        phase(P, scalar=rnd)

        def clipq(v, c0=c0):
            for dt in range(NDT):
                v.tensor_scalar(WRK_B[:, dt * TPC + c0:dt * TPC + c0 + 512],
                                SCR_F[:, dt * 512:(dt + 1) * 512],
                                127.0, -128.0, ALU.min, ALU.max)
        phase(P, vector=clipq)


def emit_bitlinear_core(P):
    RES, WRK_B, SCR_F, SCR_B = P.sb["RES"], P.sb["WRK_B"], P.sb["SCR_F"], P.sb["SCR_B"]
    IXROW, TROW, FROW, BROW = P.sb["IXROW"], P.sb["TROW"], P.sb["FROW"], P.sb["BROW"]
    WSROW, COLS, IDENT = P.sb["WSROW"], P.sb["COLS"], P.sb["IDENT"]
    WBUF = P.sb["WBUF"]
    WBUF2 = P.sb["POOL_B"][:, 4096:5120]
    ONES1B = P.sb["ONES1B"]
    ONESROW_F = P.sb["ONESROW_F"]
    PF = P.sb["POOL"][:, 16384:24576]
    for cb in range(8):
        c0 = cb * 256

        def rowprep(v, c0=c0):
            v.tensor_scalar(FROW[0:1, 0:256], IXROW[0:1, c0:c0 + 256],
                            WSROW[0:1, 0:1], None, ALU.mult)
            v.tensor_scalar(FROW[0:1, 256:512], IXROW[0:1, c0:c0 + 256],
                            WSROW[0:1, 1:2], None, ALU.mult)
        phase(P, vector=rowprep)

        def bcg(t):
            t.matmul(P.bank(6)[:, 0:256], ONESROW_F[0:1, 0:128], FROW[0:1, 0:256],
                     start=True, stop=True)
            t.matmul(P.bank(7)[:, 0:256], ONESROW_F[0:1, 0:128], FROW[0:1, 256:512],
                     start=True, stop=True)
        phase(P, tensor=bcg)

        CGBC = SCR_F[:, 4608:4864]
        CUBC = SCR_F[:, 7168:7424]

        def evbc(sc):
            sc.copy(CGBC, P.bank(6)[:, 0:256])
            sc.copy(CUBC, P.bank(7)[:, 0:256])
        phase(P, scalar=evbc)

        for it in range(32):
            def wload(e, it=it):
                e.dma_start(out=WBUF[:, 0:1024],
                            in_=P.dram["wqg"][it:it + 1, :].rearrange("a (p b) -> (a p) b", p=128)
                            ).then_inc(P.s_tick, 16)
                e.dma_start(out=WBUF[:, 1024:2048],
                            in_=P.dram["wqu"][it:it + 1, :].rearrange("a (p b) -> (a p) b", p=128)
                            ).then_inc(P.s_tick, 16)
                return 2
            dphase(P, "sync", wload)

            def gu(t, c0=c0):
                for kt in range(NDT):
                    t.matmul(P.bank(0)[:, 0:256], WBUF[:, kt * 128:(kt + 1) * 128],
                             WRK_B[:, kt * TPC + c0:kt * TPC + c0 + 256],
                             start=(kt == 0), stop=(kt == NDT - 1))
                for kt in range(NDT):
                    t.matmul(P.bank(1)[:, 0:256], WBUF[:, 1024 + kt * 128:1024 + (kt + 1) * 128],
                             WRK_B[:, kt * TPC + c0:kt * TPC + c0 + 256],
                             start=(kt == 0), stop=(kt == NDT - 1))
            phase(P, tensor=gu)

            def scl(v, it=it):
                v.tensor_tensor(SCR_F[:, 0:256], P.bank(0)[:, 0:256], CGBC, ALU.mult)
                v.tensor_tensor(SCR_F[:, 256:512], P.bank(1)[:, 0:256], CUBC, ALU.mult)
            phase(P, vector=scl)

            def act(sc):
                sc.activation(SCR_F[:, 0:256], SCR_F[:, 0:256], AF.Silu)
            phase(P, scalar=act)

            def prod(v, it=it):
                v.tensor_tensor(PF[:, it * 256:(it + 1) * 256], SCR_F[:, 0:256],
                                SCR_F[:, 256:512], ALU.mult)
                selfsync(P, v, "v")
                if it == 0:
                    v.tensor_copy(SCR_F[:, 1024:1280], PF[:, 0:256])
                    v.tensor_copy(SCR_F[:, 1280:1536], PF[:, 0:256])
                else:
                    v.tensor_tensor(SCR_F[:, 1024:1280], SCR_F[:, 1024:1280],
                                    PF[:, it * 256:(it + 1) * 256], ALU.max)
                    v.tensor_tensor(SCR_F[:, 1280:1536], SCR_F[:, 1280:1536],
                                    PF[:, it * 256:(it + 1) * 256], ALU.min)
            phase(P, vector=prod)

        def amax2(v):
            v.tensor_scalar(SCR_F[:, 1280:1536], SCR_F[:, 1280:1536],
                            -1.0, None, ALU.mult)
            selfsync(P, v, "v")
            v.tensor_tensor(SCR_F[:, 1024:1280], SCR_F[:, 1024:1280],
                            SCR_F[:, 1280:1536], ALU.max)
        phase(P, vector=amax2)

        def tr(t):
            for i in range(2):
                t.transpose(P.bank(i)[:, 0:128],
                            SCR_F[:, 1024 + i * 128:1024 + (i + 1) * 128], IDENT[:, :])
        phase(P, tensor=tr)

        def red(v):
            for i in range(2):
                v.tensor_reduce(COLS[:, 8 + i:9 + i], P.bank(i)[:, 0:128],
                                axis=AX.X, op=ALU.max)
        phase(P, vector=red)

        def tr2(t):
            for i in range(2):
                t.transpose(P.bank(6)[0:1, i * 128:(i + 1) * 128],
                            COLS[:, 8 + i:9 + i], IDENT[:, :])
        phase(P, tensor=tr2)

        def sy(v):
            v.tensor_scalar(TROW[0:1, 0:256], P.bank(6)[0:1, 0:256],
                            1.0, 1e-5, ALU.mult, ALU.max)
            selfsync(P, v, "v")
            v.reciprocal(FROW[0:1, 0:256], TROW[0:1, 0:256])
            v.tensor_scalar(FROW[0:1, 256:512], TROW[0:1, 0:256],
                            1.0 / 127.0, None, ALU.mult)
            selfsync(P, v, "v")
            v.tensor_scalar(FROW[0:1, 0:256], FROW[0:1, 0:256], 127.0, None, ALU.mult)
            v.tensor_scalar(FROW[0:1, 256:512], FROW[0:1, 256:512],
                            WSROW[0:1, 2:3], None, ALU.mult)
        phase(P, vector=sy)

        def bcy(t):
            t.matmul(P.bank(7)[:, 0:256], ONESROW_F[0:1, 0:128], FROW[0:1, 0:256],
                     start=True, stop=True)
        phase(P, tensor=bcy)

        for g8 in range(4):
            def q1(v, g8=g8):
                for j in range(8):
                    it = g8 * 8 + j
                    v.tensor_tensor(SCR_F[:, 1536 + j * 256:1536 + (j + 1) * 256],
                                    PF[:, it * 256:(it + 1) * 256],
                                    P.bank(7)[:, 0:256], ALU.mult)
            phase(P, vector=q1)

            def q2(sc, g8=g8):
                for j in range(8):
                    sc.activation(SCR_F[:, 1536 + j * 256:1536 + (j + 1) * 256],
                                  SCR_F[:, 1536 + j * 256:1536 + (j + 1) * 256],
                                  AF.Identity, bias=P.sb["MAGCOL"][:, 0:1])
                selfsync(P, sc, "a")
                for j in range(8):
                    sc.activation(SCR_F[:, 1536 + j * 256:1536 + (j + 1) * 256],
                                  SCR_F[:, 1536 + j * 256:1536 + (j + 1) * 256],
                                  AF.Identity, bias=P.sb["MAGCOL"][:, 1:2])
            phase(P, scalar=q2)

            def q3(v, g8=g8):
                for j in range(8):
                    it = g8 * 8 + j
                    v.tensor_scalar(SCR_B[:, it * 256:(it + 1) * 256],
                                    SCR_F[:, 1536 + j * 256:1536 + (j + 1) * 256],
                                    127.0, -128.0, ALU.min, ALU.max)
            phase(P, vector=q3)

        for kt in range(32):
            def wload2(e, kt=kt):
                e.dma_start(out=WBUF2[:, 0:1024],
                            in_=P.dram["wqd"][kt * 128:(kt + 1) * 128, :]).then_inc(P.s_tick, 16)
                return 1
            dphase(P, "sync", wload2)

            def down(t, kt=kt):
                for dt in range(NDT):
                    t.matmul(P.bank(dt)[:, 0:256], WBUF2[:, dt * 128:(dt + 1) * 128],
                             SCR_B[:, kt * 256:(kt + 1) * 256],
                             start=(kt == 0), stop=(kt == 31))
            phase(P, tensor=down)

        def evac(sc):
            for dt in range(NDT):
                sc.copy(SCR_F[:, 1536 + dt * 256:1536 + (dt + 1) * 256],
                        P.bank(dt)[:, 0:256])
        phase(P, scalar=evac)

        def bccs(t):
            t.matmul(P.bank(0)[:, 0:256], ONESROW_F[0:1, 0:128], FROW[0:1, 256:512],
                     start=True, stop=True)
        phase(P, tensor=bccs)

        def resadd(v, c0=c0):
            for dt in range(NDT):
                v.tensor_tensor(SCR_F[:, 1536 + dt * 256:1536 + (dt + 1) * 256],
                                SCR_F[:, 1536 + dt * 256:1536 + (dt + 1) * 256],
                                P.bank(0)[:, 0:256], ALU.mult)
            selfsync(P, v, "v")
            for dt in range(NDT):
                v.tensor_tensor(RES[:, dt * TPC + c0:dt * TPC + c0 + 256],
                                RES[:, dt * TPC + c0:dt * TPC + c0 + 256],
                                SCR_F[:, 1536 + dt * 256:1536 + (dt + 1) * 256], ALU.add)
        phase(P, vector=resadd)


def emit_ffn(P):
    if not P.state.get("ffn_quant_done"):
        emit_ffn_weight_quant(P)
        P.state["ffn_quant_done"] = True
    emit_ln(P, want_bf16=False)
    emit_act_quant_x(P)
    emit_bitlinear_core(P)


EMITTERS["ffn"] = emit_ffn
EMITTERS["ffn2"] = emit_ffn






def emit_moe_cast(P):
    SCR_F, SCR_B = P.sb["SCR_F"], P.sb["SCR_B"]
    for name_in, name_out in (("moe_w1s", "w1b"), ("moe_w2r", "w2b")):
        for s0 in range(0, 256, 2):
            def load(e, s0=s0, name_in=name_in):
                for j in range(2):
                    e.dma_start(out=SCR_F[:, j * 1024:(j + 1) * 1024],
                                in_=P.inp[name_in][(s0 + j) * 128:(s0 + j + 1) * 128, :]
                                ).then_inc(P.s_tick, 16)
                return 2
            dphase(P, "sync", load)

            def cast(v):
                v.tensor_copy(SCR_B[:, 0:2048], SCR_F[:, 0:2048])
            phase(P, vector=cast)

            def store(e, s0=s0, name_out=name_out):
                for j in range(2):
                    e.dma_start(out=P.dram[name_out][(s0 + j) * 128:(s0 + j + 1) * 128, :],
                                in_=SCR_B[:, j * 1024:(j + 1) * 1024]).then_inc(P.s_tick, 16)
                return 2
            dphase(P, "sync", store)


def emit_moe(P):
    nc = P.nc
    RES, WRK_B, SCR_F, SCR_B = P.sb["RES"], P.sb["WRK_B"], P.sb["SCR_F"], P.sb["SCR_B"]
    POOL = P.sb["POOL"]
    SCR_FB = P.sb["POOL_B"][:, 0:16384]
    SQ_B = P.sb["POOL_B"][:, 40960:49152]
    IDXROW = POOL[0:1, 16384:18432]
    GATROW = POOL[0:1, 18432:20480]
    MROW, IROW, TROW, FROW = P.sb["MROW"], P.sb["IROW"], P.sb["TROW"], P.sb["FROW"]
    IBROW, MIBROW, BROW = P.sb["IBROW"], P.sb["MIBROW"], P.sb["BROW"]
    COLS, IDENT = P.sb["COLS"], P.sb["IDENT"]
    ONES1B, ONESBC = P.sb["ONES1B"], P.sb["ONESBC"]
    WRB, BRB, IOTA8 = P.sb["WRB"], P.sb["BRB"], P.sb["IOTA8"]
    GCOL, BCOL, B1C = P.sb["GCOL"], P.sb["BCOL"], P.sb["B1C"]
    WBUF = P.sb["WBUF"]
    WBUF2 = P.sb["POOL_B"][:, 44032:45056]

    if not P.state.get("moe_done"):
        P.state["moe_done"] = True
        emit_moe_cast(P)

        def ldsmall(e):
            for kt in range(NDT):
                e.dma_start(out=SCR_F[:, kt * 8:(kt + 1) * 8],
                            in_=P.inp["moe_wr"][kt * 128:(kt + 1) * 128, :]).then_inc(P.s_tick, 16)
            e.dma_start(out=SCR_F[0:1, 64:72], in_=P.inp["moe_br"][:, :]).then_inc(P.s_tick, 16)
            e.dma_start(out=IOTA8[:, :], in_=P.inp["iota8"][:, :]).then_inc(P.s_tick, 16)
            e.dma_start(out=GCOL[:, :], in_=P.inp["moe_gcol"][:, :]).then_inc(P.s_tick, 16)
            e.dma_start(out=BCOL[:, :], in_=P.inp["moe_bcol"][:, :]).then_inc(P.s_tick, 16)
            return 12
        dphase(P, "sync", ldsmall)

        def caster(v):
            v.tensor_copy(WRB[:, 0:64], SCR_F[:, 0:64])
            v.tensor_copy(BRB[0:1, 0:8], SCR_F[0:1, 64:72])
            v.tensor_copy(P.sb["WRF"][:, 0:64], SCR_F[:, 0:64])
            v.tensor_copy(P.sb["MOER"][0:1, 8:16], SCR_F[0:1, 64:72])
        phase(P, vector=caster)

        def mmcw(t):
            for kt in range(NDT):
                t.matmul(P.bank(2)[0:1, 0:8], P.sb["ONES_F"][:, 0:1],
                         P.sb["WRF"][:, kt * 8:(kt + 1) * 8],
                         start=(kt == 0), stop=(kt == NDT - 1))
            t.matmul(P.bank(3)[:, 0:8], P.sb["ONESROW_F"][0:1, 0:128],
                     P.sb["MOER"][0:1, 8:16], start=True, stop=True)
        phase(P, tensor=mmcw)

        def evcw(sc):
            sc.activation(P.sb["MOER"][0:1, 0:8], P.bank(2)[0:1, 0:8], AF.Identity,
                          scale=-1.0)
            sc.copy(P.sb["MOEC"][:, 16:24], P.bank(3)[:, 0:8])
        phase(P, scalar=evcw)

    emit_ln(P, want_bf16=True)

    MOER, MOEC, WRF = P.sb["MOER"], P.sb["MOEC"], P.sb["WRF"]
    LG = SCR_F[:, 384:512]

    def ivc(t):
        for tt in range(16):
            t.matmul(P.bank(2 + tt // 8)[:, (tt % 8):(tt % 8) + 1],
                     IROW[0:1, tt * 128:(tt + 1) * 128],
                     P.sb["ONESROW_F"][0:1, 0:1], start=True, stop=True)
    phase(P, tensor=ivc)

    def evivc(sc):
        sc.copy(MOEC[:, 0:8], P.bank(2)[:, 0:8])
        sc.copy(MOEC[:, 8:16], P.bank(3)[:, 0:8])
    phase(P, scalar=evivc)

    for tg in range(2):
        def logits(t, tg=tg):
            for ti in range(8):
                tt = tg * 8 + ti
                for kt in range(NDT):
                    t.matmul(P.bank(tg)[:, ti * 8:(ti + 1) * 8],
                             RES[:, kt * TPC + tt * 128:kt * TPC + (tt + 1) * 128],
                             WRF[:, kt * 8:(kt + 1) * 8],
                             start=(kt == 0), stop=False)
                t.matmul(P.bank(tg)[:, ti * 8:(ti + 1) * 8],
                         MROW[0:1, tt * 128:(tt + 1) * 128],
                         MOER[0:1, 0:8], start=False, stop=True)
        phase(P, tensor=logits)

    def lgsc(v):
        for tt in range(16):
            v.scalar_tensor_tensor(LG[:, tt * 8:(tt + 1) * 8],
                                   P.bank(tt // 8)[:, (tt % 8) * 8:(tt % 8) * 8 + 8],
                                   MOEC[:, tt:tt + 1], MOEC[:, 16:24],
                                   ALU.mult, ALU.add)
    phase(P, vector=lgsc)

    def rmax(v):
        for tt in range(16):
            lg = LG[:, tt * 8:(tt + 1) * 8]
            v.tensor_reduce(COLS[:, tt:tt + 1], lg, axis=AX.X, op=ALU.max)
        selfsync(P, v, "v")
        for tt in range(16):
            v.tensor_scalar(COLS[:, 16 + tt:17 + tt], COLS[:, tt:tt + 1],
                            -1.0, None, ALU.mult)
    phase(P, vector=rmax)

    def masks(v):
        for tt in range(16):
            lg = LG[:, tt * 8:(tt + 1) * 8]
            v.tensor_tensor(SCR_F[:, tt * 8:(tt + 1) * 8], lg,
                            COLS[:, tt:tt + 1].broadcast_to([128, 8]), ALU.is_ge)
        selfsync(P, v, "v")
        for tt in range(16):
            v.scalar_tensor_tensor(SCR_F[:, 128 + tt * 8:128 + (tt + 1) * 8],
                                   SCR_F[:, tt * 8:(tt + 1) * 8], -16.0, IOTA8[:, :],
                                   ALU.mult, ALU.add)
        selfsync(P, v, "v")
        for tt in range(16):
            v.tensor_reduce(COLS[:, 32 + tt:33 + tt], SCR_F[:, 128 + tt * 8:128 + (tt + 1) * 8],
                            axis=AX.X, op=ALU.min)
        selfsync(P, v, "v")
        for tt in range(16):
            v.tensor_scalar(COLS[:, 32 + tt:33 + tt], COLS[:, 32 + tt:33 + tt],
                            16.0, None, ALU.add)
    phase(P, vector=masks)

    def expden(sc):
        for tt in range(16):
            lg = LG[:, tt * 8:(tt + 1) * 8]
            sc.activation(SCR_F[:, 256 + tt * 8:256 + (tt + 1) * 8], lg, AF.Exp,
                          bias=COLS[:, 16 + tt:17 + tt], accum_out=COLS[:, 48 + tt:49 + tt])
    phase(P, scalar=expden)

    def gates(v):
        for tt in range(16):
            v.reciprocal(COLS[:, tt:tt + 1], COLS[:, 48 + tt:49 + tt])
    phase(P, vector=gates)

    for g in range(4):
        def trr(t, g=g):
            for j in range(4):
                t.transpose(P.bank(2)[0:1, j * 128:(j + 1) * 128],
                            COLS[:, 32 + g * 4 + j:33 + g * 4 + j], IDENT[:, :])
            for j in range(4):
                t.transpose(P.bank(3)[0:1, j * 128:(j + 1) * 128],
                            COLS[:, g * 4 + j:1 + g * 4 + j], IDENT[:, :])
        phase(P, tensor=trr)

        def cpr(sc, g=g):
            sc.copy(IDXROW[0:1, g * 512:(g + 1) * 512], P.bank(2)[0:1, 0:512])
            sc.copy(GATROW[0:1, g * 512:(g + 1) * 512], P.bank(3)[0:1, 0:512])
        phase(P, scalar=cpr)

    for half in range(2):
        cks = (2 * half, 2 * half + 1)

        def squares(v, cks=cks):
            for j, ck in enumerate(cks):
                for dt in range(NDT):
                    v.tensor_tensor(SQ_B[:, j * 4096 + dt * 512:j * 4096 + (dt + 1) * 512],
                                    WRK_B[:, dt * TPC + ck * 512:dt * TPC + ck * 512 + 512],
                                    WRK_B[:, dt * TPC + ck * 512:dt * TPC + ck * 512 + 512],
                                    ALU.mult)
        phase(P, vector=squares)

        def sums2(t, cks=cks):
            for j, ck in enumerate(cks):
                for dt in range(NDT):
                    t.matmul(P.bank(4 + j)[0:1, 0:512], ONESBC[:, 0:1],
                             WRK_B[:, dt * TPC + ck * 512:dt * TPC + ck * 512 + 512],
                             start=(dt == 0), stop=(dt == NDT - 1))
                for dt in range(NDT):
                    t.matmul(P.bank(6 + j)[0:1, 0:512], ONESBC[:, 0:1],
                             SQ_B[:, j * 4096 + dt * 512:j * 4096 + (dt + 1) * 512],
                             start=(dt == 0), stop=(dt == NDT - 1))
        phase(P, tensor=sums2)

        def rows(v, cks=cks):
            for j, ck in enumerate(cks):
                c0 = ck * 512
                v.tensor_scalar(MROW[0:1, c0:c0 + 512], P.bank(4 + j)[0:1, 0:512],
                                1.0 / E, None, ALU.mult)
            selfsync(P, v, "v")
            for j, ck in enumerate(cks):
                c0 = ck * 512
                v.tensor_tensor(TROW[0:1, c0:c0 + 512], MROW[0:1, c0:c0 + 512],
                                MROW[0:1, c0:c0 + 512], ALU.mult)
            selfsync(P, v, "v")
            for j, ck in enumerate(cks):
                c0 = ck * 512
                v.scalar_tensor_tensor(TROW[0:1, c0:c0 + 512], P.bank(6 + j)[0:1, 0:512],
                                       1.0 / E, TROW[0:1, c0:c0 + 512],
                                       ALU.mult, ALU.subtract)
        phase(P, vector=rows)

    def inv2a(v):
        v.tensor_scalar(TROW[0:1, 0:TPC], TROW[0:1, 0:TPC], 1e-5, None, ALU.add)
        selfsync(P, v, "v")
        v.reciprocal(IROW[0:1, 0:TPC], TROW[0:1, 0:TPC])
    phase(P, vector=inv2a)

    def inv2b(sc):
        sc.activation(IROW[0:1, 0:TPC], IROW[0:1, 0:TPC], AF.Sqrt)
    phase(P, scalar=inv2b)

    def brows(v):
        v.tensor_copy(IBROW[0:1, 0:TPC], IROW[0:1, 0:TPC])
        v.tensor_tensor(TROW[0:1, 0:TPC], MROW[0:1, 0:TPC], IROW[0:1, 0:TPC], ALU.mult)
        selfsync(P, v, "v")
        v.tensor_copy(MIBROW[0:1, 0:TPC], TROW[0:1, 0:TPC])
    phase(P, vector=brows)

    for pair in range(2):
        def bc(t, pair=pair):
            for j in range(2):
                c0 = (pair * 2 + j) * 512
                t.matmul(P.bank(j)[:, 0:512], ONES1B[0:1, :], IBROW[0:1, c0:c0 + 512],
                         start=True, stop=True)
                t.matmul(P.bank(2 + j)[:, 0:512], ONES1B[0:1, :], MIBROW[0:1, c0:c0 + 512],
                         start=True, stop=True)
        phase(P, tensor=bc)

        def apply(v, pair=pair):
            for j in range(2):
                c0 = (pair * 2 + j) * 512
                for dt in range(NDT):
                    v.tensor_tensor(SCR_F[:, j * 4096 + dt * 512:j * 4096 + (dt + 1) * 512],
                                    WRK_B[:, dt * TPC + c0:dt * TPC + c0 + 512],
                                    P.bank(j)[:, 0:512], ALU.mult)
            selfsync(P, v, "v")
            for j in range(2):
                c0 = (pair * 2 + j) * 512
                for dt in range(NDT):
                    v.tensor_tensor(SCR_F[:, j * 4096 + dt * 512:j * 4096 + (dt + 1) * 512],
                                    SCR_F[:, j * 4096 + dt * 512:j * 4096 + (dt + 1) * 512],
                                    P.bank(2 + j)[:, 0:512], ALU.subtract)
        phase(P, vector=apply)

        def gb(sc, pair=pair):
            for j in range(2):
                c0 = (pair * 2 + j) * 512
                for dt in range(NDT):
                    sc.activation(WRK_B[:, dt * TPC + c0:dt * TPC + c0 + 512],
                                  SCR_F[:, j * 4096 + dt * 512:j * 4096 + (dt + 1) * 512],
                                  AF.Identity, bias=BCOL[:, dt:dt + 1],
                                  scale=GCOL[:, dt:dt + 1])
        phase(P, scalar=gb)

    for ck in range(4):
        c0 = ck * 512
        for e in range(8):
            def ldb(eng, e=e):
                eng.dma_start(out=B1C[:, :], in_=P.inp["moe_b1c"][e, :, :]).then_inc(P.s_tick, 16)
                eng.dma_start(out=FROW[0:1, 0:1024], in_=P.inp["moe_b2r"][e:e + 1, :]).then_inc(P.s_tick, 16)
                return 2
            dphase(P, "sync", ldb)

            GBC = POOL[:, 20992:21504]
            YEV = POOL[:, 21504:22016]

            def gmrow(v, e=e, c0=c0):
                v.tensor_scalar(POOL[0:1, 20480:20992], IDXROW[0:1, c0:c0 + 512],
                                float(e), None, ALU.is_equal)
                selfsync(P, v, "v")
                v.tensor_tensor(POOL[0:1, 20480:20992], POOL[0:1, 20480:20992],
                                GATROW[0:1, c0:c0 + 512], ALU.mult)
                selfsync(P, v, "v")
                v.tensor_copy(BROW[0:1, 0:512], POOL[0:1, 20480:20992])
                v.tensor_copy(BROW[0:1, 512:1536], FROW[0:1, 0:1024])
            phase(P, vector=gmrow)

            def bcgm(t):
                t.matmul(P.bank(0)[:, 0:512], ONES1B[0:1, :], BROW[0:1, 0:512],
                         start=True, stop=True)
            phase(P, tensor=bcgm)

            def cpgm(sc):
                sc.copy(GBC, P.bank(0)[:, 0:512])
            phase(P, scalar=cpgm)

            for it in range(32):
                def wl(eng, e=e, it=it):
                    eng.dma_start(out=WBUF[:, 0:1024],
                                  in_=P.dram["w1b"][(e * 32 + it) * 128:(e * 32 + it + 1) * 128, :]
                                  ).then_inc(P.s_tick, 16)
                    return 1
                dphase(P, "sync", wl)

                def mm(t, it=it, c0=c0):
                    bk = it % 2
                    for kt in range(NDT):
                        t.matmul(P.bank(bk)[:, 0:512], WBUF[:, kt * 128:(kt + 1) * 128],
                                 WRK_B[:, kt * TPC + c0:kt * TPC + c0 + 512],
                                 start=(kt == 0), stop=(kt == NDT - 1))
                phase(P, tensor=mm)

                def relu(sc, it=it):
                    sc.activation(SCR_FB[:, it * 512:(it + 1) * 512],
                                  P.bank(it % 2)[:, 0:512], AF.Relu,
                                  bias=B1C[:, it:it + 1])
                phase(P, scalar=relu)

            for kt in range(32):
                def wl2(eng, e=e, kt=kt):
                    eng.dma_start(out=WBUF2[:, 0:1024],
                                  in_=P.dram["w2b"][(e * 32 + kt) * 128:(e * 32 + kt + 1) * 128, :]
                                  ).then_inc(P.s_tick, 16)
                    return 1
                dphase(P, "sync", wl2)

                def mm2(t, kt=kt):
                    for dt in range(NDT):
                        t.matmul(P.bank(dt)[:, 0:512], WBUF2[:, dt * 128:(dt + 1) * 128],
                                 SCR_FB[:, kt * 512:(kt + 1) * 512],
                                 start=(kt == 0), stop=False)
                phase(P, tensor=mm2)

            def b2add(t):
                for dt in range(NDT):
                    t.matmul(P.bank(dt)[:, 0:512], BROW[0:1, 512 + dt * 128:512 + (dt + 1) * 128],
                             P.sb["ONESB5"][0:1, 0:512], start=False, stop=True)
            phase(P, tensor=b2add)

            def evac(v, c0=c0):
                for dt in range(NDT):
                    v.tensor_tensor(YEV, P.bank(dt)[:, 0:512], GBC, ALU.mult)
                    selfsync(P, v, "v")
                    v.tensor_tensor(RES[:, dt * TPC + c0:dt * TPC + c0 + 512],
                                    RES[:, dt * TPC + c0:dt * TPC + c0 + 512],
                                    YEV, ALU.add)
                    selfsync(P, v, "v")
            phase(P, vector=evac)


EMITTERS["moe"] = emit_moe
EMITTERS["moe2"] = emit_moe




def emit_cast_generic(P, src_name, dst_name, rows):
    SCR_F, SCR_B = P.sb["SCR_F"], P.sb["SCR_B"]
    cols = P.inp[src_name].shape[1]
    for r0 in range(0, rows, 256):
        def load(e, r0=r0, src_name=src_name, cols=cols):
            for j in range(2):
                e.dma_start(out=SCR_F[:, j * cols:(j + 1) * cols],
                            in_=P.inp[src_name][r0 + j * 128:r0 + (j + 1) * 128, :]
                            ).then_inc(P.s_tick, 16)
            return 2
        dphase(P, "sync", load)

        def cast(v, cols=cols):
            v.tensor_copy(SCR_B[:, 0:2 * cols], SCR_F[:, 0:2 * cols])
        phase(P, vector=cast)

        def store(e, r0=r0, dst_name=dst_name, cols=cols):
            for j in range(2):
                e.dma_start(out=P.dram[dst_name][r0 + j * 128:r0 + (j + 1) * 128, :],
                            in_=SCR_B[:, j * cols:(j + 1) * cols]).then_inc(P.s_tick, 16)
            return 2
        dphase(P, "sync", store)


GRIF_W = [("g_win_t", 1024), ("g_wgate_t", 1024), ("g_rgin_t", 1536),
          ("g_rggate_t", 1536), ("g_wout_t", 1536), ("g_mlpg_t", 1024),
          ("g_mlpu_t", 1024), ("g_mlpd_t", 1536)]


def emit_rmsnorm(P, src, dst, wcol_off, GC):
    SCR_F = P.sb["SCR_F"]
    SQ = P.sb["POOL_B"][:, 40960:49152]
    IROW, TROW, IBROW = P.sb["IROW"], P.sb["TROW"], P.sb["IBROW"]
    ONESBC, ONES1B = P.sb["ONESBC"], P.sb["ONES1B"]
    for half in range(2):
        cks = (2 * half, 2 * half + 1)

        def squares(v, cks=cks):
            for j, ck in enumerate(cks):
                for dt in range(NDT):
                    v.tensor_tensor(SQ[:, j * 4096 + dt * 512:j * 4096 + (dt + 1) * 512],
                                    src[:, dt * TPC + ck * 512:dt * TPC + ck * 512 + 512],
                                    src[:, dt * TPC + ck * 512:dt * TPC + ck * 512 + 512],
                                    ALU.mult)
        phase(P, vector=squares)

        def sums2(t, cks=cks):
            for j, ck in enumerate(cks):
                for dt in range(NDT):
                    t.matmul(P.bank(4 + j)[0:1, 0:512], ONESBC[:, 0:1],
                             SQ[:, j * 4096 + dt * 512:j * 4096 + (dt + 1) * 512],
                             start=(dt == 0), stop=(dt == NDT - 1))
        phase(P, tensor=sums2)

        def rows(v, cks=cks):
            for j, ck in enumerate(cks):
                c0 = ck * 512
                v.tensor_scalar(TROW[0:1, c0:c0 + 512], P.bank(4 + j)[0:1, 0:512],
                                1.0 / E, 1e-5, ALU.mult, ALU.add)
        phase(P, vector=rows)

    def inv(v):
        v.reciprocal(IROW[0:1, 0:TPC], TROW[0:1, 0:TPC])
    phase(P, vector=inv)

    def invs(sc):
        sc.activation(IROW[0:1, 0:TPC], IROW[0:1, 0:TPC], AF.Sqrt)
    phase(P, scalar=invs)

    def ibr(v):
        v.tensor_copy(IBROW[0:1, 0:TPC], IROW[0:1, 0:TPC])
    phase(P, vector=ibr)

    for pair in range(2):
        def bc(t, pair=pair):
            for j in range(2):
                c0 = (pair * 2 + j) * 512
                t.matmul(P.bank(j)[:, 0:512], ONES1B[0:1, :], IBROW[0:1, c0:c0 + 512],
                         start=True, stop=True)
        phase(P, tensor=bc)

        def mul(v, pair=pair):
            for j in range(2):
                c0 = (pair * 2 + j) * 512
                for dt in range(NDT):
                    v.tensor_tensor(SCR_F[:, j * 4096 + dt * 512:j * 4096 + (dt + 1) * 512],
                                    src[:, dt * TPC + c0:dt * TPC + c0 + 512],
                                    P.bank(j)[:, 0:512], ALU.mult)
        phase(P, vector=mul)

        def scale(sc, pair=pair):
            for j in range(2):
                c0 = (pair * 2 + j) * 512
                for dt in range(NDT):
                    sc.activation(dst[:, dt * TPC + c0:dt * TPC + c0 + 512],
                                  SCR_F[:, j * 4096 + dt * 512:j * 4096 + (dt + 1) * 512],
                                  AF.Identity, scale=GC[:, wcol_off + dt:wcol_off + dt + 1])
        phase(P, scalar=scale)


def grif_stream_slab(P, dst_sb, dram_name, col0, nkt):
    def wl(e):
        for kt in range(nkt):
            e.dma_start(out=dst_sb[:, kt * 128:(kt + 1) * 128],
                        in_=P.dram[dram_name][kt * 128:(kt + 1) * 128, col0:col0 + 128]
                        ).then_inc(P.s_tick, 16)
        return nkt
    dphase(P, "sync", wl)


def grif_proj_1536(P, rhs_tiles, dram_name, out_cb, gelu_to_dram=None):
    WBUF = P.sb["WBUF"]
    for ct in range(12):
        grif_stream_slab(P, WBUF, dram_name, ct * 128, NDT)
        for ck in range(4):
            def mm(t, ct=ct, ck=ck):
                for kt in range(NDT):
                    t.matmul(P.bank(ck % 2)[:, 0:512], WBUF[:, kt * 128:(kt + 1) * 128],
                             rhs_tiles[:, kt * TPC + ck * 512:kt * TPC + ck * 512 + 512],
                             start=(kt == 0), stop=(kt == NDT - 1))
            phase(P, tensor=mm)
            out_cb(ct, ck)


def emit_griffin(P):
    nc = P.nc
    RES, WRK_B, SCR_F, SCR_B = P.sb["RES"], P.sb["WRK_B"], P.sb["SCR_F"], P.sb["SCR_B"]
    POOL, POOL_B = P.sb["POOL"], P.sb["POOL_B"]
    MROW, IROW, TROW, FROW = P.sb["MROW"], P.sb["IROW"], P.sb["TROW"], P.sb["FROW"]
    IBROW, MIBROW, BROW = P.sb["IBROW"], P.sb["MIBROW"], P.sb["BROW"]
    COLS, IDENT, ONES_F = P.sb["COLS"], P.sb["IDENT"], P.sb["ONES_F"]
    ONES1B, ONESBC = P.sb["ONES1B"], P.sb["ONESBC"]
    WBUF = P.sb["WBUF"]
    GC = P.sb["GRIFC"]
    NH = 12

    for nm, rows in GRIF_W:
        emit_cast_generic(P, nm, nm + "_b", rows)

    def ldcols(e):
        e.dma_start(out=GC[:, :], in_=P.inp["g_cols"][:, :]).then_inc(P.s_tick, 16)
        e.dma_start(out=P.state["halosrc_i"][:, 0:2], in_=P.inp["g_halosrc"][:, :]
                    ).then_inc(P.s_tick, 16)
        return 2
    dphase(P, "sync", ldcols)

    emit_ln(P, want_bf16=True)
    emit_rmsnorm(P, WRK_B, SCR_B, 48, GC)

    def gate_cb(ct, ck):
        def ev(sc, ct=ct, ck=ck):
            sc.activation(SCR_F[:, 0:512], P.bank(ck % 2)[:, 0:512], AF.Gelu)
        phase(P, scalar=ev)

        def st(e, ct=ct, ck=ck):
            e.dma_start(out=P.dram["g2d"][ct * 128:(ct + 1) * 128, ck * 512:(ck + 1) * 512],
                        in_=SCR_F[:, 0:512]).then_inc(P.s_tick, 16)
            return 1
        dphase(P, "sync", st)
    grif_proj_1536(P, SCR_B, "g_wgate_t_b", gate_cb)

    def u_cb(ct, ck):
        def ev(sc, ct=ct, ck=ck):
            sc.copy(SCR_F[:, (ck % 2) * 512:(ck % 2) * 512 + 512],
                    P.bank(ck % 2)[:, 0:512])
        phase(P, scalar=ev)
        if ck % 2 == 1:
            def cvt(v):
                v.tensor_copy(WBUF[:, 2045:2048], SCR_F[:, 1021:1024])
            phase(P, vector=cvt)

            def st(e, ct=ct, ck=ck):
                e.dma_start(out=P.dram["ud"][ct * 128:(ct + 1) * 128,
                                             (ck // 2) * 1024:(ck // 2) * 1024 + 1024],
                            in_=SCR_F[:, 0:1024]).then_inc(P.s_tick, 16)
                e.dma_start(out=P.dram["g_ccin"][ct * 128:(ct + 1) * 128,
                                                 0 + 3 * (ck // 2):3 + 3 * (ck // 2)],
                            in_=WBUF[:, 2045:2048]).then_inc(P.s_tick, 16)
                return 2
            dphase(P, "sync", st)
    grif_proj_1536(P, SCR_B, "g_win_t_b", u_cb)

    def cc(g):
        g.collective_compute("AllGather", ALU.bypass,
                             replica_groups=[[0, 1, 2, 3], [4, 5, 6, 7]],
                             ins=[P.dram["g_ccin"][:, :].opt()],
                             outs=[P.dram["g_ccout"][:, :].opt()])
    phase(P, gpsimd=cc)

    def clrh(v):
        v.memset(WBUF[:, 1024:1168], 0.0)
    phase(P, vector=clrh)

    def rdhalo(g):
        padsA = P.sb["WBUF"][:, 1024:1060].rearrange("p (a b) -> p a b", a=NH)
        padsB = P.sb["WBUF"][:, 1096:1132].rearrange("p (a b) -> p a b", a=NH)
        with g.register("rA") as rA, g.register("rB") as rB:
            g.reg_load(rA, P.state["halosrc_i"][0:1, 0:1])
            g.reg_load(rB, P.state["halosrc_i"][0:1, 1:2])
            g.dma_start(out=padsA,
                        in_=bass.AP(P.state["g_ccout_h"], rA, [[6, 128], [768, NH], [1, 3]])
                        ).then_inc(P.s_swdma, 16)
            g.dma_start(out=padsB,
                        in_=bass.AP(P.state["g_ccout_h"], rB, [[6, 128], [768, NH], [1, 3]])
                        ).then_inc(P.s_swdma, 16)
            P.swdma_n += 32
            g.wait_ge(P.s_swdma, P.swdma_n)
    phase(P, gpsimd=rdhalo)

    def maskh(v):
        v.tensor_scalar(WBUF[:, 1024:1060], WBUF[:, 1024:1060],
                        GC[:, 112:113], None, ALU.mult)
    phase(P, vector=maskh)

    CV = POOL[:, 16384:17408]
    UPAD = POOL[:, 0:2054]
    UCMP = POOL_B[:, 4608:6656]
    for ct in range(NH):
        def ldu(e, ct=ct):
            e.dma_start(out=UPAD[:, 3:1027],
                        in_=P.dram["ud"][ct * 128:(ct + 1) * 128, 0:1024]).then_inc(P.s_tick, 16)
            e.dma_start(out=UPAD[:, 1030:2054],
                        in_=P.dram["ud"][ct * 128:(ct + 1) * 128, 1024:2048]).then_inc(P.s_tick, 16)
            return 2
        dphase(P, "sync", ldu)

        def pads(v, ct=ct):
            v.tensor_copy(UPAD[:, 0:3], WBUF[:, 1024 + ct * 3:1024 + ct * 3 + 3])
            v.tensor_copy(UPAD[:, 1027:1030], WBUF[:, 1096 + ct * 3:1096 + ct * 3 + 3])
        phase(P, vector=pads)

        for side in range(2):
            def conv(v, ct=ct, side=side):
                off = 3 + side * 1027
                v.tensor_scalar(CV[:, 0:1024], UPAD[:, off - 3:off + 1021],
                                GC[:, 64 + 0 * 12 + ct:64 + 0 * 12 + ct + 1], None, ALU.mult)
                selfsync(P, v, "v")
                for j in range(1, 4):
                    v.scalar_tensor_tensor(CV[:, 0:1024], UPAD[:, off - 3 + j:off - 3 + j + 1024],
                                           GC[:, 64 + j * 12 + ct:64 + j * 12 + ct + 1],
                                           CV[:, 0:1024], ALU.mult, ALU.add)
                    selfsync(P, v, "v")
                v.tensor_scalar(CV[:, 0:1024], CV[:, 0:1024], GC[:, 12 + ct:13 + ct],
                                None, ALU.add)
                selfsync(P, v, "v")
                v.tensor_copy(UCMP[:, side * 1024:side * 1024 + 1024], CV[:, 0:1024])
                selfsync(P, v, "v")
            phase(P, vector=conv)

            def stuc32(e, ct=ct, side=side):
                e.dma_start(out=P.dram["ucd32"][ct * 128:(ct + 1) * 128,
                                                side * 1024:(side + 1) * 1024],
                            in_=CV[:, 0:1024]).then_inc(P.s_tick, 16)
                return 1
            dphase(P, "sync", stuc32)

        def stuc(e, ct=ct):
            e.dma_start(out=P.dram["ucd"][ct * 128:(ct + 1) * 128, :],
                        in_=UCMP[:, 0:2048]).then_inc(P.s_tick, 16)
            return 1
        dphase(P, "sync", stuc)

    dbg_hook(P, "uconv", P.sb["POOL"][:, 0:1027], 1027)
    dbg_hook(P, "uconv5", P.sb["POOL"][:, 5135:6162], 1027)
    if STOP_AFTER == "uconv":
        return
    emit_griffin_scan(P)
    if STOP_AFTER == "cols":
        return
    dbg_hook(P, "hrec", P.sb["POOL"][:, 0:1027], 1027)
    if STOP_AFTER == "hrec":
        return
    emit_griffin_tail(P)



def emit_griffin_scan(P):
    RES, WRK_B, SCR_F, SCR_B = P.sb["RES"], P.sb["WRK_B"], P.sb["SCR_F"], P.sb["SCR_B"]
    POOL, POOL_B = P.sb["POOL"], P.sb["POOL_B"]
    WBUF = P.sb["WBUF"]
    GC = P.sb["GRIFC"]
    GC2 = POOL[:, 20480:20576]
    GFOLD = POOL[:, 20576:20768]
    CV = POOL[:, 16384:17408]
    CV2 = POOL[:, 17408:18432]
    AA = POOL[:, 18432:19456]
    SB2 = POOL[:, 19456:20480]
    NH = 12

    def fl2(sc):
        sc.activation(GC2[:, 84:96], GC[:, 0:12], AF.Abs)
        selfsync(P, sc, "a")
        sc.activation(GC2[:, 84:96], GC2[:, 84:96], AF.Exp, scale=-1.0)
        selfsync(P, sc, "a")
        sc.activation(GC2[:, 84:96], GC2[:, 84:96], AF.Ln, bias=1.0)
    phase(P, scalar=fl2)

    def fl3(v):
        v.tensor_scalar(GC2[:, 60:72], GC[:, 0:12], 0.0, None, ALU.max)
        selfsync(P, v, "v")
        v.tensor_tensor(GC2[:, 60:72], GC2[:, 60:72], GC2[:, 84:96], ALU.add)
        selfsync(P, v, "v")
        v.tensor_scalar(GC2[:, 60:72], GC2[:, 60:72], -8.0, None, ALU.mult)
    phase(P, vector=fl3)

    UCBUF = POOL_B[:, 0:12288]

    XBU32 = POOL[:, 21504:22528]

    def scan_pass(ct, ck2, phase_b):
        def lduc(e, ct=ct, ck2=ck2):
            for kt in range(NH):
                e.dma_start(out=UCBUF[:, kt * 1024:(kt + 1) * 1024],
                            in_=P.dram["ucd"][kt * 128:(kt + 1) * 128,
                                              ck2 * 1024:ck2 * 1024 + 1024]
                            ).then_inc(P.s_tick, 16)
            e.dma_start(out=XBU32[:, 0:1024],
                        in_=P.dram["ucd32"][ct * 128:(ct + 1) * 128,
                                            ck2 * 1024:ck2 * 1024 + 1024]
                        ).then_inc(P.s_tick, 16)
            return NH + 1
        dphase(P, "sync", lduc)

        grif_stream_slab(P, WBUF, "g_rggate_t_b", ct * 128, NH)

        def mmrg(t, ck2=ck2):
            for half in range(2):
                for kt in range(NH):
                    t.matmul(P.bank(half)[:, 0:512], WBUF[:, kt * 128:(kt + 1) * 128],
                             UCBUF[:, kt * 1024 + half * 512:kt * 1024 + half * 512 + 512],
                             start=(kt == 0), stop=(kt == NH - 1))
        phase(P, tensor=mmrg)

        def ael(sc, ct=ct):
            sc.activation(CV2[:, 0:512], P.bank(0)[:, 0:512], AF.Sigmoid,
                          bias=GC[:, 36 + ct:37 + ct], scale=1.0)
            sc.activation(CV2[:, 512:1024], P.bank(1)[:, 0:512], AF.Sigmoid,
                          bias=GC[:, 36 + ct:37 + ct], scale=1.0)
        phase(P, scalar=ael)

        def ael2(v, ct=ct):
            v.tensor_scalar(CV2[:, 0:1024], CV2[:, 0:1024], GC2[:, 60 + ct:61 + ct],
                            None, ALU.mult)
        phase(P, vector=ael2)

        def ael3(sc):
            sc.activation(AA[:, 0:1024], CV2[:, 0:1024], AF.Exp)
        phase(P, scalar=ael3)

        grif_stream_slab(P, WBUF, "g_rgin_t_b", ct * 128, NH)

        def mmig(t, ck2=ck2):
            for half in range(2):
                for kt in range(NH):
                    t.matmul(P.bank(2 + half)[:, 0:512], WBUF[:, kt * 128:(kt + 1) * 128],
                             UCBUF[:, kt * 1024 + half * 512:kt * 1024 + half * 512 + 512],
                             start=(kt == 0), stop=(kt == NH - 1))
        phase(P, tensor=mmig)

        def bel(sc, ct=ct):
            sc.activation(SB2[:, 0:512], P.bank(2)[:, 0:512], AF.Sigmoid,
                          bias=GC[:, 24 + ct:25 + ct], scale=1.0)
            sc.activation(SB2[:, 512:1024], P.bank(3)[:, 0:512], AF.Sigmoid,
                          bias=GC[:, 24 + ct:25 + ct], scale=1.0)
        phase(P, scalar=bel)

        def bet(v):
            v.tensor_tensor(CV[:, 0:1024], AA[:, 0:1024], AA[:, 0:1024], ALU.mult)
            selfsync(P, v, "v")
            v.tensor_scalar(CV[:, 0:1024], CV[:, 0:1024], -1.0, 1.0 + 1e-6,
                            ALU.mult, ALU.add)
        phase(P, vector=bet)

        def bet2(sc):
            sc.activation(CV[:, 0:1024], CV[:, 0:1024], AF.Sqrt)
        phase(P, scalar=bet2)

        def xbv(v, ct=ct):
            v.tensor_tensor(SB2[:, 0:1024], SB2[:, 0:1024], CV[:, 0:1024], ALU.mult)
            selfsync(P, v, "v")
            v.tensor_tensor(SB2[:, 0:1024], SB2[:, 0:1024], XBU32[:, 0:1024], ALU.mult)
        phase(P, vector=xbv)

        if not phase_b:
            def lna(sc):
                sc.activation(CV2[:, 0:1024], AA[:, 0:1024], AF.Ln)
            phase(P, scalar=lna)

            def sc1(v, ct=ct, ck2=ck2):
                v.tensor_tensor_scan(CV[:, 0:1024], AA[:, 0:1024], SB2[:, 0:1024],
                                     0.0, ALU.mult, ALU.add)
                v.tensor_reduce(GC2[:, ct * 4 + 2 * ck2:ct * 4 + 2 * ck2 + 1],
                                CV2[:, 0:1024], axis=AX.X, op=ALU.add)
                selfsync(P, v, "v")
                v.tensor_copy(GC2[:, ct * 4 + 2 * ck2 + 1:ct * 4 + 2 * ck2 + 2],
                              CV[:, 1023:1024])
            phase(P, vector=sc1)

            def expa(sc, ct=ct, ck2=ck2):
                sc.activation(GC2[:, ct * 4 + 2 * ck2:ct * 4 + 2 * ck2 + 1],
                              GC2[:, ct * 4 + 2 * ck2:ct * 4 + 2 * ck2 + 1], AF.Exp)
            phase(P, scalar=expa)
        else:
            def sc2(v, ct=ct, ck2=ck2):
                hin = GC2[:, (48 if ck2 == 0 else 72) + ct:(48 if ck2 == 0 else 72) + ct + 1]
                v.tensor_tensor_scan(CV[:, 0:1024], AA[:, 0:1024], SB2[:, 0:1024],
                                     hin, ALU.mult, ALU.add)
            phase(P, vector=sc2)

            def sthr(e, ct=ct, ck2=ck2):
                e.dma_start(out=P.dram["hrecd"][ct * 128:(ct + 1) * 128,
                                                ck2 * 1024:(ck2 + 1) * 1024],
                            in_=CV[:, 0:1024]).then_inc(P.s_tick, 16)
                return 1
            dphase(P, "sync", sthr)

    for ct in range(NH):
        for ck2 in range(2):
            scan_pass(ct, ck2, False)

    def pub2(e):
        for ct in range(NH):
            e.dma_start(out=P.dram["g_ccin2"][ct * 128:(ct + 1) * 128, :],
                        in_=GC2[:, ct * 4:ct * 4 + 4]).then_inc(P.s_tick, 16)
        return NH
    dphase(P, "sync", pub2)

    def cc2(g):
        g.collective_compute("AllGather", ALU.bypass,
                             replica_groups=[[0, 1, 2, 3], [4, 5, 6, 7]],
                             ins=[P.dram["g_ccin2"][:, :].opt()],
                             outs=[P.dram["g_ccout2"][:, :].opt()])
    phase(P, gpsimd=cc2)

    def rdfold(g):
        n = 0
        for j in range(7):
            slot = j if j < 4 else 7 - j
            colA = 0 if j < 4 else 2
            g.dma_start(out=GFOLD[:, j * 24:j * 24 + 12],
                        in_=bass.AP(P.state["g_ccout2_h"], (slot * 1536) * 4 + colA,
                                    [[4, 128], [512, 12], [1, 1]]),
                        allow_slow_non_contiguous=True).then_inc(P.s_swdma, 16)
            g.dma_start(out=GFOLD[:, j * 24 + 12:j * 24 + 24],
                        in_=bass.AP(P.state["g_ccout2_h"], (slot * 1536) * 4 + colA + 1,
                                    [[4, 128], [512, 12], [1, 1]]),
                        allow_slow_non_contiguous=True).then_inc(P.s_swdma, 16)
            n += 2
        P.swdma_n += 16 * n
        g.wait_ge(P.s_swdma, P.swdma_n)
    phase(P, gpsimd=rdfold)

    def fold(v):
        acc = GFOLD[:, 168:180]
        v.memset(acc, 0.0)
        v.memset(GC2[:, 48:60], 0.0)
        v.memset(GC2[:, 72:84], 0.0)
        selfsync(P, v, "v")
        for j in range(7):
            v.tensor_tensor(acc, acc, GFOLD[:, j * 24:j * 24 + 12], ALU.mult)
            selfsync(P, v, "v")
            v.tensor_tensor(acc, acc, GFOLD[:, j * 24 + 12:j * 24 + 24], ALU.add)
            selfsync(P, v, "v")
            v.scalar_tensor_tensor(GC2[:, 48:60], acc, GC[:, 114 + j:115 + j],
                                   GC2[:, 48:60], ALU.mult, ALU.add)
            v.scalar_tensor_tensor(GC2[:, 72:84], acc, GC[:, 121 + j:122 + j],
                                   GC2[:, 72:84], ALU.mult, ALU.add)
            selfsync(P, v, "v")
    phase(P, vector=fold)

    dbg_hook(P, "cols", POOL[:, 20480:20768], 288)
    dbg_hook(P, "hin", POOL[:, 20480:20768], 288)
    if STOP_AFTER == "cols":
        return
    for ct in range(NH):
        for ck2 in range(2):
            scan_pass(ct, ck2, True)


def emit_griffin_tail(P):
    RES, WRK_B, SCR_F, SCR_B = P.sb["RES"], P.sb["WRK_B"], P.sb["SCR_F"], P.sb["SCR_B"]
    POOL, POOL_B = P.sb["POOL"], P.sb["POOL_B"]
    WBUF = P.sb["WBUF"]
    GC = P.sb["GRIFC"]
    CV = POOL[:, 16384:17408]
    NH = 12

    def g2h_base(ct):
        return ct * 2048 if ct < 8 else 32768 + (ct - 8) * 2048

    STGF = POOL[:, 20480:22528]
    for ct in range(NH):
        for hf in range(2):
            def ldg2(e, ct=ct, hf=hf):
                e.dma_start(out=STGF[:, 0:1024],
                            in_=P.dram["g2d"][ct * 128:(ct + 1) * 128,
                                              hf * 1024:(hf + 1) * 1024]
                            ).then_inc(P.s_tick, 16)
                e.dma_start(out=STGF[:, 1024:2048],
                            in_=P.dram["hrecd"][ct * 128:(ct + 1) * 128,
                                                hf * 1024:(hf + 1) * 1024]
                            ).then_inc(P.s_tick, 16)
                return 2
            dphase(P, "sync", ldg2)

            def mulg(v, ct=ct, hf=hf):
                b0 = g2h_base(ct) + hf * 1024
                v.tensor_tensor(P.sb["POOL_B"][:, b0:b0 + 1024], STGF[:, 1024:2048],
                                STGF[:, 0:1024], ALU.mult)
            phase(P, vector=mulg)

    dbg_hook(P, "g2hlo", P.sb["POOL"][:, 0:4096], 4096)
    dbg_hook(P, "g2hmd", P.sb["POOL"][:, 4096:8192], 4096)
    dbg_hook(P, "g2hhi", P.sb["POOL"][:, 16384:20480], 4096)
    if STOP_AFTER == "g2hX":
        return

    for dt in range(NDT):
        def wl(e, dt=dt):
            for kt in range(NH):
                e.dma_start(out=WBUF[:, kt * 128:(kt + 1) * 128],
                            in_=P.dram["g_wout_t_b"][kt * 128:(kt + 1) * 128,
                                                     dt * 128:(dt + 1) * 128]
                            ).then_inc(P.s_tick, 16)
            return NH
        dphase(P, "sync", wl)

        for ck in range(4):
            def mm(t, dt=dt, ck=ck):
                for kt in range(NH):
                    koff = g2h_base(kt) + ck * 512
                    t.matmul(P.bank(4 + ck % 2)[:, 0:512], WBUF[:, kt * 128:(kt + 1) * 128],
                             P.sb["POOL_B"][:, koff:koff + 512],
                             start=(kt == 0), stop=(kt == NH - 1))
            phase(P, tensor=mm)

            CVX = POOL[:, 22528:23040]

            def ev(v, dt=dt, ck=ck):
                c0 = ck * 512
                v.tensor_tensor(CVX, P.bank(4 + ck % 2)[:, 0:512],
                                WRK_B[:, dt * TPC + c0:dt * TPC + c0 + 512], ALU.add)
                selfsync(P, v, "v")
                v.tensor_tensor(RES[:, dt * TPC + c0:dt * TPC + c0 + 512],
                                RES[:, dt * TPC + c0:dt * TPC + c0 + 512],
                                CVX, ALU.add)
            phase(P, vector=ev)

            def xic(sc, dt=dt, ck=ck):
                c0 = ck * 512
                sc.copy(WRK_B[:, dt * TPC + c0:dt * TPC + c0 + 512], CVX)
            phase(P, scalar=xic)

    dbg_hook(P, "g2h", P.sb["POOL"][:, 0:1027], 1027)
    dbg_hook(P, "g2h5", P.sb["POOL"][:, 5135:6162], 1027)
    if STOP_AFTER == "g2h":
        return
    dbg_hook(P, "xi", P.sb["POOL"][:, 8192:12288], 4096)
    if STOP_AFTER == "xi":
        return
    emit_rmsnorm(P, WRK_B, SCR_B, 56, GC)
    dbg_hook(P, "n2", P.sb["POOL"][:, 16384:20480], 4096)
    dbg_hook(P, "n2b", P.sb["POOL"][:, 20480:24576], 4096)
    if STOP_AFTER in ("n2", "n2b"):
        return

    for ct in range(NH):
        def wl2(e, ct=ct):
            for kt in range(NDT):
                e.dma_start(out=WBUF[:, kt * 128:(kt + 1) * 128],
                            in_=P.dram["g_mlpg_t_b"][kt * 128:(kt + 1) * 128,
                                                     ct * 128:(ct + 1) * 128]
                            ).then_inc(P.s_tick, 16)
            for kt in range(NDT):
                e.dma_start(out=WBUF[:, 1024 + kt * 128:1024 + (kt + 1) * 128],
                            in_=P.dram["g_mlpu_t_b"][kt * 128:(kt + 1) * 128,
                                                     ct * 128:(ct + 1) * 128]
                            ).then_inc(P.s_tick, 16)
            return 2 * NDT
        dphase(P, "sync", wl2)

        for ck in range(4):
            def mm(t, ct=ct, ck=ck):
                c0 = ck * 512
                for kt in range(NDT):
                    t.matmul(P.bank(0)[:, 0:512], WBUF[:, kt * 128:(kt + 1) * 128],
                             SCR_B[:, kt * TPC + c0:kt * TPC + c0 + 512],
                             start=(kt == 0), stop=(kt == NDT - 1))
                for kt in range(NDT):
                    t.matmul(P.bank(1)[:, 0:512], WBUF[:, 1024 + kt * 128:1024 + (kt + 1) * 128],
                             SCR_B[:, kt * TPC + c0:kt * TPC + c0 + 512],
                             start=(kt == 0), stop=(kt == NDT - 1))
            phase(P, tensor=mm)

            CVM = POOL[:, 12288:12800]

            def gel(sc, ct=ct, ck=ck):
                sc.activation(CVM, P.bank(0)[:, 0:512], AF.Gelu)
            phase(P, scalar=gel)

            def pr(v, ct=ct, ck=ck):
                v.tensor_tensor(POOL_B[:, ct * 2048 + ck * 512:ct * 2048 + (ck + 1) * 512],
                                CVM, P.bank(1)[:, 0:512], ALU.mult)
            phase(P, vector=pr)

    for dt in range(NDT):
        def wl3(e, dt=dt):
            for kt in range(NH):
                e.dma_start(out=WBUF[:, kt * 128:(kt + 1) * 128],
                            in_=P.dram["g_mlpd_t_b"][kt * 128:(kt + 1) * 128,
                                                     dt * 128:(dt + 1) * 128]
                            ).then_inc(P.s_tick, 16)
            return NH
        dphase(P, "sync", wl3)

        for ck in range(4):
            def mm(t, dt=dt, ck=ck):
                c0 = ck * 512
                for kt in range(NH):
                    t.matmul(P.bank(4 + ck % 2)[:, 0:512], WBUF[:, kt * 128:(kt + 1) * 128],
                             POOL_B[:, kt * 2048 + c0:kt * 2048 + c0 + 512],
                             start=(kt == 0), stop=(kt == NH - 1))
            phase(P, tensor=mm)

            def ev(v, dt=dt, ck=ck):
                c0 = ck * 512
                v.tensor_tensor(RES[:, dt * TPC + c0:dt * TPC + c0 + 512],
                                RES[:, dt * TPC + c0:dt * TPC + c0 + 512],
                                P.bank(4 + ck % 2)[:, 0:512], ALU.add)
            phase(P, vector=ev)


EMITTERS["griffin"] = emit_griffin




MLA_W = [("mla_wq", 1024, 1024), ("mla_wqr", 1024, 512), ("mla_wkr", 1024, 32),
         ("mla_wkvd", 1024, 256), ("mla_wku", 256, 1024), ("mla_wvu", 256, 1024),
         ("mla_wo", 1024, 1024)]

B_KC = 64 * 49152
B_LATR = 0
B_VT = 8192
B_PT = 12352
B_QCT = 15424
B_DGA = 32768
B_DGB = 35328
B_RT = 37376
B_KILL = 41472


def emit_mla(P):
    nc = P.nc
    RES, WRK_B, SCR_F = P.sb["RES"], P.sb["WRK_B"], P.sb["SCR_F"]
    POOL, POOL_B = P.sb["POOL"], P.sb["POOL_B"]
    poolh = P.poolh
    WBUF, IDENT = P.sb["WBUF"], P.sb["IDENT"]
    IDENTB = P.sb["IDENTB"]
    COLS, FROW = P.sb["COLS"], P.sb["FROW"]
    ONES1B = P.sb["ONES1B"]
    LOBOC = P.sb["LOBOC"]
    QSCALE = 1.0 / (96.0 ** 0.5)

    for nm, rows, cols in MLA_W:
        emit_cast_generic(P, nm, nm + "_b", rows)

    emit_ln(P, want_bf16=True)

    def ldc(e):
        e.dma_start(out=SCR_F[0:32, 0:4096], in_=P.inp["mla_rope"][:, :]).then_inc(P.s_tick, 16)
        e.dma_start(out=LOBOC[:, :], in_=P.inp["mla_loboc"][:, :]).then_inc(P.s_tick, 16)
        return 2
    dphase(P, "sync", ldc)

    def cstc(v):
        v.tensor_copy(POOL_B[0:32, B_RT:B_RT + 4096], SCR_F[0:32, 0:4096])
    phase(P, vector=cstc)

    def ldk(e):
        e.dma_start(out=SCR_F[0:1, 0:7680], in_=P.inp["mla_kill"][:, :]).then_inc(P.s_tick, 16)
        return 1
    dphase(P, "sync", ldk)

    def cstk(v):
        v.tensor_copy(POOL_B[0:1, B_KILL:B_KILL + 7680], SCR_F[0:1, 0:7680])
    phase(P, vector=cstk)


    def cstc2(sc):
        sc.activation(LOBOC[:, :], LOBOC[:, :], AF.Exp)
    phase(P, scalar=cstc2)

    for half in range(2):
        def ldd(e, half=half):
            nm = "mla_dgA" if half == 0 else "mla_dgB"
            w = 2560 if half == 0 else 2048
            e.dma_start(out=SCR_F[:, 0:w], in_=P.inp[nm][:, :]).then_inc(P.s_tick, 16)
            return 1
        dphase(P, "sync", ldd)

        def cstd(v, half=half):
            if half == 0:
                v.tensor_copy(POOL_B[:, B_DGA:B_DGA + 2560], SCR_F[:, 0:2560])
            else:
                v.tensor_copy(POOL_B[:, B_DGB:B_DGB + 2048], SCR_F[:, 0:2048])
        phase(P, vector=cstd)

    for mt in range(2):
        def wl(e, mt=mt):
            for kt in range(NDT):
                e.dma_start(out=WBUF[:, kt * 128:(kt + 1) * 128],
                            in_=P.dram["mla_wkvd_b"][kt * 128:(kt + 1) * 128,
                                                     mt * 128:(mt + 1) * 128]
                            ).then_inc(P.s_tick, 16)
            return NDT
        dphase(P, "sync", wl)

        for ck in range(4):
            def mm(t, mt=mt, ck=ck):
                for kt in range(NDT):
                    t.matmul(P.bank(ck % 2)[:, 0:512], WBUF[:, kt * 128:(kt + 1) * 128],
                             WRK_B[:, kt * TPC + ck * 512:kt * TPC + ck * 512 + 512],
                             start=(kt == 0), stop=(kt == NDT - 1))
            phase(P, tensor=mm)

            def ev(sc, ck=ck):
                sc.copy(POOL_B[:, B_PT:B_PT + 512], P.bank(ck % 2)[:, 0:512])
            phase(P, scalar=ev)

            def st(e, mt=mt, ck=ck):
                e.dma_start(out=P.dram["mla_ccin"][mt * 128:(mt + 1) * 128,
                                                   ck * 512:(ck + 1) * 512],
                            in_=POOL_B[:, B_PT:B_PT + 512]).then_inc(P.s_tick, 16)
                return 1
            dphase(P, "sync", st)

    def wlr(e):
        for kt in range(NDT):
            e.dma_start(out=WBUF[:, kt * 32:(kt + 1) * 32],
                        in_=P.dram["mla_wkr_b"][kt * 128:(kt + 1) * 128, 0:32]
                        ).then_inc(P.s_tick, 16)
        return NDT
    dphase(P, "sync", wlr)

    for ck in range(4):
        def mm(t, ck=ck):
            for kt in range(NDT):
                t.matmul(P.bank(2)[0:32, 0:512], WBUF[:, kt * 32:(kt + 1) * 32],
                         WRK_B[:, kt * TPC + ck * 512:kt * TPC + ck * 512 + 512],
                         start=(kt == 0), stop=(kt == NDT - 1))
        phase(P, tensor=mm)
        if STOP_AFTER == "mlakr" and ck == 0:
            def dmp0(sc):
                sc.copy(P.sb["SCR_F"][0:32, 0:512], P.bank(2)[0:32, 0:512])
            phase(P, scalar=dmp0)
            dbg_hook(P, "mlakr", P.sb["SCR_F"][0:32, 0:512], 512)
        emit_mla_rope(P, P.bank(2), ck, POOL_B[0:32, 14912:15424], 1.0)
        if STOP_AFTER == "mlakr" and ck == 0:
            dbg_hook(P, "mlakr2", POOL[0:32, 6176:7200], 1024)
            dbg_hook(P, "mlakr3", POOL[0:32, 7456:7712], 256)
            return

        def st(e, ck=ck):
            e.dma_start(out=P.dram["mla_ccin"][256:288, ck * 512:(ck + 1) * 512],
                        in_=POOL_B[0:32, 14912:15424]).then_inc(P.s_tick, 16)
            return 1
        dphase(P, "sync", st)

    def cc(g):
        g.collective_compute("AllGather", ALU.bypass,
                             replica_groups=[[0, 1, 2, 3], [4, 5, 6, 7]],
                             ins=[P.dram["mla_ccin"][:, :].opt()],
                             outs=[P.dram["mla_ccout"][:, :].opt()])
    phase(P, gpsimd=cc)

    def ldsrc(e):
        e.dma_start(out=P.state["mla_ssrc_i"][:, 0:8],
                    in_=P.inp["mla_ssrc"][:, :]).then_inc(P.s_tick, 16)
        return 1
    dphase(P, "sync", ldsrc)

    def rearr(e):
        n = 0
        for s in range(8):
            P.regn += 1
            with e.register(f"sr{P.regn}") as r1:
                e.reg_load(r1, P.state["mla_ssrc_i"][0:1, s:s + 1])
                e.dma_start(out=P.dram["mla_slotbuf"][s * 288:(s + 1) * 288, :],
                            in_=bass.AP(P.state["mla_ccout_h"], r1, [[2048, 288], [1, 1024]])
                            ).then_inc(P.s_swdma, 16)
                n += 1
        P.swdma_n += 16 * n
        e.wait_ge(P.s_swdma, P.swdma_n)
        e.nop().then_inc(P.s_tick, 1)
    P.sec("sync", lambda e: (e.wait_ge(P.s_tick, P.tick), rearr(e)))
    P.tick += 1

    def ldlr(e):
        for s in range(8):
            e.dma_start(out=POOL_B[0:32, s * 1024:(s + 1) * 1024],
                        in_=P.dram["mla_slotbuf"][s * 288 + 256:s * 288 + 288, :]
                        ).then_inc(P.s_tick, 16)
        return 8
    dphase(P, "sync", ldlr)

    def vto(v):
        for g64 in range(64):
            v.memset(POOL_B[:, B_VT + g64 * 65 + 64:B_VT + g64 * 65 + 65], 1.0)
        v.memset(P.sb["E64"][:, 0:1], 0.0)
        selfsync(P, v, "v")
        v.memset(P.sb["E64"][64:65, 0:1], 1.0)
    phase(P, vector=vto)

    dbg_hook(P, "killb4", POOL[0:1, 20736:24576], 3840)
    if STOP_AFTER == "killb4":
        return
    for h in range(16):
        emit_mla_head(P, h)

    if not P.state.get("mla_stop"):
        emit_mla_wo(P)


def emit_mla_rope(P, psum_src, ck, dst_bf, scale):
    POOL, POOL_B = P.sb["POOL"], P.sb["POOL_B"]
    EE = POOL[0:32, 6176:6688]
    SH = POOL[0:32, 6688:7200]
    COSS = POOL_B[0:32, B_RT + ck * 512:B_RT + ck * 512 + 512]
    SINS = POOL_B[0:32, B_RT + 2048 + ck * 512:B_RT + 2048 + ck * 512 + 512]

    def ev(sc):
        sc.copy(EE, psum_src[0:32, 0:512])
    phase(P, scalar=ev)

    def rp(v):
        v.stream_shuffle(SH, EE, list(range(16, 32)) + list(range(0, 16)))
        selfsync(P, v, "v")
        v.tensor_tensor(EE, EE, COSS, ALU.mult)
        v.tensor_tensor(SH, SH, SINS, ALU.mult)
        selfsync(P, v, "v")
        v.tensor_tensor(EE, EE, SH, ALU.add)
        selfsync(P, v, "v")
        v.tensor_scalar(dst_bf, EE, scale, None, ALU.mult)
    phase(P, vector=rp)


def emit_mla_head(P, h):
    nc = P.nc
    RES, WRK_B = P.sb["RES"], P.sb["WRK_B"]
    POOL, POOL_B = P.sb["POOL"], P.sb["POOL_B"]
    poolh = P.poolh
    WBUF = P.sb["WBUF"]
    IDENT, IDENTB = P.sb["IDENT"], P.sb["IDENTB"]
    COLS, FROW = P.sb["COLS"], P.sb["FROW"]
    ONES1B = P.sb["ONES1B"]
    LOBOC = P.sb["LOBOC"]
    QSCALE = 1.0 / (96.0 ** 0.5)
    QCT = POOL_B[64:128, B_QCT:B_QCT + 512]
    QRT = POOL_B[0:32, B_QCT:B_QCT + 512]
    VSTG = POOL[0:64, 6176:6688]

    def wl(e, h=h):
        for kt in range(2):
            e.dma_start(out=WBUF[:, kt * 64:(kt + 1) * 64],
                        in_=P.dram["mla_wku_b"][kt * 128:(kt + 1) * 128,
                                                h * 64:(h + 1) * 64]).then_inc(P.s_tick, 16)
            e.dma_start(out=WBUF[:, 128 + kt * 64:128 + (kt + 1) * 64],
                        in_=P.dram["mla_wvu_b"][kt * 128:(kt + 1) * 128,
                                                h * 64:(h + 1) * 64]).then_inc(P.s_tick, 16)
        for kt in range(NDT):
            e.dma_start(out=WBUF[:, 256 + kt * 64:256 + (kt + 1) * 64],
                        in_=P.dram["mla_wq_b"][kt * 128:(kt + 1) * 128,
                                               h * 64:(h + 1) * 64]).then_inc(P.s_tick, 16)
            e.dma_start(out=WBUF[:, 768 + kt * 32:768 + (kt + 1) * 32],
                        in_=P.dram["mla_wqr_b"][kt * 128:(kt + 1) * 128,
                                                h * 32:(h + 1) * 32]).then_inc(P.s_tick, 16)
        return 4 + 2 * NDT
    dphase(P, "sync", wl)

    for c in range(8):
        def ldl(e, c=c):
            e.dma_start(out=POOL_B[:, 13376:14400],
                        in_=P.dram["mla_slotbuf"][c * 288:c * 288 + 128, :]
                        ).then_inc(P.s_tick, 16)
            e.dma_start(out=POOL_B[:, 14400:15424],
                        in_=P.dram["mla_slotbuf"][c * 288 + 128:c * 288 + 256, :]
                        ).then_inc(P.s_tick, 16)
            return 2
        dphase(P, "sync", ldl)

        for sub in range(2):
            def mmk(t, c=c, sub=sub):
                t.matmul(P.bank(0)[64:128, 0:512], WBUF[:, 0:64],
                         POOL_B[:, 13376 + sub * 512:13376 + sub * 512 + 512],
                         start=True, stop=False, tile_position=(0, 64))
                t.matmul(P.bank(0)[64:128, 0:512], WBUF[:, 64:128],
                         POOL_B[:, 14400 + sub * 512:14400 + sub * 512 + 512],
                         start=False, stop=True, tile_position=(0, 64))
                t.matmul(P.bank(1)[0:64, 0:512], WBUF[:, 128:192],
                         POOL_B[:, 13376 + sub * 512:13376 + sub * 512 + 512],
                         start=True, stop=False)
                t.matmul(P.bank(1)[0:64, 0:512], WBUF[:, 192:256],
                         POOL_B[:, 14400 + sub * 512:14400 + sub * 512 + 512],
                         start=False, stop=True)
            phase(P, tensor=mmk)

            def evk(sc, c=c, sub=sub):
                sc.copy(POOL_B[64:128, c * 1024 + sub * 512:c * 1024 + sub * 512 + 512],
                        P.bank(0)[64:128, 0:512])
                sc.copy(VSTG, P.bank(1)[0:64, 0:512])
            phase(P, scalar=evk)

            for jp in range(2):
                def trv(t, jp=jp):
                    for j2 in range(2):
                        j = jp * 2 + j2
                        t.transpose(P.bank(2 + j2)[:, 0:64], VSTG[:, j * 128:(j + 1) * 128],
                                    IDENT[0:64, 0:64])
                phase(P, tensor=trv)

                def evv(sc, c=c, sub=sub, jp=jp):
                    for j2 in range(2):
                        kt_g = c * 8 + sub * 4 + jp * 2 + j2
                        sc.copy(POOL_B[:, B_VT + kt_g * 65:B_VT + kt_g * 65 + 64],
                                P.bank(2 + j2)[:, 0:64])
                phase(P, scalar=evv)

    for qc in range(4):
        emit_mla_qc(P, h, qc)


def emit_mla_qc(P, h, qc):
    nc = P.nc
    RES, WRK_B = P.sb["RES"], P.sb["WRK_B"]
    POOL, POOL_B = P.sb["POOL"], P.sb["POOL_B"]
    WBUF = P.sb["WBUF"]
    IDENT = P.sb["IDENT"]
    COLS, FROW = P.sb["COLS"], P.sb["FROW"]
    ONES1B = P.sb["ONES1B"]
    LOBOC = P.sb["LOBOC"]
    QSCALE = 1.0 / (96.0 ** 0.5)
    QCT = POOL_B[64:128, B_QCT:B_QCT + 512]
    QRT = POOL_B[0:32, B_QCT:B_QCT + 512]
    YS = POOL[1:65, 6688:7200]
    NKB = POOL_B[0:1, 14912:15424]
    c0 = qc * 512
    jj = qc % 2

    def mmq(t):
        for kt in range(NDT):
            t.matmul(P.bank(0)[64:128, 0:512], WBUF[:, 256 + kt * 64:256 + (kt + 1) * 64],
                     WRK_B[:, kt * TPC + c0:kt * TPC + c0 + 512],
                     start=(kt == 0), stop=(kt == NDT - 1), tile_position=(0, 64))
        for kt in range(NDT):
            t.matmul(P.bank(1)[0:32, 0:512], WBUF[:, 768 + kt * 32:768 + (kt + 1) * 32],
                     WRK_B[:, kt * TPC + c0:kt * TPC + c0 + 512],
                     start=(kt == 0), stop=(kt == NDT - 1))
    phase(P, tensor=mmq)

    def evq(sc):
        sc.activation(QCT, P.bank(0)[64:128, 0:512], AF.Identity, scale=QSCALE)
    phase(P, scalar=evq)

    emit_mla_rope(P, P.bank(1), qc, QRT, QSCALE)

    dbg_hook(P, "killhd", P.sb["POOL"][0:1, 20736:24576], 3840)
    if STOP_AFTER == "killhd" and h == 0 and qc == 0:
        P.state["mla_stop"] = True
    if P.state.get("mla_stop"):
        return

    def clr(v):
        for qt in range(4):
            v.memset(COLS[:, qt:qt + 1], -3e38)
    phase(P, vector=clr)

    if qc < 2:
        visits = [(s, False) for s in range(1, 4)] + [(s, True) for s in range(1, 5)]
    else:
        visits = [(0, True)] + [(s, False) for s in range(1, 8)]
    kbase = 0 if qc < 2 else 3584

    for vi, (s, isdiag) in enumerate(visits):
        krow = kbase + vi * 512
        for blk in range(2):
            for qt in range(4):
                def mmA(t, s=s, krow=krow, blk=blk, qt=qt):
                    bk = P.bank(2 + ((blk * 4 + qt) % 2))
                    t.matmul(bk[:, 0:512],
                             POOL_B[64:128, B_QCT + qt * 128:B_QCT + qt * 128 + 128],
                             POOL_B[64:128, s * 1024 + blk * 512:s * 1024 + blk * 512 + 512],
                             start=True, stop=False, tile_position=(64, 0))
                    t.matmul(bk[:, 0:512],
                             POOL_B[0:32, B_QCT + qt * 128:B_QCT + qt * 128 + 128],
                             POOL_B[0:32, s * 1024 + blk * 512:s * 1024 + blk * 512 + 512],
                             start=False, stop=False)
                    t.matmul(bk[:, 0:512], ONES1B[0:1, :],
                             POOL_B[0:1, B_KILL + krow:B_KILL + krow + 512],
                             start=False, stop=True)
                phase(P, tensor=mmA)

                if isdiag and blk >= jj:
                    pat = qt if blk == jj else 4

                    def msk(v, blk=blk, qt=qt, pat=pat):
                        bk = P.bank(2 + ((blk * 4 + qt) % 2))
                        v.tensor_tensor(bk[:, 0:512], bk[:, 0:512],
                                        POOL_B[:, B_DGA + pat * 512:B_DGA + pat * 512 + 512],
                                        ALU.add)
                    phase(P, vector=msk)

                def red(v, blk=blk, qt=qt):
                    bk = P.bank(2 + ((blk * 4 + qt) % 2))
                    v.tensor_reduce(COLS[:, 8:9], bk[:, 0:512], axis=AX.X, op=ALU.max)
                    selfsync(P, v, "v")
                    v.tensor_tensor(COLS[:, qt:qt + 1], COLS[:, qt:qt + 1],
                                    COLS[:, 8:9], ALU.max)
                phase(P, vector=red)

    def trm(t):
        for qt in range(4):
            t.transpose(P.bank(6)[0:1, qt * 128:(qt + 1) * 128], COLS[:, qt:qt + 1],
                        IDENT[:, :])
    phase(P, tensor=trm)

    def negm(sc):
        sc.activation(FROW[0:1, 0:512], P.bank(6)[0:1, 0:512], AF.Identity, scale=-1.0)
    phase(P, scalar=negm)

    ny = len(visits) * 8
    iy = [0]
    for vi, (s, isdiag) in enumerate(visits):
        krow = kbase + vi * 512

        def nkr(v, krow=krow):
            v.tensor_tensor(FROW[0:1, 512:1024], FROW[0:1, 0:512],
                            POOL_B[0:1, B_KILL + krow:B_KILL + krow + 512], ALU.add)
            selfsync(P, v, "v")
            v.tensor_copy(NKB, FROW[0:1, 512:1024])
        phase(P, vector=nkr)

        for kt in range(8):
            def mmB(t, s=s, kt=kt):
                bk = P.bank(4 + kt % 2)
                t.matmul(bk[:, 0:512],
                         POOL_B[64:128, s * 1024 + kt * 128:s * 1024 + kt * 128 + 128],
                         POOL_B[64:128, B_QCT:B_QCT + 512],
                         start=True, stop=False, tile_position=(64, 0))
                t.matmul(bk[:, 0:512],
                         POOL_B[0:32, s * 1024 + kt * 128:s * 1024 + kt * 128 + 128],
                         POOL_B[0:32, B_QCT:B_QCT + 512],
                         start=False, stop=False)
                t.matmul(bk[:, 0:512], ONES1B[0:1, :], NKB,
                         start=False, stop=True)
            phase(P, tensor=mmB)

            if isdiag:
                d = kt - 4 * jj
                if d >= 0:
                    if d <= 3:
                        moff = B_DGB + d * 512
                    else:
                        moff = B_DGA + 4 * 512

                    def mskB(v, kt=kt, moff=moff):
                        bk = P.bank(4 + kt % 2)
                        v.tensor_tensor(bk[:, 0:512], bk[:, 0:512],
                                        POOL_B[:, moff:moff + 512], ALU.add)
                    phase(P, vector=mskB)

            def expB(sc, kt=kt):
                sc.activation(POOL_B[:, B_PT + (kt % 2) * 512:B_PT + (kt % 2) * 512 + 512],
                              P.bank(4 + kt % 2)[:, 0:512], AF.Exp)
            phase(P, scalar=expB)

            def pv(t, s=s, kt=kt, first=(iy[0] == 0), last=(iy[0] == ny - 1)):
                t.matmul(P.bank(6)[0:65, 0:512],
                         POOL_B[:, B_VT + s * 520 + kt * 65:B_VT + s * 520 + kt * 65 + 65],
                         POOL_B[:, B_PT + (kt % 2) * 512:B_PT + (kt % 2) * 512 + 512],
                         start=first, stop=last)
            phase(P, tensor=pv)
            iy[0] += 1

    YSF = POOL[0:96, 6688:7200]

    def evy(sc):
        sc.copy(POOL[0:65, 6688:7200], P.bank(6)[0:65, 0:512])
    phase(P, scalar=evy)

    def exd(t):
        t.matmul(P.bank(7)[0:1, 0:512], P.sb["E64"][0:65, 0:1], POOL[0:65, 6688:7200],
                 start=True, stop=True)
    phase(P, tensor=exd)

    def dn(v, h=h):
        v.tensor_scalar(FROW[0:1, 0:512], P.bank(7)[0:1, 0:512],
                        LOBOC[0:1, h:h + 1], None, ALU.add)
        selfsync(P, v, "v")
        v.reciprocal(FROW[0:1, 512:1024], FROW[0:1, 0:512])
    phase(P, vector=dn)

    def bcr(t):
        t.matmul(P.bank(7)[0:64, 0:512], P.sb["ONESROW_F"][0:1, 0:64],
                 FROW[0:1, 512:1024], start=True, stop=True)
    phase(P, tensor=bcr)

    def scy(v):
        v.tensor_tensor(POOL_B[0:64, 14400:14912], POOL[0:64, 6688:7200],
                        P.bank(7)[0:64, 0:512], ALU.mult)
    phase(P, vector=scy)

    def sty(e, h=h, qc=qc):
        e.dma_start(out=P.dram["mla_yhd"][h * 64:(h + 1) * 64, qc * 512:(qc + 1) * 512],
                    in_=POOL_B[0:64, 14400:14912]).then_inc(P.s_tick, 16)
        return 1
    dphase(P, "sync", sty)


def emit_mla_wo(P):
    RES, POOL_B = P.sb["RES"], P.sb["POOL_B"]
    WBUF = P.sb["WBUF"]
    for qc in range(4):
        def ldy(e, qc=qc):
            for kt in range(NDT):
                e.dma_start(out=POOL_B[:, 32768 + kt * 512:32768 + (kt + 1) * 512],
                            in_=P.dram["mla_yhd"][kt * 128:(kt + 1) * 128,
                                                  qc * 512:(qc + 1) * 512]).then_inc(P.s_tick, 16)
            return NDT
        dphase(P, "sync", ldy)

        for dt in range(NDT):
            def wl(e, dt=dt):
                for kt in range(NDT):
                    e.dma_start(out=WBUF[:, kt * 128:(kt + 1) * 128],
                                in_=P.dram["mla_wo_b"][kt * 128:(kt + 1) * 128,
                                                       dt * 128:(dt + 1) * 128]
                                ).then_inc(P.s_tick, 16)
                return NDT
            dphase(P, "sync", wl)

            def mm(t, dt=dt):
                for kt in range(NDT):
                    t.matmul(P.bank(dt % 2)[:, 0:512], WBUF[:, kt * 128:(kt + 1) * 128],
                             POOL_B[:, 32768 + kt * 512:32768 + (kt + 1) * 512],
                             start=(kt == 0), stop=(kt == NDT - 1))
            phase(P, tensor=mm)

            def ev(v, dt=dt, qc=qc):
                v.tensor_tensor(RES[:, dt * TPC + qc * 512:dt * TPC + qc * 512 + 512],
                                RES[:, dt * TPC + qc * 512:dt * TPC + qc * 512 + 512],
                                P.bank(dt % 2)[:, 0:512], ALU.add)
            phase(P, vector=ev)


EMITTERS["mla"] = emit_mla



WL_CAST = [("wl_wq", 1024, 1024), ("wl_wk", 1024, 1024), ("wl_wv", 1024, 1024),
           ("wl_wgq", 1024, 512), ("wl_wgk", 1024, 512), ("wl_wgv", 1024, 512),
           ("wl_rout", 512, 1024), ("wl_wout", 1024, 1024)]
W_HALA, W_HALB = 8192, 10240
W_LATB, W_PT, W_VT, W_BAND, W_QFM = 12288, 12800, 13312, 13952, 14464
W_WMA, W_WMT = 32768, 33792
W_KFM, W_VFM, W_REMH, W_ZST = 34816, 36096, 37376, 39424
W_GQB, W_GKF, W_GVF, W_GVT = 41472, 41984, 42048, 42112
W_GVEC, W_OC64, W_OC128, W_BK0C = 42176, 42208, 42210, 42212
W_GL, W_GR, W_BK0A, W_BK0B = 42240, 44288, 46336, 46848
W_LM, W_RCB, W_LGB, W_NMB = 47360, 47360, 47616, 47872
W_NLB, W_LRB = 43264, 43776


def emit_wo_generic(P, ydram, wdram):
    RES, POOL_B = P.sb["RES"], P.sb["POOL_B"]
    WBUF = P.sb["WBUF"]
    for qc in range(4):
        def ldy(e, qc=qc):
            for kt in range(NDT):
                e.dma_start(out=POOL_B[:, 32768 + kt * 512:32768 + (kt + 1) * 512],
                            in_=P.dram[ydram][kt * 128:(kt + 1) * 128,
                                              qc * 512:(qc + 1) * 512]).then_inc(P.s_tick, 16)
            return NDT
        dphase(P, "sync", ldy)

        for dt in range(NDT):
            def wl(e, dt=dt):
                for kt in range(NDT):
                    e.dma_start(out=WBUF[:, kt * 128:(kt + 1) * 128],
                                in_=P.dram[wdram][kt * 128:(kt + 1) * 128,
                                                  dt * 128:(dt + 1) * 128]
                                ).then_inc(P.s_tick, 16)
                return NDT
            dphase(P, "sync", wl)

            def mm(t, dt=dt):
                for kt in range(NDT):
                    t.matmul(P.bank(dt % 2)[:, 0:512], WBUF[:, kt * 128:(kt + 1) * 128],
                             POOL_B[:, 32768 + kt * 512:32768 + (kt + 1) * 512],
                             start=(kt == 0), stop=(kt == NDT - 1))
            phase(P, tensor=mm)

            def ev(v, dt=dt, qc=qc):
                v.tensor_tensor(RES[:, dt * TPC + qc * 512:dt * TPC + qc * 512 + 512],
                                RES[:, dt * TPC + qc * 512:dt * TPC + qc * 512 + 512],
                                P.bank(dt % 2)[:, 0:512], ALU.add)
            phase(P, vector=ev)


def emit_winlat(P):
    nc = P.nc
    WRK_B, SCR_F = P.sb["WRK_B"], P.sb["SCR_F"]
    POOL, POOL_B = P.sb["POOL"], P.sb["POOL_B"]
    WBUF = P.sb["WBUF"]
    IDENTB = P.sb["IDENTB"]
    COLS, FROW = P.sb["COLS"], P.sb["FROW"]
    ONES1B = P.sb["ONES1B"]

    for nm, rows, cols in WL_CAST:
        emit_cast_generic(P, nm, nm + "_b", rows)

    emit_ln(P, want_bf16=True)

    def ldc(e):
        e.dma_start(out=SCR_F[:, 0:1024], in_=P.inp["wl_wmA"][:, :]).then_inc(P.s_tick, 16)
        e.dma_start(out=SCR_F[:, 1024:2048], in_=P.inp["wl_wmT"][:, :]).then_inc(P.s_tick, 16)
        e.dma_start(out=SCR_F[0:1, 2048:3072], in_=P.inp["wl_bk0"][:, :]).then_inc(P.s_tick, 16)
        e.dma_start(out=SCR_F[0:1, 3072:4096], in_=P.inp["wl_lm"][:, :]).then_inc(P.s_tick, 16)
        e.dma_start(out=SCR_F[0:64, 4096:4128], in_=P.inp["wl_gvec"][:, :]).then_inc(P.s_tick, 16)
        e.dma_start(out=SCR_F[:, 4128:4130], in_=P.inp["wl_bk0c"][:, :]).then_inc(P.s_tick, 16)
        e.dma_start(out=P.state["halosrc_i"][0:1, 2:4], in_=P.inp["wl_hsrc"][:, :]
                    ).then_inc(P.s_tick, 16)
        return 7
    dphase(P, "sync", ldc)

    def cst(v):
        v.tensor_copy(POOL_B[:, W_WMA:W_WMA + 1024], SCR_F[:, 0:1024])
        v.tensor_copy(POOL_B[:, W_WMT:W_WMT + 1024], SCR_F[:, 1024:2048])
        v.tensor_copy(POOL_B[0:1, W_BK0A:W_BK0A + 1024], SCR_F[0:1, 2048:3072])
        v.tensor_copy(POOL_B[0:1, W_LM:W_LM + 1024], SCR_F[0:1, 3072:4096])
        v.tensor_copy(POOL_B[0:64, W_GVEC:W_GVEC + 32], SCR_F[0:64, 4096:4128])
        v.tensor_copy(COLS[:, 62:64], SCR_F[:, 4128:4130])
        v.memset(POOL_B[0:64, W_OC64:W_OC64 + 1], 1.0)
        v.memset(POOL_B[:, W_OC128:W_OC128 + 1], 1.0)
    phase(P, vector=cst)

    def pub(e):
        for dt in range(NDT):
            e.dma_start(out=P.dram["wl_ccin"][dt * 128:(dt + 1) * 128, 0:256],
                        in_=WRK_B[:, dt * TPC + 768:dt * TPC + 1024]).then_inc(P.s_tick, 16)
            e.dma_start(out=P.dram["wl_ccin"][dt * 128:(dt + 1) * 128, 256:512],
                        in_=WRK_B[:, dt * TPC + 1792:dt * TPC + 2048]).then_inc(P.s_tick, 16)
        return 2 * NDT
    dphase(P, "sync", pub)

    def cc(g):
        g.collective_compute("AllGather", ALU.bypass,
                             replica_groups=[[0, 1, 2, 3], [4, 5, 6, 7]],
                             ins=[P.dram["wl_ccin"][:, :].opt()],
                             outs=[P.dram["wl_ccout"][:, :].opt()])
    phase(P, gpsimd=cc)

    def rdh(g):
        with g.register("wlrA") as rA, g.register("wlrB") as rB:
            g.reg_load(rA, P.state["halosrc_i"][0:1, 2:3])
            g.reg_load(rB, P.state["halosrc_i"][0:1, 3:4])
            g.dma_start(out=POOL_B[:, W_HALA:W_HALA + 2048].rearrange("p (a b) -> p a b", a=NDT),
                        in_=bass.AP(P.state["wl_ccout_h"], rA,
                                    [[512, 128], [65536, NDT], [1, 256]])
                        ).then_inc(P.s_swdma, 16)
            g.dma_start(out=POOL_B[:, W_HALB:W_HALB + 2048].rearrange("p (a b) -> p a b", a=NDT),
                        in_=bass.AP(P.state["wl_ccout_h"], rB,
                                    [[512, 128], [65536, NDT], [1, 256]])
                        ).then_inc(P.s_swdma, 16)
            P.swdma_n += 32
            g.wait_ge(P.s_swdma, P.swdma_n)
    phase(P, gpsimd=rdh)

    def pools(v):
        for dt in range(NDT):
            for gg in range(16):
                v.tensor_reduce(COLS[:, 16 + gg:17 + gg],
                                WRK_B[:, dt * TPC + gg * 128:dt * TPC + (gg + 1) * 128],
                                axis=AX.X, op=ALU.add)
            selfsync(P, v, "v")
            v.tensor_scalar(COLS[:, 32:48], COLS[:, 16:32], 1.0 / 128.0, None, ALU.mult)
            selfsync(P, v, "v")
            v.tensor_copy(POOL_B[:, W_LATB + dt * 16:W_LATB + (dt + 1) * 16], COLS[:, 32:48])
            selfsync(P, v, "v")
    phase(P, vector=pools)

    def pubp(e):
        for dt in range(NDT):
            e.dma_start(out=P.dram["wl_pcin"][dt * 128:(dt + 1) * 128, :],
                        in_=POOL_B[:, W_LATB + dt * 16:W_LATB + (dt + 1) * 16]
                        ).then_inc(P.s_tick, 16)
        return NDT
    dphase(P, "sync", pubp)

    def cc2(g):
        g.collective_compute("AllGather", ALU.bypass,
                             replica_groups=[[0, 1, 2, 3], [4, 5, 6, 7]],
                             ins=[P.dram["wl_pcin"][:, :].opt()],
                             outs=[P.dram["wl_pcout"][:, :].opt()])
    phase(P, gpsimd=cc2)

    def ldp(e):
        for c in range(8):
            s = c if c < 4 else 7 - c
            coff = 0 if c < 4 else 8
            for dt in range(NDT):
                e.dma_start(out=POOL_B[:, W_LATB + dt * 64 + c * 8:W_LATB + dt * 64 + c * 8 + 8],
                            in_=P.dram["wl_pcout"][s * 1024 + dt * 128:s * 1024 + (dt + 1) * 128,
                                                   coff:coff + 8]).then_inc(P.s_tick, 16)
        return 8 * NDT
    dphase(P, "sync", ldp)

    emit_winlat_latent(P)
    for h in range(16):
        emit_winlat_head(P, h)
    emit_wo_generic(P, "wl_zd", "wl_wout_b")


def emit_winlat_latent(P):
    nc = P.nc
    WRK_B = P.sb["WRK_B"]
    POOL, POOL_B = P.sb["POOL"], P.sb["POOL_B"]
    WBUF, IDENTB = P.sb["WBUF"], P.sb["IDENTB"]
    COLS, FROW = P.sb["COLS"], P.sb["FROW"]
    ONES1B = P.sb["ONES1B"]
    ROUTS = POOL_B[0:64, W_KFM:W_KFM + 1024]
    LYF = POOL[0:64, 20224:20480]
    REMF0 = 0

    for tc in range(4):
        for lh in range(8):
            def wl(e, lh=lh):
                for src, cb in (("wl_wgq_b", 0), ("wl_wgk_b", 512), ("wl_wgv_b", 1024)):
                    for kt in range(NDT):
                        e.dma_start(out=WBUF[:, cb + kt * 64:cb + (kt + 1) * 64],
                                    in_=P.dram[src][kt * 128:(kt + 1) * 128,
                                                    lh * 64:(lh + 1) * 64]
                                    ).then_inc(P.s_tick, 16)
                e.dma_start(out=ROUTS,
                            in_=P.dram["wl_rout_b"][lh * 64:(lh + 1) * 64, :]
                            ).then_inc(P.s_tick, 16)
                return 3 * NDT + 1
            dphase(P, "sync", wl)

            def mmg(t, tc=tc):
                for kt in range(NDT):
                    t.matmul(P.bank(0)[0:64, 0:512], WBUF[:, kt * 64:(kt + 1) * 64],
                             WRK_B[:, kt * TPC + tc * 512:kt * TPC + tc * 512 + 512],
                             start=(kt == 0), stop=(kt == NDT - 1))
                for kt in range(NDT):
                    t.matmul(P.bank(1)[0:64, 0:64], WBUF[:, 512 + kt * 64:512 + (kt + 1) * 64],
                             POOL_B[:, W_LATB + kt * 64:W_LATB + (kt + 1) * 64],
                             start=(kt == 0), stop=(kt == NDT - 1))
                for kt in range(NDT):
                    t.matmul(P.bank(1)[0:64, 64:128], WBUF[:, 1024 + kt * 64:1024 + (kt + 1) * 64],
                             POOL_B[:, W_LATB + kt * 64:W_LATB + (kt + 1) * 64],
                             start=(kt == 0), stop=(kt == NDT - 1))
            phase(P, tensor=mmg)

            def evg(sc):
                sc.activation(POOL_B[0:64, W_GQB:W_GQB + 512], P.bank(0)[0:64, 0:512],
                              AF.Identity, scale=0.125)
                sc.copy(POOL_B[0:64, W_GKF:W_GKF + 64], P.bank(1)[0:64, 0:64])
                sc.copy(POOL[0:64, 20224:20288], P.bank(1)[0:64, 64:128])
            phase(P, scalar=evg)

            def trg(t):
                t.transpose(P.bank(2)[0:64, 0:64], POOL[0:64, 20224:20288],
                            P.sb["IDENT"][0:64, 0:64])
            phase(P, tensor=trg)

            def evg2(sc):
                sc.copy(POOL_B[0:64, W_GVT:W_GVT + 64], P.bank(2)[0:64, 0:64])
            phase(P, scalar=evg2)

            def mmA(t, tc=tc):
                for qt in range(4):
                    t.matmul(P.bank(3)[:, qt * 64:(qt + 1) * 64],
                             POOL_B[0:64, W_GQB + qt * 128:W_GQB + (qt + 1) * 128],
                             POOL_B[0:64, W_GKF:W_GKF + 64], start=True, stop=False)
                    t.matmul(P.bank(3)[:, qt * 64:(qt + 1) * 64], ONES1B[0:1, 0:128],
                             POOL_B[0:1, W_LM + (tc * 4 + qt) * 64:W_LM + (tc * 4 + qt) * 64 + 64],
                             start=False, stop=True)
            phase(P, tensor=mmA)

            def red(v):
                for qt in range(4):
                    v.tensor_reduce(COLS[:, qt:qt + 1], P.bank(3)[:, qt * 64:(qt + 1) * 64],
                                    axis=AX.X, op=ALU.max)
            phase(P, vector=red)

            def trm(t):
                for qt in range(4):
                    t.transpose(P.bank(4)[0:1, qt * 128:(qt + 1) * 128], COLS[:, qt:qt + 1],
                                P.sb["IDENT"][:, :])
            phase(P, tensor=trm)

            def negm(sc):
                sc.activation(POOL_B[0:1, W_NLB:W_NLB + 512], P.bank(4)[0:1, 0:512],
                              AF.Identity, scale=-1.0)
            phase(P, scalar=negm)

            def mmB(t, tc=tc):
                t.matmul(P.bank(5)[0:64, 0:512], POOL_B[0:64, W_GKF:W_GKF + 64],
                         POOL_B[0:64, W_GQB:W_GQB + 512], start=True, stop=False)
                for qt in range(4):
                    t.matmul(P.bank(5)[0:64, qt * 128:(qt + 1) * 128],
                             POOL_B[0:1, W_LM + (tc * 4 + qt) * 64:W_LM + (tc * 4 + qt) * 64 + 64],
                             ONES1B[0:1, 0:128], start=False, stop=False)
                t.matmul(P.bank(5)[0:64, 0:512], ONES1B[0:1, 0:64],
                         POOL_B[0:1, W_NLB:W_NLB + 512], start=False, stop=True)
            phase(P, tensor=mmB)

            def expB(sc):
                sc.activation(POOL_B[0:64, W_PT:W_PT + 512], P.bank(5)[0:64, 0:512], AF.Exp)
            phase(P, scalar=expB)

            def pv(t):
                t.matmul(P.bank(6)[0:64, 0:512], POOL_B[0:64, W_GVT:W_GVT + 64],
                         POOL_B[0:64, W_PT:W_PT + 512], start=True, stop=True)
                t.matmul(P.bank(7)[0:1, 0:512], POOL_B[0:64, W_OC64:W_OC64 + 1],
                         POOL_B[0:64, W_PT:W_PT + 512], start=True, stop=True)
            phase(P, tensor=pv)

            def dn(v):
                v.reciprocal(FROW[0:1, 0:512], P.bank(7)[0:1, 0:512])
                selfsync(P, v, "v")
                v.tensor_copy(POOL_B[0:1, W_LRB:W_LRB + 512], FROW[0:1, 0:512])
            phase(P, vector=dn)

            def bcr(t):
                t.matmul(P.bank(4)[0:64, 0:512], ONES1B[0:1, 0:64],
                         POOL_B[0:1, W_LRB:W_LRB + 512], start=True, stop=True)
            phase(P, tensor=bcr)

            def evy(sc):
                sc.copy(POOL[0:64, 20224:20736], P.bank(6)[0:64, 0:512])
            phase(P, scalar=evy)

            def band(v):
                v.tensor_tensor(POOL_B[0:64, W_BAND:W_BAND + 512], POOL[0:64, 20224:20736],
                                P.bank(4)[0:64, 0:512], ALU.mult)
            phase(P, vector=band)

            def mmr(t):
                for ot in range(8):
                    t.matmul(P.bank(ot)[:, 0:512], ROUTS[0:64, ot * 128:(ot + 1) * 128],
                             POOL_B[0:64, W_BAND:W_BAND + 512], start=True, stop=True)
            phase(P, tensor=mmr)

            if lh == 0:
                def acc0(v):
                    for ot in range(8):
                        v.tensor_copy(POOL[:, ot * 512:(ot + 1) * 512], P.bank(ot)[:, 0:512])
                phase(P, vector=acc0)
            else:
                def acc(v):
                    for ot in range(8):
                        v.tensor_tensor(POOL[:, ot * 512:(ot + 1) * 512],
                                        POOL[:, ot * 512:(ot + 1) * 512],
                                        P.bank(ot)[:, 0:512], ALU.add)
                phase(P, vector=acc)

        for ot in range(NDT):
            def evr(sc, ot=ot):
                sc.copy(POOL_B[:, W_PT:W_PT + 512], POOL[:, ot * 512:(ot + 1) * 512])
            phase(P, scalar=evr)

            def str_(e, ot=ot, tc=tc):
                e.dma_start(out=P.dram["wl_remd"][ot * 128:(ot + 1) * 128,
                                                  tc * 512:(tc + 1) * 512],
                            in_=POOL_B[:, W_PT:W_PT + 512]).then_inc(P.s_tick, 16)
                return 1
            dphase(P, "sync", str_)


def emit_winlat_head(P, h):
    nc = P.nc
    WRK_B = P.sb["WRK_B"]
    POOL, POOL_B = P.sb["POOL"], P.sb["POOL_B"]
    WBUF, IDENTB = P.sb["WBUF"], P.sb["IDENTB"]
    COLS, FROW = P.sb["COLS"], P.sb["FROW"]
    ONES1B = P.sb["ONES1B"]
    YSWF = POOL[0:64, 20224:20480]
    ZT2F = POOL[0:64, 20480:20736]
    RMF = POOL[0:64, 0:1024]

    def wl(e, h=h):
        for src, cb in (("wl_wq_b", 0), ("wl_wk_b", 512), ("wl_wv_b", 1024)):
            for kt in range(NDT):
                e.dma_start(out=WBUF[:, cb + kt * 64:cb + (kt + 1) * 64],
                            in_=P.dram[src][kt * 128:(kt + 1) * 128, h * 64:(h + 1) * 64]
                            ).then_inc(P.s_tick, 16)
        e.dma_start(out=POOL_B[0:64, W_REMH:W_REMH + 2048],
                    in_=P.dram["wl_remd"][h * 64:(h + 1) * 64, :]).then_inc(P.s_tick, 16)
        return 3 * NDT + 1
    dphase(P, "sync", wl)

    for ch in range(2):
        hal = W_HALA if ch == 0 else W_HALB
        c0 = ch * 1024
        for half in range(2):
            def mmq(t, half=half):
                for kt in range(NDT):
                    t.matmul(P.bank(half)[0:64, 0:512], WBUF[:, kt * 64:(kt + 1) * 64],
                             WRK_B[:, kt * TPC + c0 + half * 512:kt * TPC + c0 + half * 512 + 512],
                             start=(kt == 0), stop=(kt == NDT - 1))
            phase(P, tensor=mmq)

        def evq(sc):
            for half in range(2):
                sc.activation(POOL_B[0:64, W_QFM + half * 512:W_QFM + half * 512 + 512],
                              P.bank(half)[0:64, 0:512], AF.Identity, scale=0.125)
        phase(P, scalar=evq)

        def mmgl(t):
            for half in range(2):
                t.matmul(P.bank(2 + half)[0:1, 0:512],
                         POOL_B[0:64, W_GVEC + h:W_GVEC + h + 1],
                         POOL_B[0:64, W_QFM + half * 512:W_QFM + half * 512 + 512],
                         start=True, stop=True)
                t.matmul(P.bank(4 + half)[0:1, 0:512],
                         POOL_B[0:64, W_GVEC + 16 + h:W_GVEC + 17 + h],
                         POOL_B[0:64, W_QFM + half * 512:W_QFM + half * 512 + 512],
                         start=True, stop=True)
        phase(P, tensor=mmgl)

        def evgl(sc, ch=ch):
            for half in range(2):
                sc.activation(POOL_B[0:1, W_GL + ch * 1024 + half * 512:
                                     W_GL + ch * 1024 + half * 512 + 512],
                              P.bank(2 + half)[0:1, 0:512], AF.Sigmoid, scale=8.0)
                sc.activation(POOL_B[0:1, W_GR + ch * 1024 + half * 512:
                                     W_GR + ch * 1024 + half * 512 + 512],
                              P.bank(4 + half)[0:1, 0:512], AF.Sigmoid, scale=8.0)
        phase(P, scalar=evgl)

        VF32 = POOL[0:64, 0:1280]
        for dst, wb, isf in ((W_KFM, 512, False), (0, 1024, True)):
            def mmh(t, wb=wb, hal=hal):
                for kt in range(NDT):
                    t.matmul(P.bank(2)[0:64, 0:256], WBUF[:, wb + kt * 64:wb + (kt + 1) * 64],
                             POOL_B[:, hal + kt * 256:hal + (kt + 1) * 256],
                             start=(kt == 0), stop=(kt == NDT - 1))
            phase(P, tensor=mmh)

            for half in range(2):
                def mmo(t, wb=wb, half=half):
                    for kt in range(NDT):
                        t.matmul(P.bank(3 + half)[0:64, 0:512],
                                 WBUF[:, wb + kt * 64:wb + (kt + 1) * 64],
                                 WRK_B[:, kt * TPC + c0 + half * 512:
                                        kt * TPC + c0 + half * 512 + 512],
                                 start=(kt == 0), stop=(kt == NDT - 1))
                phase(P, tensor=mmo)

            if not isf:
                def evkv(sc, dst=dst):
                    sc.copy(POOL_B[0:64, dst:dst + 256], P.bank(2)[0:64, 0:256])
                    sc.copy(POOL_B[0:64, dst + 256:dst + 768], P.bank(3)[0:64, 0:512])
                    sc.copy(POOL_B[0:64, dst + 768:dst + 1280], P.bank(4)[0:64, 0:512])
                phase(P, scalar=evkv)
            else:
                def evkv(sc):
                    sc.copy(VF32[:, 0:256], P.bank(2)[0:64, 0:256])
                    sc.copy(VF32[:, 256:768], P.bank(3)[0:64, 0:512])
                    sc.copy(VF32[:, 768:1280], P.bank(4)[0:64, 0:512])
                phase(P, scalar=evkv)

        for wp in range(3):
            def trv(t, wp=wp):
                for j in range(4):
                    wt = wp * 4 + j
                    if wt >= 10:
                        break
                    t.transpose(P.bank(2 + j)[:, 0:64],
                                VF32[:, wt * 128:(wt + 1) * 128],
                                P.sb["IDENT"][0:64, 0:64])
            phase(P, tensor=trv)

            def evv(sc, wp=wp):
                for j in range(4):
                    wt = wp * 4 + j
                    if wt >= 10:
                        break
                    sc.copy(POOL_B[:, W_VT + wt * 64:W_VT + (wt + 1) * 64],
                            P.bank(2 + j)[:, 0:64])
            phase(P, scalar=evv)

        def remf(v, ch=ch):
            v.tensor_copy(RMF[:, 0:512],
                          POOL_B[0:64, W_REMH + ch * 1024:W_REMH + ch * 1024 + 512])
            selfsync(P, v, "v")
            v.tensor_copy(RMF[:, 512:1024],
                          POOL_B[0:64, W_REMH + ch * 1024 + 512:W_REMH + ch * 1024 + 1024])
        phase(P, vector=remf)

        for blk in range(4):
            emit_winlat_blk(P, h, ch, blk)

        def stz(e, h=h, ch=ch):
            e.dma_start(out=P.dram["wl_zd"][h * 64:(h + 1) * 64, ch * 1024:(ch + 1) * 1024],
                        in_=POOL_B[0:64, W_ZST:W_ZST + 1024]).then_inc(P.s_tick, 16)
            return 1
        dphase(P, "sync", stz)


def emit_winlat_blk(P, h, ch, blk):
    POOL, POOL_B = P.sb["POOL"], P.sb["POOL_B"]
    IDENTB = P.sb["IDENTB"]
    COLS, FROW = P.sb["COLS"], P.sb["FROW"]
    ONES1B = P.sb["ONES1B"]
    YSWF = POOL[0:64, 20224:20480]
    ZT2F = POOL[0:64, 20480:20736]
    b0 = blk * 256

    def mmA(t, ch=ch, blk=blk):
        for qt in range(2):
            t.matmul(P.bank(qt)[:, 0:512],
                     POOL_B[0:64, W_QFM + b0 + qt * 128:W_QFM + b0 + qt * 128 + 128],
                     POOL_B[0:64, W_KFM + b0:W_KFM + b0 + 512], start=True,
                     stop=not (blk == 0))
            if blk == 0:
                t.matmul(P.bank(qt)[:, 0:512], ONES1B[0:1, :],
                         POOL_B[0:1, W_BK0A + ch * 512:W_BK0A + ch * 512 + 512],
                         start=False, stop=True)
    phase(P, tensor=mmA)

    def mskA(v):
        for qt in range(2):
            v.tensor_tensor(P.bank(qt)[:, 0:512], P.bank(qt)[:, 0:512],
                            POOL_B[:, W_WMA + qt * 512:W_WMA + (qt + 1) * 512], ALU.add)
    phase(P, vector=mskA)

    def red(v):
        for qt in range(2):
            v.tensor_reduce(COLS[:, qt:qt + 1], P.bank(qt)[:, 0:512], axis=AX.X, op=ALU.max)
    phase(P, vector=red)

    def trm(t):
        for qt in range(2):
            t.transpose(P.bank(2)[0:1, qt * 128:(qt + 1) * 128], COLS[:, qt:qt + 1],
                        P.sb["IDENT"][:, :])
    phase(P, tensor=trm)

    def negm(sc):
        sc.activation(POOL_B[0:1, W_NMB:W_NMB + 256], P.bank(2)[0:1, 0:256],
                      AF.Identity, scale=-1.0)
    phase(P, scalar=negm)

    for kt in range(4):
        def mmB(t, kt=kt, ch=ch, blk=blk):
            bk = P.bank(3 + kt % 2)
            t.matmul(bk[:, 0:256],
                     POOL_B[0:64, W_KFM + b0 + kt * 128:W_KFM + b0 + kt * 128 + 128],
                     POOL_B[0:64, W_QFM + b0:W_QFM + b0 + 256], start=True, stop=False)
            t.matmul(bk[:, 0:256], ONES1B[0:1, :], POOL_B[0:1, W_NMB:W_NMB + 256],
                     start=False, stop=True)
        phase(P, tensor=mmB)

        def mskB(v, kt=kt, ch=ch, blk=blk):
            bk = P.bank(3 + kt % 2)
            v.tensor_tensor(bk[:, 0:256], bk[:, 0:256],
                            POOL_B[:, W_WMT + kt * 256:W_WMT + (kt + 1) * 256], ALU.add)
            if blk == 0 and kt < 2:
                selfsync(P, v, "v")
                v.tensor_scalar(bk[:, 0:256], bk[:, 0:256],
                                COLS[:, 62 + ch:63 + ch], None, ALU.add)
        phase(P, vector=mskB)

        def expB(sc, kt=kt):
            sc.activation(POOL_B[:, W_PT + (kt % 2) * 256:W_PT + (kt % 2) * 256 + 256],
                          P.bank(3 + kt % 2)[:, 0:256], AF.Exp)
        phase(P, scalar=expB)

        def pv(t, kt=kt, blk=blk):
            t.matmul(P.bank(5)[0:64, 0:256],
                     POOL_B[:, W_VT + (2 * blk + kt) * 64:W_VT + (2 * blk + kt) * 64 + 64],
                     POOL_B[:, W_PT + (kt % 2) * 256:W_PT + (kt % 2) * 256 + 256],
                     start=(kt == 0), stop=(kt == 3))
            t.matmul(P.bank(6)[0:1, 0:256], POOL_B[:, W_OC128:W_OC128 + 1],
                     POOL_B[:, W_PT + (kt % 2) * 256:W_PT + (kt % 2) * 256 + 256],
                     start=(kt == 0), stop=(kt == 3))
        phase(P, tensor=pv)

    def dn(v, h=h, ch=ch, blk=blk):
        v.reciprocal(FROW[0:1, 0:256], P.bank(6)[0:1, 0:256])
        selfsync(P, v, "v")
        v.tensor_copy(POOL_B[0:1, W_RCB:W_RCB + 256], FROW[0:1, 0:256])
        selfsync(P, v, "v")
        v.tensor_tensor(POOL_B[0:1, W_LGB:W_LGB + 256], POOL_B[0:1, W_RCB:W_RCB + 256],
                        POOL_B[0:1, W_GL + ch * 1024 + b0:W_GL + ch * 1024 + b0 + 256],
                        ALU.mult)
    phase(P, vector=dn)

    def bcr(t, ch=ch):
        t.matmul(P.bank(7)[0:64, 0:256], ONES1B[0:1, 0:64],
                 POOL_B[0:1, W_LGB:W_LGB + 256], start=True, stop=True)
        t.matmul(P.bank(2)[0:64, 0:256], ONES1B[0:1, 0:64],
                 POOL_B[0:1, W_GR + ch * 1024 + b0:W_GR + ch * 1024 + b0 + 256],
                 start=True, stop=True)
    phase(P, tensor=bcr)

    def evy(sc):
        sc.copy(YSWF, P.bank(5)[0:64, 0:256])
    phase(P, scalar=evy)

    def zf(v, blk=blk):
        v.tensor_tensor(ZT2F, YSWF, P.bank(7)[0:64, 0:256], ALU.mult)
        selfsync(P, v, "v")
        v.tensor_tensor(YSWF, POOL[0:64, blk * 256:blk * 256 + 256],
                        P.bank(2)[0:64, 0:256], ALU.mult)
        selfsync(P, v, "v")
        v.tensor_tensor(POOL_B[0:64, W_ZST + b0:W_ZST + b0 + 256], ZT2F, YSWF, ALU.add)
    phase(P, vector=zf)


EMITTERS["winlat"] = emit_winlat



DN_CAST = [("dn_wq", 1024, 1024), ("dn_wk", 1024, 1024), ("dn_wv", 1024, 1024),
           ("dn_wg", 1024, 1024), ("dn_wo", 1024, 1024),
           ("dn_wgk", 1024, 16), ("dn_wb", 1024, 16)]
D_QH, D_KH, D_VH, D_GS = 0, 2048, 4096, 6144
D_SINS = 8192
D_RB, D_R1M, D_R2M, D_R1A, D_RBE = 12288, 16384, 20480, 24576, 28672
D_KTM, D_VTM, D_K2TM, D_T1M = 32768, 32896, 33024, 33152
D_DST, D_DIT, D_ATB, D_AQT = 33280, 33408, 33536, 33664
D_XB, D_XTB, D_ZB, D_TWOI = 33792, 33920, 34048, 34176
D_S2B, D_RHSB, D_WB, D_T1B = 34304, 34368, 34432, 34496
D_SSF, D_SSB, D_PPB = 34560, 34688, 34752
D_PTB, D_MTS = 34816, 34880
D_GCC, D_CE1, D_CGQ, D_CBC, D_CBE = 35072, 35074, 35076, 35078, 35080
D_EPC, D_ONEC = 35082, 35084
D_KF32 = 35136
D_OTM, D_O0F = 35648, 35904
D_ONF, D_QSF = 36160, 36416
D_FPN = 36672
D_NF = 37696
D_SELA, D_SELB = 38720, 38736
D_PS, D_PSB = 38752, 38880
D_CC, D_CCB = 38944, 39072
D_ZD = 39136
D_XTL, D_MTL, D_O0B = 41184, 41312, 41376
D_GSF, D_DOF = 41440, 41696


def emit_deltanet(P):
    nc = P.nc
    POOL, POOL_B = P.sb["POOL"], P.sb["POOL_B"]
    WRK_B, SCR_F = P.sb["WRK_B"], P.sb["SCR_F"]
    WBUF = P.sb["WBUF"]
    COLS, FROW, MROW = P.sb["COLS"], P.sb["FROW"], P.sb["MROW"]
    ONES1B, ONESROW_F = P.sb["ONES1B"], P.sb["ONESROW_F"]

    for nm, rows, cols in DN_CAST:
        emit_cast_generic(P, nm, nm + "_b", rows)

    emit_ln(P, want_bf16=True)

    ROPC = POOL[:, 16384:18432]
    ROPS = POOL[:, 18432:20480]
    CONVC = POOL[:, 23832:23928]
    HALXA = POOL_B[:, 47616:47640]
    HALXB = POOL_B[:, 47640:47664]
    VECC = POOL[0:16, 23928:23932]
    HMC = POOL[:, 23932:23934]
    GNWC = POOL[0:64, 23934:23935]

    def ldc(e):
        e.dma_start(out=ROPC, in_=P.inp["dn_ropeC"][:, :]).then_inc(P.s_tick, 16)
        e.dma_start(out=ROPS, in_=P.inp["dn_ropeS"][:, :]).then_inc(P.s_tick, 16)
        e.dma_start(out=CONVC, in_=P.inp["dn_convs"][:, :]).then_inc(P.s_tick, 16)
        e.dma_start(out=VECC, in_=P.inp["dn_vec"][:, :]).then_inc(P.s_tick, 16)
        e.dma_start(out=HMC, in_=P.inp["dn_hmask"][:, :]).then_inc(P.s_tick, 16)
        e.dma_start(out=GNWC, in_=P.inp["dn_gnw"][:, :]).then_inc(P.s_tick, 16)
        e.dma_start(out=SCR_F[:, 0:128], in_=P.inp["dn_mski"][:, :]).then_inc(P.s_tick, 16)
        e.dma_start(out=SCR_F[:, 128:256], in_=P.inp["dn_msks"][:, :]).then_inc(P.s_tick, 16)
        e.dma_start(out=SCR_F[0:64, 256:264], in_=P.inp["dn_selA"][:, :]).then_inc(P.s_tick, 16)
        e.dma_start(out=SCR_F[0:64, 264:272], in_=P.inp["dn_selB"][:, :]).then_inc(P.s_tick, 16)
        e.dma_start(out=P.state["halosrc_i"][0:1, 0:2], in_=P.inp["dn_hsrc"][:, :]
                    ).then_inc(P.s_tick, 16)
        return 11
    dphase(P, "sync", ldc)

    MSKI_F = POOL[:, 23960:24088]
    MSKS_F = POOL[:, 24088:24216]
    SELA_F = POOL[0:64, 24216:24224]
    SELB_F = POOL[0:64, 24224:24232]

    def cst(v):
        v.tensor_copy(MSKI_F, SCR_F[:, 0:128])
        v.tensor_copy(MSKS_F, SCR_F[:, 128:256])
        v.tensor_copy(SELA_F, SCR_F[0:64, 256:264])
        v.tensor_copy(SELB_F, SCR_F[0:64, 264:272])
        v.memset(MROW[0:1, 1920:2048], -1.0)
    phase(P, vector=cst)

    def pub(e):
        for dt in range(NDT):
            e.dma_start(out=P.dram["dn_hci"][dt * 128:(dt + 1) * 128, 0:3],
                        in_=WRK_B[:, dt * TPC + 1021:dt * TPC + 1024]).then_inc(P.s_tick, 16)
            e.dma_start(out=P.dram["dn_hci"][dt * 128:(dt + 1) * 128, 3:6],
                        in_=WRK_B[:, dt * TPC + 2045:dt * TPC + 2048]).then_inc(P.s_tick, 16)
        return 2 * NDT
    dphase(P, "sync", pub)

    def cc(g):
        g.collective_compute("AllGather", ALU.bypass,
                             replica_groups=[[0, 1, 2, 3], [4, 5, 6, 7]],
                             ins=[P.dram["dn_hci"][:, :].opt()],
                             outs=[P.dram["dn_hco"][:, :].opt()])
    phase(P, gpsimd=cc)

    def rdh(g):
        with g.register("dnrA") as rA, g.register("dnrB") as rB:
            g.reg_load(rA, P.state["halosrc_i"][0:1, 0:1])
            g.reg_load(rB, P.state["halosrc_i"][0:1, 1:2])
            g.dma_start(out=HALXA.rearrange("p (a b) -> p a b", a=NDT),
                        in_=bass.AP(P.state["dn_hco_h"], rA,
                                    [[6, 128], [768, NDT], [1, 3]])).then_inc(P.s_swdma, 16)
            g.dma_start(out=HALXB.rearrange("p (a b) -> p a b", a=NDT),
                        in_=bass.AP(P.state["dn_hco_h"], rB,
                                    [[6, 128], [768, NDT], [1, 3]])).then_inc(P.s_swdma, 16)
            P.swdma_n += 32
            g.wait_ge(P.s_swdma, P.swdma_n)
    phase(P, gpsimd=rdh)

    def mskh(v):
        v.tensor_scalar(HALXA, HALXA, HMC[:, 0:1], None, ALU.mult)
        selfsync(P, v, "v")
        v.tensor_scalar(HALXB, HALXB, HMC[:, 1:2], None, ALU.mult)
    phase(P, vector=mskh)

    emit_dn_proj(P)
    emit_dn_rows(P)
    for h in range(16):
        emit_dn_pass1(P, h)
    emit_dn_fold(P)
    for h in range(16):
        emit_dn_pass2(P, h)
    emit_wo_generic(P, "dn_zd", "dn_wo_b")


def emit_dn_proj(P):
    nc = P.nc
    POOL, POOL_B = P.sb["POOL"], P.sb["POOL_B"]
    WRK_B = P.sb["WRK_B"]
    WBUF = P.sb["WBUF"]
    COLS = P.sb["COLS"]
    ROPC = POOL[:, 16384:18432]
    ROPS = POOL[:, 18432:20480]
    CONVC = POOL[:, 23832:23928]
    UPAD = POOL[:, 20480:21508]
    SIL = POOL[:, 4096:5120]
    SH = POOL[:, 5120:6144]
    HALX = {0: POOL_B[:, 47616:47640], 1: POOL_B[:, 47640:47664]}
    perm = []
    for p in range(128):
        d = p % 64
        if d < 16:
            perm.append(p + 16)
        elif d < 32:
            perm.append(p - 16)
        else:
            perm.append(p)

    def lnepsc(v):
        v.memset(COLS[:, 61:62], 1e-6)
    phase(P, vector=lnepsc)

    for pi, (proj, conv_cb) in enumerate((("q", 0), ("k", 32), ("v", 64), ("g", None))):
        wname = "dn_w" + proj + "_b"
        for dt in range(NDT):
            def wl(e, wname=wname, dt=dt):
                for kt in range(NDT):
                    e.dma_start(out=WBUF[:, kt * 128:(kt + 1) * 128],
                                in_=P.dram[wname][kt * 128:(kt + 1) * 128,
                                                  dt * 128:(dt + 1) * 128]
                                ).then_inc(P.s_tick, 16)
                return NDT
            dphase(P, "sync", wl)

            for ch in range(2):
                c0 = ch * 1024

                def mm(t, ch=ch, c0=c0, conv=conv_cb is not None):
                    for half in range(2):
                        for kt in range(NDT):
                            t.matmul(P.bank(half)[:, 0:512], WBUF[:, kt * 128:(kt + 1) * 128],
                                     WRK_B[:, kt * TPC + c0 + half * 512:
                                            kt * TPC + c0 + half * 512 + 512],
                                     start=(kt == 0), stop=(kt == NDT - 1))
                    if conv:
                        for kt in range(NDT):
                            t.matmul(P.bank(2)[:, 0:3], WBUF[:, kt * 128:(kt + 1) * 128],
                                     HALX[ch][:, kt * 3:(kt + 1) * 3],
                                     start=(kt == 0), stop=(kt == NDT - 1))
                phase(P, tensor=mm)

                if conv_cb is None:
                    def evg(sc, dt=dt, ch=ch):
                        sc.copy(POOL_B[:, 43520 + 0:43520 + 512], P.bank(0)[:, 0:512])
                        sc.copy(POOL_B[:, 44032 + 0:44032 + 512], P.bank(1)[:, 0:512])
                    phase(P, scalar=evg)

                    def stg(e, dt=dt, ch=ch):
                        e.dma_start(out=P.dram["dn_g"][dt * 128:(dt + 1) * 128,
                                                       ch * 1024:ch * 1024 + 512],
                                    in_=POOL_B[:, 43520:44032]).then_inc(P.s_tick, 16)
                        e.dma_start(out=P.dram["dn_g"][dt * 128:(dt + 1) * 128,
                                                       ch * 1024 + 512:ch * 1024 + 1024],
                                    in_=POOL_B[:, 44032:44544]).then_inc(P.s_tick, 16)
                        return 2
                    dphase(P, "sync", stg)
                    continue

                def evp(sc):
                    sc.copy(UPAD[:, 0:3], P.bank(2)[:, 0:3])
                    sc.copy(UPAD[:, 3:515], P.bank(0)[:, 0:512])
                    sc.copy(UPAD[:, 515:1027], P.bank(1)[:, 0:512])
                phase(P, scalar=evp)

                def conv(v, cb=conv_cb, dt=dt):
                    v.tensor_scalar(SIL, UPAD[:, 0:1024], CONVC[:, cb + dt * 4:cb + dt * 4 + 1],
                                    None, ALU.mult)
                    for j in range(1, 4):
                        selfsync(P, v, "v")
                        v.scalar_tensor_tensor(SIL, UPAD[:, j:j + 1024],
                                               CONVC[:, cb + dt * 4 + j:cb + dt * 4 + j + 1],
                                               SIL, ALU.mult, ALU.add)
                phase(P, vector=conv)

                def silu(sc):
                    sc.activation(SIL, SIL, AF.Silu)
                phase(P, scalar=silu)

                if proj == "v":
                    def evv(v):
                        v.tensor_copy(POOL_B[:, 43520:44544], SIL)
                    phase(P, vector=evv)

                    def stv(e, dt=dt, ch=ch):
                        e.dma_start(out=P.dram["dn_v"][dt * 128:(dt + 1) * 128,
                                                       ch * 1024:(ch + 1) * 1024],
                                    in_=POOL_B[:, 43520:44544]).then_inc(P.s_tick, 16)
                        return 1
                    dphase(P, "sync", stv)
                    continue

                def sq(v):
                    v.tensor_tensor(POOL_B[:, 44544:45568], SIL, SIL, ALU.mult)
                phase(P, vector=sq)

                def hsum(t):
                    t.matmul(P.bank(3)[0:2, 0:512], P.sb["IND64B"][:, 0:2],
                             POOL_B[:, 44544:45056], start=True, stop=True)
                    t.matmul(P.bank(4)[0:2, 0:512], P.sb["IND64B"][:, 0:2],
                             POOL_B[:, 45056:45568], start=True, stop=True)
                phase(P, tensor=hsum)

                def lnr(sc):
                    sc.activation(POOL[0:2, 0:512], P.bank(3)[0:2, 0:512], AF.Ln,
                                  bias=COLS[0:2, 61:62], scale=1.0)
                    sc.activation(POOL[0:2, 512:1024], P.bank(4)[0:2, 0:512], AF.Ln,
                                  bias=COLS[0:2, 61:62], scale=1.0)
                phase(P, scalar=lnr)

                def lnr2(v):
                    v.tensor_scalar(POOL[0:2, 0:1024], POOL[0:2, 0:1024],
                                    -0.5, None, ALU.mult)
                phase(P, vector=lnr2)

                def stl(e, proj=proj, dt=dt, ch=ch):
                    roff = 0 if proj == "q" else 16
                    e.dma_start(out=P.dram["dn_lnr"][roff + dt * 2:roff + dt * 2 + 2,
                                                     ch * 1024:(ch + 1) * 1024],
                                in_=POOL[0:2, 0:1024]).then_inc(P.s_tick, 16)
                    return 1
                dphase(P, "sync", stl)

                def rope(v, ch=ch):
                    p32 = list(range(16, 32)) + list(range(0, 16))
                    v.memset(SH[32:64, :], 0.0)
                    v.memset(SH[96:128, :], 0.0)
                    v.stream_shuffle(SH[0:32, :], SIL[0:32, :], p32)
                    v.stream_shuffle(SH[64:96, :], SIL[64:96, :], p32)
                    selfsync(P, v, "v")
                    v.tensor_tensor(SIL, SIL, ROPC[:, ch * 1024:(ch + 1) * 1024], ALU.mult)
                    v.tensor_tensor(SH, SH, ROPS[:, ch * 1024:(ch + 1) * 1024], ALU.mult)
                    selfsync(P, v, "v")
                    v.tensor_tensor(POOL_B[:, 43520:44544], SIL, SH, ALU.add)
                phase(P, vector=rope)

                def stq(e, proj=proj, dt=dt, ch=ch):
                    e.dma_start(out=P.dram["dn_" + proj][dt * 128:(dt + 1) * 128,
                                                         ch * 1024:(ch + 1) * 1024],
                                in_=POOL_B[:, 43520:44544]).then_inc(P.s_tick, 16)
                    return 1
                dphase(P, "sync", stq)


def emit_dn_rows(P):
    nc = P.nc
    POOL, POOL_B = P.sb["POOL"], P.sb["POOL_B"]
    WRK_B = P.sb["WRK_B"]
    WBUF = P.sb["WBUF"]
    VECC = POOL[0:16, 23928:23932]
    GKR = POOL[0:16, 0:1024]
    BR = POOL[0:16, 1024:2048]
    BER = POOL[0:16, 2048:3072]
    LNB = POOL[0:16, 3072:4096]
    LNX = POOL[0:16, 4096:5120]
    TMP = POOL[0:16, 5120:6144]
    ONR = POOL[0:16, 6144:7168]
    NEA = POOL[0:16, 23936:23937]
    DBC = POOL[0:16, 23937:23938]
    BBC = POOL[0:16, 23938:23939]
    ONE1 = POOL[0:16, 23939:23940]

    def wl(e):
        for kt in range(NDT):
            e.dma_start(out=WBUF[:, kt * 16:(kt + 1) * 16],
                        in_=P.dram["dn_wgk_b"][kt * 128:(kt + 1) * 128, :]
                        ).then_inc(P.s_tick, 16)
            e.dma_start(out=WBUF[:, 128 + kt * 16:128 + (kt + 1) * 16],
                        in_=P.dram["dn_wb_b"][kt * 128:(kt + 1) * 128, :]
                        ).then_inc(P.s_tick, 16)
        return 2 * NDT
    dphase(P, "sync", wl)

    def prep(sc):
        sc.activation(POOL[0:16, 23940:23941], VECC[:, 0:1], AF.Exp)
    phase(P, scalar=prep)

    def prep2(v):
        v.tensor_scalar(NEA, POOL[0:16, 23940:23941], -1.0, None, ALU.mult)
        v.tensor_copy(DBC, VECC[:, 1:2])
        v.tensor_copy(BBC, VECC[:, 2:3])
        v.memset(ONE1, 1.0)
        v.memset(ONR[:, :], 1.0)
    phase(P, vector=prep2)

    for ch in range(2):
        c0 = ch * 1024
        for cb in range(2):
            def mm(t, c0=c0, cb=cb):
                for kt in range(NDT):
                    t.matmul(P.bank(0)[0:16, 0:512], WBUF[:, kt * 16:(kt + 1) * 16],
                             WRK_B[:, kt * TPC + c0 + cb * 512:kt * TPC + c0 + cb * 512 + 512],
                             start=(kt == 0), stop=(kt == NDT - 1))
                for kt in range(NDT):
                    t.matmul(P.bank(1)[0:16, 0:512], WBUF[:, 128 + kt * 16:128 + (kt + 1) * 16],
                             WRK_B[:, kt * TPC + c0 + cb * 512:kt * TPC + c0 + cb * 512 + 512],
                             start=(kt == 0), stop=(kt == NDT - 1))
            phase(P, tensor=mm)

            def sp1(v, cb=cb):
                v.tensor_scalar(TMP[:, 0:512], P.bank(0)[0:16, 0:512], DBC[:, 0:1],
                                None, ALU.add)
            phase(P, vector=sp1)

            def sp2(sc, cb=cb):
                sc.activation(GKR[:, cb * 512:(cb + 1) * 512], TMP[:, 0:512], AF.Abs)
                sc.activation(BER[:, cb * 512:(cb + 1) * 512], P.bank(1)[0:16, 0:512],
                              AF.Sigmoid, bias=BBC[:, 0:1], scale=1.0)
            phase(P, scalar=sp2)

            def sp3(sc, cb=cb):
                sc.activation(LNX[:, cb * 512:(cb + 1) * 512],
                              GKR[:, cb * 512:(cb + 1) * 512], AF.Exp, scale=-1.0)
            phase(P, scalar=sp3)

            def sp4(sc, cb=cb):
                sc.activation(LNX[:, cb * 512:(cb + 1) * 512],
                              LNX[:, cb * 512:(cb + 1) * 512], AF.Ln,
                              bias=ONE1[:, 0:1], scale=1.0)
            phase(P, scalar=sp4)

            def sp5(v, cb=cb):
                v.tensor_scalar(TMP[:, 512:1024], TMP[:, 0:512], 0.0, None, ALU.max)
                selfsync(P, v, "v")
                v.tensor_tensor(TMP[:, 512:1024], TMP[:, 512:1024],
                                LNX[:, cb * 512:(cb + 1) * 512], ALU.add)
                selfsync(P, v, "v")
                v.tensor_scalar(GKR[:, cb * 512:(cb + 1) * 512], TMP[:, 512:1024],
                                NEA[:, 0:1], None, ALU.mult)
            phase(P, vector=sp5)

        def cums(v):
            for sub in range(8):
                v.tensor_tensor_scan(BR[:, sub * 128:(sub + 1) * 128],
                                     ONR[:, sub * 128:(sub + 1) * 128],
                                     GKR[:, sub * 128:(sub + 1) * 128],
                                     0.0, ALU.mult, ALU.add)
        phase(P, vector=cums)

        def lnb(sc):
            sc.activation(LNB[:, :], BER[:, :], AF.Ln)
        phase(P, scalar=lnb)

        def ldl(e, ch=ch):
            e.dma_start(out=LNX[:, 0:1024],
                        in_=P.dram["dn_lnr"][16:32, ch * 1024:(ch + 1) * 1024]
                        ).then_inc(P.s_tick, 16)
            e.dma_start(out=TMP[:, 0:1024],
                        in_=P.dram["dn_lnr"][0:16, ch * 1024:(ch + 1) * 1024]
                        ).then_inc(P.s_tick, 16)
            return 2
        dphase(P, "sync", ldl)

        def comb(v):
            v.tensor_tensor(GKR[:, :], BR[:, :], LNB[:, :], ALU.add)
            selfsync(P, v, "v")
            v.tensor_tensor(GKR[:, :], GKR[:, :], LNX[:, :], ALU.add)
            selfsync(P, v, "v")
            v.tensor_tensor(LNB[:, :], BR[:, :], LNX[:, :], ALU.subtract)
            selfsync(P, v, "v")
            v.tensor_tensor(TMP[:, :], BR[:, :], TMP[:, :], ALU.add)
        phase(P, vector=comb)

        def str_(e, ch=ch):
            for src, r0 in ((BR, 0), (GKR, 16), (LNB, 32), (TMP, 48), (BER, 64)):
                e.dma_start(out=P.dram["dn_rows"][r0:r0 + 16, ch * 1024:(ch + 1) * 1024],
                            in_=src[:, 0:1024]).then_inc(P.s_tick, 16)
            return 5
        dphase(P, "sync", str_)


def _dn_aps(P):
    POOL, POOL_B = P.sb["POOL"], P.sb["POOL_B"]
    return dict(
        SSF=POOL[0:64, 17280:17344], GCC=POOL[0:64, 17536:17537],
        CE1=POOL[:, 17537:17538], CGQ=POOL[:, 17538:17539],
        CBC=POOL[:, 17539:17540], CBE=POOL[:, 17540:17541],
        CR2=POOL[:, 17541:17542], EPC=POOL[:, 17542:17543],
        KF32=POOL[0:64, 17568:17824],
        OTM=POOL[:, 17824:17888], O0F=POOL[:, 17952:18016],
        ONF=POOL[:, 18080:18144], QSF=POOL[:, 18208:18272],
        NF=POOL[0:64, 18848:19360], PS=POOL[0:64, 19376:19440],
        CC=POOL[0:64, 19472:19536],
        SINS=POOL[0:64, 4096:6144],
        SELA=POOL[0:64, 24216:24224], SELB=POOL[0:64, 24224:24232],
        MSKI=POOL[:, 23960:24088], MSKS=POOL[:, 24088:24216],
        GNWC=POOL[0:64, 23934:23935],
        RB=POOL[0:1, 6144:8192], R1M=POOL[0:1, 8192:10240],
        R2M=POOL[0:1, 10240:12288], R1A=POOL[0:1, 12288:14336],
        RBE=POOL[0:1, 14336:16384],
        GSF=POOL[0:64, 20720:20848], DOF=POOL[:, 20848:20912],
        ZFT=POOL[0:64, 20912:21040],
        KTM=POOL_B[:, D_KTM:D_KTM + 64], VTM=POOL_B[:, D_VTM:D_VTM + 64],
        K2TM=POOL_B[:, D_K2TM:D_K2TM + 64], T1M=POOL_B[:, D_T1M:D_T1M + 64],
        DST=POOL_B[:, D_DST:D_DST + 128], DIT=POOL_B[:, D_DIT:D_DIT + 128],
        ATB=POOL_B[:, D_ATB:D_ATB + 128], AQT=POOL_B[:, D_AQT:D_AQT + 128],
        XB=POOL_B[:, D_XB:D_XB + 128], XTB=POOL_B[:, D_XTB:D_XTB + 128],
        ZB=POOL_B[:, D_ZB:D_ZB + 128], TWOI=POOL_B[:, D_TWOI:D_TWOI + 128],
        S2B=POOL_B[:, D_S2B:D_S2B + 64], RHSB=POOL_B[:, D_RHSB:D_RHSB + 64],
        WB=POOL_B[:, D_WB:D_WB + 64], T1B=POOL_B[:, D_T1B:D_T1B + 64],
        SSB=POOL_B[0:64, D_SSB:D_SSB + 64], PPB=POOL_B[0:64, D_PPB:D_PPB + 64],
        PTB=POOL_B[0:64, D_PTB:D_PTB + 64], MTS=POOL_B[0:64, D_MTS:D_MTS + 64],
        FPN=POOL_B[0:64, D_FPN:D_FPN + 1024], PSB=POOL_B[0:64, D_PSB:D_PSB + 64],
        CCB=POOL_B[0:64, D_CCB:D_CCB + 64], ZD=POOL_B[0:64, D_ZD:D_ZD + 2048],
        XTL=POOL_B[:, D_XTL:D_XTL + 128], MTL=POOL_B[0:64, D_MTL:D_MTL + 64],
        O0B=POOL_B[:, D_O0B:D_O0B + 64], OSTG=POOL_B[:, 42208:42272],
    )


def emit_dn_pass1(P, h):
    nc = P.nc
    POOL, POOL_B = P.sb["POOL"], P.sb["POOL_B"]
    A = _dn_aps(P)
    IDENT, IDENTB = P.sb["IDENT"], P.sb["IDENTB"]
    ONESROW_F, MROW = P.sb["ONESROW_F"], P.sb["MROW"]
    FROW = P.sb["FROW"]

    def ld(e, h=h):
        for i, dst in enumerate(("RB", "R1M", "R2M", "R1A", "RBE")):
            e.dma_start(out=A[dst], in_=P.dram["dn_rows"][i * 16 + h:i * 16 + h + 1, :]
                        ).then_inc(P.s_tick, 16)
        for nm, base in (("dn_q", D_QH), ("dn_k", D_KH), ("dn_v", D_VH)):
            e.dma_start(out=POOL_B[0:64, base:base + 2048],
                        in_=P.dram[nm][h * 64:(h + 1) * 64, :]).then_inc(P.s_tick, 16)
        return 8
    dphase(P, "sync", ld)

    if h == 0:
        def twoi(v):
            v.tensor_scalar(A["TWOI"], IDENTB[:, :], 2.0, None, ALU.mult)
        phase(P, vector=twoi)

    for ch in range(2):
        def init(v):
            v.memset(A["SSF"][:, :], 0.0)
            v.memset(A["SSB"][:, :], 0.0)
            v.tensor_copy(A["PPB"], IDENTB[0:64, 0:64])
            v.tensor_copy(A["PTB"], IDENTB[0:64, 0:64])
        phase(P, vector=init)

        for sub in range(8):
            emit_dn_sub1(P, h, ch, sub, A)

        slot = ch * 128

        def pubs(e, h=h, slot=slot):
            e.dma_start(out=P.dram["dn_ccin"][0:64, h * 256 + slot:h * 256 + slot + 64],
                        in_=A["PTB"]).then_inc(P.s_tick, 16)
            e.dma_start(out=P.dram["dn_ccin"][0:64, h * 256 + slot + 64:h * 256 + slot + 128],
                        in_=A["SSB"]).then_inc(P.s_tick, 16)
            return 2
        dphase(P, "sync", pubs)


def emit_dn_sub1(P, h, ch, sub, A):
    POOL, POOL_B = P.sb["POOL"], P.sb["POOL_B"]
    IDENT, IDENTB = P.sb["IDENT"], P.sb["IDENTB"]
    ONESROW_F, MROW, FROW = P.sb["ONESROW_F"], P.sb["MROW"], P.sb["FROW"]
    s0 = ch * 1024 + sub * 128
    KSL = POOL_B[0:64, D_KH + s0:D_KH + s0 + 128]
    QSL = POOL_B[0:64, D_QH + s0:D_QH + s0 + 128]
    VSL = POOL_B[0:64, D_VH + s0:D_VH + s0 + 128]
    rsl = lambda nm: A[nm][0:1, s0:s0 + 128]
    NEGF = MROW[0:1, 1920:2048]

    def bcv(v):
        v.tensor_reduce(FROW[0:1, 8:9], rsl("RB"), axis=AX.X, op=ALU.min)
    phase(P, vector=bcv)

    def colb(t):
        t.matmul(P.bank(5)[:, 0:1], rsl("R1M"), ONESROW_F[0:1, 0:1], start=True, stop=True)
        t.matmul(P.bank(5)[:, 1:2], rsl("R1A"), ONESROW_F[0:1, 0:1], start=True, stop=True)
        t.matmul(P.bank(5)[:, 2:3], rsl("R2M"), ONESROW_F[0:1, 0:1], start=True, stop=True)
        t.matmul(P.bank(5)[:, 3:4], rsl("RBE"), ONESROW_F[0:1, 0:1], start=True, stop=True)
        t.matmul(P.bank(5)[:, 4:5], ONESROW_F[0:1, 0:128], FROW[0:1, 8:9],
                 start=True, stop=True)
    phase(P, tensor=colb)

    def cole(sc):
        sc.activation(A["CE1"], P.bank(5)[:, 0:1], AF.Exp)
        sc.activation(A["CGQ"], P.bank(5)[:, 1:2], AF.Exp)
        sc.copy(A["CR2"], P.bank(5)[:, 2:3])
        sc.copy(A["CBE"], P.bank(5)[:, 3:4])
        sc.activation(A["GCC"], P.bank(5)[0:64, 4:5], AF.Exp)
    phase(P, scalar=cole)

    def colc(v):
        v.tensor_tensor(A["CBC"], P.bank(5)[:, 4:5], A["CR2"], ALU.subtract)
    phase(P, vector=colc)

    def colc2(sc):
        sc.activation(A["CBC"], A["CBC"], AF.Exp)
    phase(P, scalar=colc2)

    def kf(v):
        v.tensor_copy(A["KF32"][:, 0:128], KSL)
        selfsync(P, v, "v")
        v.tensor_copy(A["KF32"][:, 128:256], VSL)
    phase(P, vector=kf)

    def trk(t):
        t.transpose(P.bank(6)[:, 0:64], A["KF32"][:, 0:128], IDENT[0:64, 0:64])
        t.transpose(P.bank(7)[:, 0:64], A["KF32"][:, 128:256], IDENT[0:64, 0:64])
    phase(P, tensor=trk)

    def evk(sc):
        sc.copy(A["KTM"], P.bank(6)[:, 0:64])
        sc.copy(A["VTM"], P.bank(7)[:, 0:64])
    phase(P, scalar=evk)

    def dmm(t):
        t.matmul(P.bank(0)[:, 0:128], rsl("R2M"), NEGF, start=True, stop=False)
        t.matmul(P.bank(0)[:, 0:128], ONESROW_F[0:1, 0:128], rsl("R1M"),
                 start=False, stop=True)
        t.matmul(P.bank(1)[:, 0:128], rsl("R2M"), NEGF, start=True, stop=False)
        t.matmul(P.bank(1)[:, 0:128], ONESROW_F[0:1, 0:128], rsl("R1A"),
                 start=False, stop=True)
    phase(P, tensor=dmm)

    def dmsk(v):
        v.tensor_tensor(P.bank(0)[:, 0:128], P.bank(0)[:, 0:128], A["MSKS"], ALU.add)
        v.tensor_tensor(P.bank(1)[:, 0:128], P.bank(1)[:, 0:128], A["MSKI"], ALU.add)
    phase(P, vector=dmsk)

    def dexp(sc):
        sc.activation(A["DST"], P.bank(0)[:, 0:128], AF.Exp)
        sc.activation(A["DIT"], P.bank(1)[:, 0:128], AF.Exp)
    phase(P, scalar=dexp)

    def kk(t):
        t.matmul(P.bank(2)[:, 0:128], KSL, KSL, start=True, stop=True)
        t.matmul(P.bank(3)[:, 0:128], KSL, QSL, start=True, stop=True)
    phase(P, tensor=kk)

    def mats(v):
        v.tensor_tensor(A["ATB"], P.bank(2)[:, 0:128], A["DST"], ALU.mult)
        v.tensor_tensor(A["AQT"], P.bank(3)[:, 0:128], A["DIT"], ALU.mult)
        selfsync(P, v, "v")
        v.tensor_tensor(A["ATB"], A["ATB"], IDENTB[:, :], ALU.add)
        selfsync(P, v, "v")
        v.tensor_copy(A["XB"], IDENTB[:, :])
        v.tensor_copy(A["XTB"], IDENTB[:, :])
    phase(P, vector=mats)

    for it in range(7):
        def nw1(t):
            t.matmul(P.bank(0)[:, 0:128], A["ATB"], A["XB"], start=True, stop=True)
        phase(P, tensor=nw1)

        def nwz(v):
            v.scalar_tensor_tensor(A["ZB"], P.bank(0)[:, 0:128], -1.0, A["TWOI"],
                                   ALU.mult, ALU.add)
        phase(P, vector=nwz)

        def nw2(t):
            t.matmul(P.bank(1)[:, 0:128], A["XTB"], A["ZB"], start=True, stop=True)
            t.matmul(P.bank(2)[:, 0:128], A["ZB"], A["XTB"], start=True, stop=True)
        phase(P, tensor=nw2)

        def nwe(sc):
            sc.copy(A["XB"], P.bank(1)[:, 0:128])
            sc.copy(A["XTB"], P.bank(2)[:, 0:128])
        phase(P, scalar=nwe)

    def w1(t):
        t.matmul(P.bank(3)[:, 0:64], KSL, A["SSB"], start=True, stop=True)
    phase(P, tensor=w1)

    def w2(v):
        v.tensor_scalar(A["S2B"], P.bank(3)[:, 0:64], A["CE1"], None, ALU.mult)
        v.tensor_scalar(A["T1B"], A["VTM"], A["CBE"], None, ALU.mult)
        selfsync(P, v, "v")
        v.tensor_tensor(A["RHSB"], A["T1B"], A["S2B"], ALU.subtract)
    phase(P, vector=w2)

    def w3(t):
        t.matmul(P.bank(4)[:, 0:64], A["XTB"], A["RHSB"], start=True, stop=True)
    phase(P, tensor=w3)

    def w4(sc):
        sc.copy(A["WB"], P.bank(4)[:, 0:64])
    phase(P, scalar=w4)

    def o1(t):
        t.matmul(P.bank(5)[:, 0:64], QSL, A["SSB"], start=True, stop=True)
        t.matmul(P.bank(6)[:, 0:64], A["AQT"], A["WB"], start=True, stop=True)
    phase(P, tensor=o1)

    def o2(v):
        v.tensor_scalar(A["OTM"], P.bank(5)[:, 0:64], A["CGQ"], None, ALU.mult)
        selfsync(P, v, "v")
        v.tensor_tensor(A["OSTG"], A["OTM"], P.bank(6)[:, 0:64], ALU.add)
    phase(P, vector=o2)

    def o3(e, h=h, s0=s0):
        e.dma_start(out=P.dram["dn_o0"][s0:s0 + 128, h * 64:(h + 1) * 64],
                    in_=A["OSTG"]).then_inc(P.s_tick, 16)
        return 1
    dphase(P, "sync", o3)

    def s1p(v):
        v.tensor_scalar(A["K2TM"], A["KTM"], A["CBC"], None, ALU.mult)
        v.tensor_scalar(A["T1M"], A["KTM"], A["CE1"], None, ALU.mult)
    phase(P, vector=s1p)

    def s2p(t):
        t.matmul(P.bank(7)[0:64, 0:64], A["K2TM"], A["WB"], start=True, stop=True)
        t.matmul(P.bank(0)[:, 0:64], A["XB"], A["K2TM"], start=True, stop=True)
    phase(P, tensor=s2p)

    def s3p(v):
        v.scalar_tensor_tensor(A["SSF"], A["SSF"], A["GCC"], P.bank(7)[0:64, 0:64],
                               ALU.mult, ALU.add)
        selfsync(P, v, "v")
        v.tensor_copy(A["SSB"], A["SSF"])
    phase(P, vector=s3p)

    def m2(sc):
        sc.activation(A["T1B"], P.bank(0)[:, 0:64], AF.Identity, scale=-1.0)
    phase(P, scalar=m2)

    def m3(t):
        t.matmul(P.bank(1)[0:64, 0:64], A["T1M"], A["T1B"], start=True, stop=True)
    phase(P, tensor=m3)

    def m4(v):
        v.scalar_tensor_tensor(A["MTS"], IDENT[0:64, 0:64], A["GCC"],
                               P.bank(1)[0:64, 0:64], ALU.mult, ALU.add)
    phase(P, vector=m4)

    def m5(t):
        t.matmul(P.bank(2)[0:64, 0:64], A["MTS"], A["PPB"], start=True, stop=True)
        t.matmul(P.bank(3)[0:64, 0:64], A["PPB"], A["MTS"], start=True, stop=True)
    phase(P, tensor=m5)

    def m6(sc):
        sc.copy(A["PPB"], P.bank(2)[0:64, 0:64])
        sc.copy(A["PTB"], P.bank(3)[0:64, 0:64])
    phase(P, scalar=m6)

    def m7(e, h=h, ch=ch, sub=sub):
        cs = ch * 8 + sub
        e.dma_start(out=P.dram["dn_xt"][h * 128:(h + 1) * 128, cs * 128:(cs + 1) * 128],
                    in_=A["XTB"]).then_inc(P.s_tick, 16)
        e.dma_start(out=P.dram["dn_mt"][h * 64:(h + 1) * 64, cs * 64:(cs + 1) * 64],
                    in_=A["MTS"]).then_inc(P.s_tick, 16)
        return 2
    dphase(P, "sync", m7)


def emit_dn_fold(P):
    POOL, POOL_B = P.sb["POOL"], P.sb["POOL_B"]
    A = _dn_aps(P)

    def cc(g):
        g.collective_compute("AllGather", ALU.bypass,
                             replica_groups=[[0, 1, 2, 3], [4, 5, 6, 7]],
                             ins=[P.dram["dn_ccin"][:, :].opt()],
                             outs=[P.dram["dn_ccout"][:, :].opt()])
    phase(P, gpsimd=cc)

    for h in range(16):
        def ldf(e, h=h):
            for j in range(8):
                slot = j if j < 4 else 7 - j
                colblk = 0 if j < 4 else 128
                e.dma_start(out=A["FPN"][:, j * 128:(j + 1) * 128],
                            in_=P.dram["dn_ccout"][slot * 64:(slot + 1) * 64,
                                                   h * 256 + colblk:h * 256 + colblk + 128]
                            ).then_inc(P.s_tick, 16)
            return 8
        dphase(P, "sync", ldf)

        def nfv(v, h=h):
            for j in range(8):
                v.tensor_copy(A["NF"][:, j * 64:(j + 1) * 64],
                              A["FPN"][:, j * 128 + 64:j * 128 + 128])
            selfsync(P, v, "v")
            v.memset(A["PS"][:, :], 0.0)
            v.memset(A["PSB"][:, :], 0.0)
            v.memset(A["SINS"][:, h * 128:(h + 1) * 128], 0.0)
        phase(P, vector=nfv)

        for j in range(8):
            def selp(v, h=h, j=j):
                v.scalar_tensor_tensor(A["SINS"][:, h * 128:h * 128 + 64], A["PS"],
                                       A["SELA"][:, j:j + 1],
                                       A["SINS"][:, h * 128:h * 128 + 64],
                                       ALU.mult, ALU.add)
                v.scalar_tensor_tensor(A["SINS"][:, h * 128 + 64:h * 128 + 128], A["PS"],
                                       A["SELB"][:, j:j + 1],
                                       A["SINS"][:, h * 128 + 64:h * 128 + 128],
                                       ALU.mult, ALU.add)
                selfsync(P, v, "v")
                v.tensor_copy(A["PSB"], A["PS"])
            phase(P, vector=selp)

            def fmm(t, j=j):
                t.matmul(P.bank(0)[0:64, 0:64], A["FPN"][:, j * 128:j * 128 + 64],
                         A["PSB"], start=True, stop=True)
            phase(P, tensor=fmm)

            def fup(v, j=j):
                v.tensor_copy(A["PS"], P.bank(0)[0:64, 0:64])
                selfsync(P, v, "v")
                v.tensor_tensor(A["PS"], A["PS"], A["NF"][:, j * 64:(j + 1) * 64], ALU.add)
            phase(P, vector=fup)


def emit_dn_pass2(P, h):
    POOL, POOL_B = P.sb["POOL"], P.sb["POOL_B"]
    A = _dn_aps(P)
    IDENT = P.sb["IDENT"]
    ONESROW_F, MROW, FROW = P.sb["ONESROW_F"], P.sb["MROW"], P.sb["FROW"]
    COLS = P.sb["COLS"]
    NEGF = MROW[0:1, 1920:2048]

    def ld(e, h=h):
        for i, dst in ((1, "R1M"), (2, "R2M"), (3, "R1A")):
            e.dma_start(out=A[dst], in_=P.dram["dn_rows"][i * 16 + h:i * 16 + h + 1, :]
                        ).then_inc(P.s_tick, 16)
        for nm, base in (("dn_q", D_QH), ("dn_k", D_KH), ("dn_g", D_VH)):
            e.dma_start(out=POOL_B[0:64, base:base + 2048],
                        in_=P.dram[nm][h * 64:(h + 1) * 64, :]).then_inc(P.s_tick, 16)
        return 6
    dphase(P, "sync", ld)

    def gs(sc):
        sc.activation(POOL_B[0:64, D_GS:D_GS + 2048], POOL_B[0:64, D_VH:D_VH + 2048],
                      AF.Silu)
    phase(P, scalar=gs)

    if h == 0:
        def epc(v):
            v.memset(A["EPC"], 1e-5)
        phase(P, vector=epc)

    for ch in range(2):
        def init(v, h=h, ch=ch):
            v.tensor_copy(A["CC"], A["SINS"][:, h * 128 + ch * 64:h * 128 + ch * 64 + 64])
            selfsync(P, v, "v")
            v.tensor_copy(A["CCB"], A["CC"])
        phase(P, vector=init)

        for sub in range(8):
            emit_dn_sub2(P, h, ch, sub, A)

        def stz(e, h=h, ch=ch):
            e.dma_start(out=P.dram["dn_zd"][h * 64:(h + 1) * 64,
                                            ch * 1024:(ch + 1) * 1024],
                        in_=A["ZD"][:, ch * 1024:(ch + 1) * 1024]).then_inc(P.s_tick, 16)
            return 1
        dphase(P, "sync", stz)


def emit_dn_sub2(P, h, ch, sub, A):
    POOL, POOL_B = P.sb["POOL"], P.sb["POOL_B"]
    IDENT = P.sb["IDENT"]
    ONESROW_F, MROW, FROW = P.sb["ONESROW_F"], P.sb["MROW"], P.sb["FROW"]
    COLS = P.sb["COLS"]
    NEGF = MROW[0:1, 1920:2048]
    s0 = ch * 1024 + sub * 128
    KSL = POOL_B[0:64, D_KH + s0:D_KH + s0 + 128]
    QSL = POOL_B[0:64, D_QH + s0:D_QH + s0 + 128]
    rsl = lambda nm: A[nm][0:1, s0:s0 + 128]

    def ldx(e, h=h, ch=ch, sub=sub, s0=s0):
        cs = ch * 8 + sub
        e.dma_start(out=A["XTL"], in_=P.dram["dn_xt"][h * 128:(h + 1) * 128,
                                                      cs * 128:(cs + 1) * 128]
                    ).then_inc(P.s_tick, 16)
        e.dma_start(out=A["MTL"], in_=P.dram["dn_mt"][h * 64:(h + 1) * 64,
                                                      cs * 64:(cs + 1) * 64]
                    ).then_inc(P.s_tick, 16)
        e.dma_start(out=A["O0B"], in_=P.dram["dn_o0"][s0:s0 + 128, h * 64:(h + 1) * 64]
                    ).then_inc(P.s_tick, 16)
        return 3
    dphase(P, "sync", ldx)

    def colb(t):
        t.matmul(P.bank(5)[:, 0:1], rsl("R1M"), ONESROW_F[0:1, 0:1], start=True, stop=True)
        t.matmul(P.bank(5)[:, 1:2], rsl("R1A"), ONESROW_F[0:1, 0:1], start=True, stop=True)
        t.matmul(P.bank(0)[:, 0:128], rsl("R2M"), NEGF, start=True, stop=False)
        t.matmul(P.bank(0)[:, 0:128], ONESROW_F[0:1, 0:128], rsl("R1A"),
                 start=False, stop=True)
    phase(P, tensor=colb)

    def cole(sc):
        sc.activation(A["CE1"], P.bank(5)[:, 0:1], AF.Exp)
        sc.activation(A["CGQ"], P.bank(5)[:, 1:2], AF.Exp)
    phase(P, scalar=cole)

    def dmsk(v):
        v.tensor_tensor(P.bank(0)[:, 0:128], P.bank(0)[:, 0:128], A["MSKI"], ALU.add)
    phase(P, vector=dmsk)

    def dexp(sc):
        sc.activation(A["DIT"], P.bank(0)[:, 0:128], AF.Exp)
    phase(P, scalar=dexp)

    def aqt(t):
        t.matmul(P.bank(1)[:, 0:128], KSL, QSL, start=True, stop=True)
        t.matmul(P.bank(2)[:, 0:64], KSL, A["CCB"], start=True, stop=True)
    phase(P, tensor=aqt)

    def aqv(v):
        v.tensor_tensor(A["AQT"], P.bank(1)[:, 0:128], A["DIT"], ALU.mult)
        v.tensor_scalar(A["S2B"], P.bank(2)[:, 0:64], A["CE1"], None, ALU.mult)
    phase(P, vector=aqv)

    def s3(t):
        t.matmul(P.bank(3)[:, 0:64], A["XTL"], A["S2B"], start=True, stop=True)
    phase(P, tensor=s3)

    def s3n(sc):
        sc.activation(A["T1B"], P.bank(3)[:, 0:64], AF.Identity, scale=-1.0)
    phase(P, scalar=s3n)

    def oc(t):
        t.matmul(P.bank(4)[:, 0:64], QSL, A["CCB"], start=True, stop=True)
        t.matmul(P.bank(5)[:, 0:64], A["AQT"], A["T1B"], start=True, stop=True)
    phase(P, tensor=oc)

    def ov(v):
        v.tensor_scalar(A["DOF"], P.bank(4)[:, 0:64], A["CGQ"], None, ALU.mult)
        v.tensor_copy(A["O0F"], A["O0B"])
        selfsync(P, v, "v")
        v.tensor_tensor(A["OTM"], A["DOF"], P.bank(5)[:, 0:64], ALU.add)
        selfsync(P, v, "v")
        v.tensor_tensor(A["OTM"], A["OTM"], A["O0F"], ALU.add)
    phase(P, vector=ov)

    def gn(sc):
        sc.activation(A["QSF"], A["OTM"], AF.Square)
    phase(P, scalar=gn)

    def gn2(v):
        v.tensor_reduce(COLS[:, 60:61], A["QSF"], axis=AX.X, op=ALU.add)
    phase(P, vector=gn2)

    def gn3(sc):
        sc.activation(COLS[:, 59:60], COLS[:, 60:61], AF.Sqrt, bias=A["EPC"],
                      scale=1.0 / 64.0)
    phase(P, scalar=gn3)

    def gn4(v):
        v.reciprocal(COLS[:, 58:59], COLS[:, 59:60])
        selfsync(P, v, "v")
        v.tensor_scalar(A["ONF"], A["OTM"], COLS[:, 58:59], None, ALU.mult)
    phase(P, vector=gn4)

    def tr(t):
        t.transpose(P.bank(6)[0:64, 0:128], A["ONF"], IDENT[:, :])
    phase(P, tensor=tr)

    def zv(v, s0=s0):
        v.tensor_copy(A["GSF"], POOL_B[0:64, D_GS + s0:D_GS + s0 + 128])
        v.tensor_scalar(A["ZFT"], P.bank(6)[0:64, 0:128], A["GNWC"], None, ALU.mult)
        selfsync(P, v, "v")
        v.tensor_tensor(A["ZD"][:, s0:s0 + 128], A["ZFT"], A["GSF"], ALU.mult)
    phase(P, vector=zv)

    def cp(t):
        t.matmul(P.bank(7)[0:64, 0:64], A["MTL"], A["CCB"], start=True, stop=True)
    phase(P, tensor=cp)

    def cp2(v):
        v.tensor_copy(A["CC"], P.bank(7)[0:64, 0:64])
        selfsync(P, v, "v")
        v.tensor_copy(A["CCB"], A["CC"])
    phase(P, vector=cp2)


EMITTERS["deltanet"] = emit_deltanet



def make_build(layers=ALL_LAYERS):
    nc = bass.Bass("TRN2", target_bir_lowering=False, debug=False)
    es = contextlib.ExitStack()
    with es:
        P = Prog(nc, es)

        def din(name, shape, dt=F32):
            P.inp[name] = nc.dram_tensor(name, list(shape), dt, kind="ExternalInput").ap()
            return P.inp[name]

        din("x", (TPC, E))
        din("ident", (128, 128))
        din("onesf", (128, 1))
        P.out["y"] = nc.dram_tensor("y", [TPC, E], F32, kind="ExternalOutput").ap()
        P.out["dbg"] = nc.dram_tensor("dbg", [128, 4096], F32, kind="ExternalOutput").ap()
        declare_layer_inputs(P, layers, din)

        sb = P.sb
        sb["RES"] = es.enter_context(nc.sbuf_tensor("RES", [128, NDT * TPC], F32))
        poolh = es.enter_context(nc.sbuf_tensor("POOL", [128, 49152], BF16))
        P.poolh = poolh
        sb["POOL"] = poolh[:, :].bitcast(F32)
        pool = sb["POOL"]
        sb["POOL_B"] = poolh[:, :]
        sb["SCR_F"] = pool[:, 0:8192]
        sb["WRK_B"] = poolh[:, 16384:32768]
        sb["SCR_B"] = poolh[:, 32768:49152]
        for nm, w, dt in (("MROW", TPC, F32), ("IROW", TPC, F32), ("TROW", TPC, F32),
                          ("IBROW", TPC, BF16), ("MIBROW", TPC, BF16),
                          ("BROW", 1536, BF16), ("FROW", 1024, F32)):
            sb[nm] = es.enter_context(nc.sbuf_tensor(nm, [1, w], dt))
        sb["IXROW"] = pool[0:1, 5120:7168]
        sb["COLS"] = es.enter_context(nc.sbuf_tensor("COLS", [128, 64], F32))
        sb["IDENT"] = es.enter_context(nc.sbuf_tensor("IDENT", [128, 128], F32))
        sb["ONES_F"] = es.enter_context(nc.sbuf_tensor("ONES_F", [128, 1], F32))
        sb["ONES1B"] = es.enter_context(nc.sbuf_tensor("ONES1B", [1, 128], BF16))
        sb["ONESROW_F"] = es.enter_context(nc.sbuf_tensor("ONESROW_F", [1, 128], F32))
        sb["WSROW"] = es.enter_context(nc.sbuf_tensor("WSROW", [1, 16], F32))
        sb["WSCOL"] = es.enter_context(nc.sbuf_tensor("WSCOL", [128, 4], F32))
        sb["MAGCOL"] = es.enter_context(nc.sbuf_tensor("MAGCOL", [128, 2], F32))
        sb["ONESB5"] = es.enter_context(nc.sbuf_tensor("ONESB5", [1, 512], BF16))
        sb["ONESBC"] = es.enter_context(nc.sbuf_tensor("ONESBC", [128, 1], BF16))
        sb["WRB"] = es.enter_context(nc.sbuf_tensor("WRB", [128, 64], BF16))
        sb["BRB"] = es.enter_context(nc.sbuf_tensor("BRB", [1, 8], BF16))
        sb["WRF"] = es.enter_context(nc.sbuf_tensor("WRF", [128, 64], F32))
        sb["MOER"] = es.enter_context(nc.sbuf_tensor("MOER", [1, 16], F32))
        sb["MOEC"] = es.enter_context(nc.sbuf_tensor("MOEC", [128, 24], F32))
        sb["IOTA8"] = es.enter_context(nc.sbuf_tensor("IOTA8", [128, 8], F32))
        sb["GCOL"] = es.enter_context(nc.sbuf_tensor("GCOL", [128, 8], F32))
        sb["BCOL"] = es.enter_context(nc.sbuf_tensor("BCOL", [128, 8], F32))
        sb["B1C"] = es.enter_context(nc.sbuf_tensor("B1C", [128, 32], F32))
        sb["GRIFC"] = es.enter_context(nc.sbuf_tensor("GRIFC", [128, 128], F32))
        P.state["halosrc_i"] = es.enter_context(
            nc.sbuf_tensor("halosrc_i", [1, 4], mybir.dt.int32))
        P.state["mla_ssrc_i"] = es.enter_context(
            nc.sbuf_tensor("mla_ssrc_i", [1, 8], mybir.dt.int32))
        sb["LOBOC"] = es.enter_context(nc.sbuf_tensor("LOBOC", [128, 16], F32))
        sb["E64"] = es.enter_context(nc.sbuf_tensor("E64", [128, 1], F32))
        sb["IDENTB"] = es.enter_context(nc.sbuf_tensor("IDENTB", [128, 128], BF16))
        sb["IND64B"] = es.enter_context(nc.sbuf_tensor("IND64B", [128, 2], BF16))
        sb["WBUF"] = es.enter_context(nc.sbuf_tensor("WBUF", [128, 2048], BF16))

        declare_layer_sbuf(P, layers, es, nc)

        for i in range(8):
            P.ps.append(es.enter_context(nc.psum_tensor(f"pb{i}", [128, 512], F32)))

        P.s_tick = es.enter_context(nc.semaphore("s_tick"))
        P.s_swdma = es.enter_context(nc.semaphore("s_swdma"))
        for e in ("v", "t", "a", "g", "s"):
            P.selfsem[e] = es.enter_context(nc.semaphore("self_" + e))
            P.selfcnt[e] = 0

        block = es.enter_context(nc.Block())
        P.block = block

        def preload(e):
            e.dma_start(out=sb["IDENT"][:, :], in_=P.inp["ident"][:, :]).then_inc(P.s_tick, 16)
            e.dma_start(out=sb["ONES_F"][:, 0:1], in_=P.inp["onesf"][:, :]).then_inc(P.s_tick, 16)
            return 2
        dphase(P, "sync", preload)

        def prevec(v):
            v.memset(sb["ONES1B"][:, :], 1.0)
            v.memset(sb["ONESROW_F"][:, :], 1.0)
            v.memset(sb["MAGCOL"][:, 0:1], MAGIC)
            v.memset(sb["MAGCOL"][:, 1:2], -MAGIC)
            v.memset(sb["ONESB5"][:, :], 1.0)
            v.memset(sb["ONESBC"][:, :], 1.0)
            v.tensor_copy(sb["IDENTB"][:, :], sb["IDENT"][:, :])
            v.memset(sb["IND64B"][:, 0:2], 0.0)
            selfsync(P, v, "v")
            v.memset(sb["IND64B"][0:64, 0:1], 1.0)
            v.memset(sb["IND64B"][64:128, 1:2], 1.0)
        phase(P, vector=prevec)

        emit_input_stage(P)
        for lay in layers:
            EMITTERS[lay](P)
        emit_output_stage(P)
    return nc


def declare_layer_inputs(P, layers, din):
    nc = P.nc
    if "ffn" in layers or "ffn2" in layers:
        din("ffn_wg_t", (E, 4096))
        din("ffn_wu_t", (E, 4096))
        din("ffn_wd_t", (4096, E))
        P.dram["wqg"] = nc.dram_tensor("wqg", [32, 131072], BF16).ap()
        P.dram["wqu"] = nc.dram_tensor("wqu", [32, 131072], BF16).ap()
        P.dram["wqd"] = nc.dram_tensor("wqd", [4096, E], BF16).ap()
    if "moe" in layers or "moe2" in layers:
        din("moe_wr", (E, 8))
        din("moe_br", (1, 8))
        din("iota8", (128, 8))
        din("moe_gcol", (128, 8))
        din("moe_bcol", (128, 8))
        din("moe_b1c", (8, 128, 32))
        din("moe_b2r", (8, E))
        din("moe_w1s", (256 * 128, 1024))
        din("moe_w2r", (256 * 128, 1024))
        P.dram["w1b"] = nc.dram_tensor("w1b", [256 * 128, 1024], BF16).ap()
        P.dram["w2b"] = nc.dram_tensor("w2b", [256 * 128, 1024], BF16).ap()
    if "griffin" in layers:
        for nm, rows in GRIF_W:
            cols = 1536 if nm in ("g_win_t", "g_wgate_t", "g_rgin_t", "g_rggate_t",
                                  "g_mlpg_t", "g_mlpu_t") else 1024
            din(nm, (rows, cols))
            P.dram[nm + "_b"] = nc.dram_tensor(nm + "_b", [rows, cols], BF16).ap()
        din("g_cols", (128, 128))
        P.inp["g_halosrc"] = nc.dram_tensor("g_halosrc", [1, 2], mybir.dt.int32,
                                            kind="ExternalInput").ap()
        kwg = {"kind": "ExternalOutput"} if DBG_DRAM else {}
        P.dram["g2d"] = nc.dram_tensor("g2d", [1536, 2048], F32, **kwg).ap()
        P.dram["hrecd"] = nc.dram_tensor("hrecd", [1536, 2048], F32, **kwg).ap()
        P.dram["ud"] = nc.dram_tensor("ud", [1536, 2048], F32).ap()
        P.dram["ucd"] = nc.dram_tensor("ucd", [1536, 2048], BF16).ap()
        P.dram["ucd32"] = nc.dram_tensor("ucd32", [1536, 2048], F32, **kwg).ap()
        P.dram["g_ccin"] = nc.dram_tensor("g_ccin", [1536, 6], BF16).ap()
        h = nc.dram_tensor("g_ccout", [4 * 1536, 6], BF16)
        P.dram["g_ccout"] = h.ap()
        P.state["g_ccout_h"] = h
        P.dram["g_ccin2"] = nc.dram_tensor("g_ccin2", [1536, 4], F32).ap()
        h2 = nc.dram_tensor("g_ccout2", [4 * 1536, 4], F32)
        P.dram["g_ccout2"] = h2.ap()
        P.state["g_ccout2_h"] = h2
    if "mla" in layers:
        for nm, rows, cols in MLA_W:
            din(nm, (rows, cols))
            P.dram[nm + "_b"] = nc.dram_tensor(nm + "_b", [rows, cols], BF16).ap()
        din("mla_rope", (32, 4096))
        din("mla_kill", (1, 7680))
        din("mla_dgA", (128, 2560))
        din("mla_dgB", (128, 2048))
        din("mla_loboc", (128, 16))
        P.inp["mla_ssrc"] = nc.dram_tensor("mla_ssrc", [1, 8], mybir.dt.int32,
                                           kind="ExternalInput").ap()
        P.dram["mla_ccin"] = nc.dram_tensor("mla_ccin", [288, 2048], BF16).ap()
        cco = nc.dram_tensor("mla_ccout", [1152, 2048], BF16)
        P.dram["mla_ccout"] = cco.ap()
        P.state["mla_ccout_h"] = cco
        P.dram["mla_slotbuf"] = nc.dram_tensor("mla_slotbuf", [2304, 1024], BF16).ap()
        P.dram["mla_yhd"] = nc.dram_tensor("mla_yhd", [1024, 2048], BF16).ap()
    if "winlat" in layers:
        for nm, rows, cols in WL_CAST:
            din(nm, (rows, cols))
            P.dram[nm + "_b"] = nc.dram_tensor(nm + "_b", [rows, cols], BF16).ap()
        din("wl_wmA", (128, 1024))
        din("wl_wmT", (128, 1024))
        din("wl_bk0", (1, 1024))
        din("wl_lm", (1, 1024))
        din("wl_gvec", (64, 32))
        din("wl_bk0c", (128, 2))
        P.inp["wl_hsrc"] = nc.dram_tensor("wl_hsrc", [1, 2], mybir.dt.int32,
                                          kind="ExternalInput").ap()
        P.dram["wl_ccin"] = nc.dram_tensor("wl_ccin", [1024, 512], BF16).ap()
        wlo = nc.dram_tensor("wl_ccout", [4096, 512], BF16)
        P.dram["wl_ccout"] = wlo.ap()
        P.state["wl_ccout_h"] = wlo
        P.dram["wl_pcin"] = nc.dram_tensor("wl_pcin", [1024, 16], BF16).ap()
        P.dram["wl_pcout"] = nc.dram_tensor("wl_pcout", [4096, 16], BF16).ap()
        P.dram["wl_remd"] = nc.dram_tensor("wl_remd", [1024, 2048], BF16).ap()
        P.dram["wl_zd"] = nc.dram_tensor("wl_zd", [1024, 2048], BF16).ap()
    if "deltanet" in layers:
        for nm, rows, cols in DN_CAST:
            din(nm, (rows, cols))
            P.dram[nm + "_b"] = nc.dram_tensor(nm + "_b", [rows, cols], BF16).ap()
        din("dn_ropeC", (128, 2048))
        din("dn_ropeS", (128, 2048))
        din("dn_convs", (128, 96))
        din("dn_vec", (16, 4))
        din("dn_hmask", (128, 2))
        din("dn_gnw", (64, 1))
        din("dn_mski", (128, 128))
        din("dn_msks", (128, 128))
        din("dn_selA", (64, 8))
        din("dn_selB", (64, 8))
        P.inp["dn_hsrc"] = nc.dram_tensor("dn_hsrc", [1, 2], mybir.dt.int32,
                                          kind="ExternalInput").ap()
        kw = {"kind": "ExternalOutput"} if DBG_DRAM else {}
        for nm in ("dn_q", "dn_k", "dn_v", "dn_g", "dn_zd"):
            P.dram[nm] = nc.dram_tensor(nm, [1024, 2048], BF16, **kw).ap()
        P.dram["dn_lnr"] = nc.dram_tensor("dn_lnr", [32, 2048], F32, **kw).ap()
        P.dram["dn_rows"] = nc.dram_tensor("dn_rows", [80, 2048], F32, **kw).ap()
        P.dram["dn_xt"] = nc.dram_tensor("dn_xt", [2048, 2048], BF16).ap()
        P.dram["dn_mt"] = nc.dram_tensor("dn_mt", [1024, 1024], BF16).ap()
        P.dram["dn_o0"] = nc.dram_tensor("dn_o0", [2048, 1024], BF16, **kw).ap()
        P.dram["dn_hci"] = nc.dram_tensor("dn_hci", [1024, 6], BF16).ap()
        dnh = nc.dram_tensor("dn_hco", [4096, 6], BF16)
        P.dram["dn_hco"] = dnh.ap()
        P.state["dn_hco_h"] = dnh
        P.dram["dn_ccin"] = nc.dram_tensor("dn_ccin", [64, 4096], BF16, **kw).ap()
        P.dram["dn_ccout"] = nc.dram_tensor("dn_ccout", [256, 4096], BF16).ap()


def declare_layer_sbuf(P, layers, es, nc):
    pass



def _core_slices(c):
    return c // 4, c % 4, 7 - (c % 4)


def make_in_maps(inputs, layers=ALL_LAYERS):
    x = np.asarray(inputs["L0_attn_a_x"])
    ident = np.eye(128, dtype=np.float32)
    onesf = np.ones((128, 1), dtype=np.float32)
    shared = make_shared_arrays(inputs, layers)
    in_maps = []
    for c in range(CORES):
        b, qa, qb = _core_slices(c)
        xs = np.ascontiguousarray(np.concatenate(
            [x[b, qa * CH:(qa + 1) * CH], x[b, qb * CH:(qb + 1) * CH]], axis=0))
        m = {"x": xs, "ident": ident, "onesf": onesf}
        m.update(shared)
        core = make_core_arrays(inputs, layers, c)
        if "_grif_core" in core:
            maskA, selA, selB = core.pop("_grif_core")
            core.pop("_gc_patch", None)
            gc = np.array(shared["g_cols"], copy=True)
            gc[:, 112] = maskA
            gc[:, 114:121] = np.tile(selA, (128, 1))
            gc[:, 121:128] = np.tile(selB, (128, 1))
            core["g_cols"] = gc
        m.update(core)
        in_maps.append(m)
    return in_maps


def make_shared_arrays(inputs, layers):
    d = {}
    if "ffn" in layers or "ffn2" in layers:
        d["ffn_wg_t"] = np.ascontiguousarray(np.asarray(inputs["L0_ffn_w_gate"]).T.astype(np.float32))
        d["ffn_wu_t"] = np.ascontiguousarray(np.asarray(inputs["L0_ffn_w_up"]).T.astype(np.float32))
        d["ffn_wd_t"] = np.ascontiguousarray(np.asarray(inputs["L0_ffn_w_down"]).T.astype(np.float32))
    if "moe" in layers or "moe2" in layers:
        W1 = np.asarray(inputs["L1_ffn_W1"]).astype(np.float32)
        W2 = np.asarray(inputs["L1_ffn_W2"]).astype(np.float32)
        d["moe_w1s"] = np.ascontiguousarray(
            W1.reshape(8, 8, 128, 32, 128).transpose(0, 3, 2, 1, 4).reshape(256 * 128, 1024))
        d["moe_w2r"] = np.ascontiguousarray(W2.reshape(8, 32, 128, 1024).reshape(256 * 128, 1024))
        d["moe_wr"] = np.ascontiguousarray(np.asarray(inputs["L1_ffn_Wr"]).astype(np.float32))
        d["moe_br"] = np.asarray(inputs["L1_ffn_br"]).astype(np.float32).reshape(1, 8)
        d["iota8"] = np.tile(np.arange(8, dtype=np.float32), (128, 1))
        d["moe_gcol"] = np.ascontiguousarray(np.asarray(inputs["L1_ffn_ln_g"]).astype(np.float32).reshape(8, 128).T)
        d["moe_bcol"] = np.ascontiguousarray(np.asarray(inputs["L1_ffn_ln_b"]).astype(np.float32).reshape(8, 128).T)
        d["moe_b1c"] = np.ascontiguousarray(np.asarray(inputs["L1_ffn_b1"]).astype(np.float32).reshape(8, 32, 128).transpose(0, 2, 1))
        d["moe_b2r"] = np.ascontiguousarray(np.asarray(inputs["L1_ffn_b2"]).astype(np.float32))
    if "griffin" in layers:
        f32 = np.float32
        d["g_win_t"] = np.ascontiguousarray(np.asarray(inputs["L0_attn_b_W_in"]).astype(f32).T)
        d["g_wgate_t"] = np.ascontiguousarray(np.asarray(inputs["L0_attn_b_W_gate"]).astype(f32).T)
        d["g_rgin_t"] = np.ascontiguousarray(np.asarray(inputs["L0_attn_b_rg_in_w"]).astype(f32).T)
        d["g_rggate_t"] = np.ascontiguousarray(np.asarray(inputs["L0_attn_b_rg_gate_w"]).astype(f32).T)
        d["g_wout_t"] = np.ascontiguousarray(np.asarray(inputs["L0_attn_b_W_out"]).astype(f32).T)
        d["g_mlpg_t"] = np.ascontiguousarray(np.asarray(inputs["L0_attn_b_mlp_gate_w"]).astype(f32).T)
        d["g_mlpu_t"] = np.ascontiguousarray(np.asarray(inputs["L0_attn_b_mlp_up_w"]).astype(f32).T)
        d["g_mlpd_t"] = np.ascontiguousarray(np.asarray(inputs["L0_attn_b_mlp_down_w"]).astype(f32).T)
        gc = np.zeros((128, 128), dtype=f32)
        gc[:, 0:12] = np.asarray(inputs["L0_attn_b_forget_lambda"]).astype(f32).reshape(12, 128).T
        gc[:, 12:24] = np.asarray(inputs["L0_attn_b_conv_b"]).astype(f32).reshape(12, 128).T
        gc[:, 24:36] = np.asarray(inputs["L0_attn_b_rg_in_b"]).astype(f32).reshape(12, 128).T
        gc[:, 36:48] = np.asarray(inputs["L0_attn_b_rg_gate_b"]).astype(f32).reshape(12, 128).T
        gc[:, 48:56] = np.asarray(inputs["L0_attn_b_norm1_w"]).astype(f32).reshape(8, 128).T
        gc[:, 56:64] = np.asarray(inputs["L0_attn_b_norm2_w"]).astype(f32).reshape(8, 128).T
        cw = np.asarray(inputs["L0_attn_b_conv_w"]).astype(f32).reshape(12, 128, 4)
        for j in range(4):
            gc[:, 64 + j * 12:64 + (j + 1) * 12] = cw[:, :, j].T
        d["g_cols"] = gc
    if "mla" in layers:
        f32 = np.float32
        d["mla_wq"] = np.asarray(inputs["L0_attn_a_Wq"]).astype(f32)
        d["mla_wqr"] = np.asarray(inputs["L0_attn_a_Wqr"]).astype(f32)
        d["mla_wkr"] = np.asarray(inputs["L0_attn_a_Wkr"]).astype(f32)
        d["mla_wkvd"] = np.asarray(inputs["L0_attn_a_Wkvd"]).astype(f32)
        d["mla_wku"] = np.asarray(inputs["L0_attn_a_Wku"]).astype(f32)
        d["mla_wvu"] = np.asarray(inputs["L0_attn_a_Wvu"]).astype(f32)
        d["mla_wo"] = np.asarray(inputs["L0_attn_a_Wo"]).astype(f32)
        d["mla_loboc"] = np.tile(np.asarray(inputs["L0_attn_a_lobo_log"]).astype(f32), (128, 1))
        dga = np.full((128, 2560), -1e30, dtype=f32)
        dgb = np.full((128, 2560), -1e30, dtype=f32)
        pcol = np.arange(128)[:, None]
        frow = np.arange(512)[None, :]
        for pat in range(4):
            dga[:, pat * 512:(pat + 1) * 512] = np.where(pat * 128 + pcol >= frow, 0.0, -1e30)
            dgb[:, pat * 512:(pat + 1) * 512] = np.where(frow >= pat * 128 + pcol, 0.0, -1e30)
        d["mla_dgA"] = dga
        d["mla_dgB"] = np.ascontiguousarray(dgb[:, 0:2048])
    if "winlat" in layers:
        f32 = np.float32
        d["wl_wq"] = np.asarray(inputs["L1_attn_b_wq"]).astype(f32)
        d["wl_wk"] = np.asarray(inputs["L1_attn_b_wk"]).astype(f32)
        d["wl_wv"] = np.asarray(inputs["L1_attn_b_wv"]).astype(f32)
        d["wl_wgq"] = np.asarray(inputs["L1_attn_b_wgq"]).astype(f32)
        d["wl_wgk"] = np.asarray(inputs["L1_attn_b_wgk"]).astype(f32)
        d["wl_wgv"] = np.asarray(inputs["L1_attn_b_wgv"]).astype(f32)
        d["wl_rout"] = np.asarray(inputs["L1_attn_b_r_out_w"]).astype(f32)
        d["wl_wout"] = np.asarray(inputs["L1_attn_b_w_out"]).astype(f32)
        gv = np.zeros((64, 32), dtype=f32)
        gv[:, 0:16] = np.asarray(inputs["L1_attn_b_gate_local_w"]).astype(f32).T
        gv[:, 16:32] = np.asarray(inputs["L1_attn_b_gate_remote_w"]).astype(f32).T
        d["wl_gvec"] = gv
        pcol = np.arange(128)[:, None]
        wmA = np.full((128, 1024), -1e30, dtype=f32)
        wmT = np.full((128, 1024), -1e30, dtype=f32)
        for qt in range(2):
            i = qt * 128 + pcol
            j = np.arange(512)[None, :]
            wmA[:, qt * 512:(qt + 1) * 512] = np.where((j > i) & (j <= i + 256), 0.0, -1e30)
        for kt in range(4):
            j = kt * 128 + pcol
            i = np.arange(256)[None, :]
            wmT[:, kt * 256:(kt + 1) * 256] = np.where((j > i) & (j <= i + 256), 0.0, -1e30)
        d["wl_wmA"] = wmA
        d["wl_wmT"] = wmT
    if "deltanet" in layers:
        f32 = np.float32
        d["dn_wq"] = np.asarray(inputs["L1_attn_a_Wq"]).astype(f32)
        d["dn_wk"] = np.asarray(inputs["L1_attn_a_Wk"]).astype(f32)
        d["dn_wv"] = np.asarray(inputs["L1_attn_a_Wv"]).astype(f32)
        d["dn_wg"] = np.asarray(inputs["L1_attn_a_Wg"]).astype(f32)
        d["dn_wo"] = np.asarray(inputs["L1_attn_a_Wo"]).astype(f32)
        d["dn_wgk"] = np.asarray(inputs["L1_attn_a_Wgk"]).astype(f32)
        d["dn_wb"] = np.asarray(inputs["L1_attn_a_Wb"]).astype(f32)
        cvs = np.zeros((128, 96), dtype=f32)
        for ci, nm in enumerate(("conv_q_w", "conv_k_w", "conv_v_w")):
            cw = np.asarray(inputs["L1_attn_a_" + nm]).astype(f32).reshape(8, 128, 4)
            for dt in range(8):
                cvs[:, ci * 32 + dt * 4:ci * 32 + (dt + 1) * 4] = cw[dt]
        d["dn_convs"] = cvs
        vec = np.zeros((16, 4), dtype=f32)
        vec[:, 0] = np.asarray(inputs["L1_attn_a_A_log"]).astype(f32)
        vec[:, 1] = np.asarray(inputs["L1_attn_a_dt_bias"]).astype(f32)
        vec[:, 2] = np.asarray(inputs["L1_attn_a_bb"]).astype(f32)
        d["dn_vec"] = vec
        d["dn_gnw"] = np.asarray(inputs["L1_attn_a_gnorm_w"]).astype(f32).reshape(64, 1)
        pcol = np.arange(128)[:, None]
        frow = np.arange(128)[None, :]
        d["dn_mski"] = np.where(frow >= pcol, 0.0, -1e30).astype(f32)
        d["dn_msks"] = np.where(frow > pcol, 0.0, -1e30).astype(f32)
    return d


def make_core_arrays(inputs, layers, c):
    d = {}
    b, qa, qb = _core_slices(c)
    if "griffin" in layers:
        def owner_slot(j):
            return j if j < 4 else 7 - j

        def tailcol(j):
            return 0 if j < 4 else 3
        ja = qa - 1 if qa > 0 else 0
        jb = qb - 1
        offA = (owner_slot(ja) * 1536) * 6 + tailcol(ja)
        offB = (owner_slot(jb) * 1536) * 6 + tailcol(jb)
        d["g_halosrc"] = np.array([[offA, offB]], dtype=np.int32)
        gc = d.setdefault("_gc_patch", None)
        maskA = 0.0 if qa == 0 else 1.0
        selA = np.zeros(7, dtype=np.float32)
        selB = np.zeros(7, dtype=np.float32)
        if qa > 0:
            selA[qa - 1] = 1.0
        selB[qb - 1] = 1.0
        d["_grif_core"] = (maskA, selA, selB)
    if "mla" in layers:
        f32 = np.float32
        pos = np.concatenate([qa * CH + np.arange(CH), qb * CH + np.arange(CH)]).astype(f32)
        inv = 10000.0 ** (-np.arange(16, dtype=f32) / 16.0)
        ang = pos[None, :] * inv[:, None]
        rope = np.zeros((32, 4096), dtype=f32)
        rope[0:16, 0:2048] = np.cos(ang)
        rope[16:32, 0:2048] = np.cos(ang)
        rope[0:16, 2048:4096] = -np.sin(ang)
        rope[16:32, 2048:4096] = np.sin(ang)
        d["mla_rope"] = rope
        ssrc = np.zeros((1, 8), dtype=np.int32)
        for s in range(8):
            c = qb if s == 0 else s - 1
            slotrow = (c if c < 4 else 7 - c) * 288
            coloff = 0 if c < 4 else 1024
            ssrc[0, s] = slotrow * 2048 + coloff
        d["mla_ssrc"] = ssrc
        kill = np.full((1, 7680), -1e30, dtype=f32)
        for vi, s in enumerate(range(1, 4)):
            if s - 1 < qa:
                kill[0, vi * 512:(vi + 1) * 512] = 0.0
        for vi, s in zip(range(3, 7), range(1, 5)):
            if s - 1 == qa:
                kill[0, vi * 512:(vi + 1) * 512] = 0.0
        kill[0, 3584:3584 + 512] = 0.0
        for vi, s in zip(range(1, 8), range(1, 8)):
            if s - 1 < qb:
                kill[0, 3584 + vi * 512:3584 + (vi + 1) * 512] = 0.0
        d["mla_kill"] = kill
    if "winlat" in layers:
        f32 = np.float32
        bk0 = np.zeros((1, 1024), dtype=f32)
        if qa == 0:
            bk0[0, 0:256] = -1e30
        d["wl_bk0"] = bk0
        bk0c = np.zeros((128, 2), dtype=f32)
        if qa == 0:
            bk0c[:, 0] = -1e30
        d["wl_bk0c"] = bk0c
        lm = np.full((1, 1024), -1e30, dtype=f32)
        for idx in range(16):
            gp = qa * 8 + idx if idx < 8 else qb * 8 + (idx - 8)
            lm[0, idx * 64:idx * 64 + gp + 1] = 0.0
        d["wl_lm"] = lm

        def _wl_off(q):
            j = q - 1 if q > 0 else 0
            slot = j if j < 4 else 7 - j
            colblk = 0 if j < 4 else 1
            return slot * 1024 * 512 + colblk * 256
        d["wl_hsrc"] = np.array([[_wl_off(qa), _wl_off(qb)]], dtype=np.int32)
    if "deltanet" in layers:
        f32 = np.float32
        pos = np.concatenate([qa * CH + np.arange(CH), qb * CH + np.arange(CH)]).astype(f32)
        inv = 10000.0 ** (-np.arange(16, dtype=f32) / 16.0)
        dloc = np.arange(128) % 64
        C = np.ones((128, 2048), dtype=f32)
        S = np.zeros((128, 2048), dtype=f32)
        for p in range(128):
            dl = dloc[p]
            if dl < 16:
                ang = pos * inv[dl]
                C[p] = np.cos(ang)
                S[p] = -np.sin(ang)
            elif dl < 32:
                ang = pos * inv[dl - 16]
                C[p] = np.cos(ang)
                S[p] = np.sin(ang)
        d["dn_ropeC"] = C
        d["dn_ropeS"] = S
        hm = np.ones((128, 2), dtype=f32)
        if qa == 0:
            hm[:, 0] = 0.0
        d["dn_hmask"] = hm

        def _dn_off(q):
            j = q - 1 if q > 0 else 0
            slot = j if j < 4 else 7 - j
            colblk = 0 if j < 4 else 1
            return slot * 1024 * 6 + colblk * 3
        d["dn_hsrc"] = np.array([[_dn_off(qa), _dn_off(qb)]], dtype=np.int32)
        selA = np.zeros((64, 8), dtype=f32)
        selB = np.zeros((64, 8), dtype=f32)
        selA[:, qa] = 1.0
        selB[:, qb] = 1.0
        d["dn_selA"] = selA
        d["dn_selB"] = selB
    return d


def assemble_output(results):
    out = np.zeros((B, T, E), dtype=np.float32)
    for c in range(CORES):
        b, qa, qb = _core_slices(c)
        y = results[c]["y"]
        out[b, qa * CH:(qa + 1) * CH] = y[:CH]
        out[b, qb * CH:(qb + 1) * CH] = y[CH:]
    return out


def kernel(**inputs):
    nc = make_build(ALL_LAYERS)
    in_maps = make_in_maps(inputs, ALL_LAYERS)
    res = run_bass_kernel_spmd(nc, in_maps, core_ids=list(range(CORES)))
    return assemble_output(res.results)
